# version 02
# baseline (speedup 1.0000x reference)
import math
import numpy as np
import ml_dtypes
import concourse.bass as bass
import concourse.mybir as mybir
from concourse.bass_utils import run_bass_kernel_spmd
from contextlib import ExitStack

F32 = mybir.dt.float32
BF16 = mybir.dt.bfloat16
AF = mybir.ActivationFunctionType
ALU = mybir.AluOpType
AX = mybir.AxisListType
BF = ml_dtypes.bfloat16

B, SQ, D, H, HKV, HD = 4, 2048, 4096, 32, 8, 128
NQB, NDT = SQ // 128, D // 128
NREM = 205
NEG30 = -1.0e30
SQRT_HD = float(np.float32(np.sqrt(np.float32(HD))))
OWN = {0: [0, 3, 4, 7, 8, 11, 12, 15], 1: [1, 2, 5, 6, 9, 10, 13, 14]}
PAIR_OFF = {0: (0, 384), 1: (128, 256)}
N_IT = 8
TARGET = 200.5

ENGS = ("sync", "gpsimd", "vector", "scalar", "tensor")


class Sched:
    def __init__(self, nc, ctx):
        self.nc = nc
        self.ops = {e: [] for e in ENGS}
        self.ctr = {}
        self.ctx = ctx

    def sem(self, name):
        s = self.ctx.enter_context(self.nc.semaphore(name))
        self.ctr[name] = [s, 0]

    def val(self, name):
        return self.ctr[name][1]

    def after(self, name):
        return (name, self.ctr[name][1])

    def emit(self, eng, fn, waits=(), inc=None, n=1):
        if inc is not None:
            self.ctr[inc][1] += n
        sem = self.ctr[inc][0] if inc is not None else None
        wl = [(self.ctr[w][0], v) for (w, v) in waits if v > 0]
        def op(h, fn=fn, wl=wl, sem=sem, n=n):
            for (s, v) in wl:
                h.wait_ge(s, v)
            ins = fn(h)
            if sem is not None:
                ins.then_inc(sem, n)
        self.ops[eng].append(op)

    def dma(self, eng, fn, waits=(), inc=None):
        self.emit(eng, fn, waits, inc, n=16)


def host_constants(parity):
    inv_freq = (1.0 / (np.float32(10000.0) ** (np.arange(0, HD, 2, dtype=np.float32) / np.float32(HD)))).astype(np.float32)
    t = np.arange(SQ, dtype=np.float32)
    freqs = np.outer(t, inv_freq).astype(np.float32)
    emb = np.concatenate([freqs, freqs], axis=-1)
    cos = np.cos(emb).astype(np.float32)
    sin = np.sin(emb).astype(np.float32)
    cosT = np.ascontiguousarray(cos.T)
    sswapT = np.ascontiguousarray(sin.T).copy()
    sswapT[:64, :] = -sswapT[:64, :]
    own = OWN[parity]
    owncols = np.concatenate([np.arange(qb * 128, (qb + 1) * 128) for qb in own])
    cosT_own = np.ascontiguousarray(cosT[:, owncols])
    sswapT_own = np.ascontiguousarray(sswapT[:, owncols])
    ident = np.eye(128, dtype=np.float32)
    causal_add = np.where(np.arange(128)[:, None] >= np.arange(128)[None, :], 0.0,
                          np.float32(NEG30)).astype(np.float32)
    valid = np.stack([qb * 128 + np.arange(128) + 1 for qb in own], axis=1).astype(np.float32)
    sel = (valid > NREM).astype(np.float32)
    from statistics import NormalDist
    nd = NormalDist()
    frac = np.clip(1.0 - NREM / np.maximum(valid, NREM + 1.0), 1e-6, 1 - 1e-6)
    zq = np.vectorize(nd.inv_cdf)(frac).astype(np.float32)
    phi = (np.exp(-0.5 * zq * zq) / np.sqrt(2 * np.pi)).astype(np.float32)
    sl0 = (1.0 / np.maximum(valid * phi, 1.0)).astype(np.float32)
    iota8 = np.tile(np.arange(8, dtype=np.float32)[None, :], (128, 1)).astype(np.float32)
    maskin = np.zeros((8, 128, 256), np.float32)
    for i, qb in enumerate(own):
        kpad = 256 * (i + 1)
        kcols = np.arange(kpad - 256, kpad)
        qpos = qb * 128 + np.arange(128)
        maskin[i] = np.where(kcols[None, :] <= qpos[:, None], 0.0, np.float32(NEG30))
    selb = np.repeat(sel.astype(np.int32)[:, :, None], 8, axis=2).reshape(128, 64)
    return dict(cosT=cosT, sswapT=sswapT, cosT_own=cosT_own, sswapT_own=sswapT_own,
                identc=ident, validc=valid, selc=sel, selbc=selb, zqc=zq, sl0c=sl0,
                iota8c=iota8, onesc=np.ones((1, 128), np.float32),
                sqhdc=np.full((128, 1), SQRT_HD, np.float32),
                maskin=maskin.reshape(8, 128 * 256))


def host_weights(inputs):
    WqT = np.ascontiguousarray(np.asarray(inputs["L0_attn_Wq"]).T)
    WkT = np.ascontiguousarray(np.asarray(inputs["L0_attn_Wk"]).T)
    WvT = np.ascontiguousarray(np.asarray(inputs["L0_attn_Wv"]).T)
    WoT = np.ascontiguousarray(np.asarray(inputs["L0_attn_Wo"]).T)
    routeT = np.ascontiguousarray(np.asarray(inputs["L0_ffn_route_w"]).T)
    A = np.asarray(inputs["L0_ffn_A"])
    AcatT = np.ascontiguousarray(A.reshape(128, D).T)
    Bw = np.asarray(inputs["L0_ffn_Bw"])
    BwcatT = np.ascontiguousarray(Bw.transpose(0, 2, 1).reshape(128, D))
    E8 = np.zeros((8, 128), dtype=np.float32)
    for e in range(8):
        E8[e, e * 16:(e + 1) * 16] = 1.0
    return dict(WqT=WqT, WkT=WkT, WvT=WvT, WoT=WoT, routeT=routeT,
                AcatT=AcatT, BwcatT=BwcatT, E8=E8)


def build_program(dev_phase="full"):
    own = OWN[0]
    nc = bass.Bass("TRN2", target_bir_lowering=False)

    def DI(name, shape, dt=F32):
        return nc.dram_tensor(name, list(shape), dt, kind="ExternalInput").ap()
    def DS(name, shape, dt):
        return nc.dram_tensor(name, list(shape), dt, kind="Internal").ap()

    hs = DI("hs", [SQ, D])
    hs_own = DI("hs_own", [1024, D])
    WqT = DI("WqT", [D, D]); WkT_i = DI("WkT", [D, HKV * HD]); WvT_i = DI("WvT", [D, HKV * HD])
    WoT_i = DI("WoT", [D, D])
    routeT_i = DI("routeT", [D, 8]); AcatT_i = DI("AcatT", [D, 128]); BwcatT_i = DI("BwcatT", [128, D])
    E8_i = DI("E8", [8, 128])
    cosT_i = DI("cosT", [HD, SQ]); sswapT_i = DI("sswapT", [HD, SQ])
    cosTo_i = DI("cosT_own", [HD, 1024]); sswapTo_i = DI("sswapT_own", [HD, 1024])
    ident_i = DI("identc", [128, 128])
    maskin_i = DI("maskin", [8, 128 * 256])
    valid_i = DI("validc", [128, 8]); sel_i = DI("selc", [128, 8])
    selb_i = nc.dram_tensor("selbc", [128, 64], mybir.dt.int32, kind="ExternalInput").ap()
    zq_i = DI("zqc", [128, 8]); sl0_i = DI("sl0c", [128, 8])
    ones_i = DI("onesc", [1, 128]); iota8_i = DI("iota8c", [128, 8])
    sqhd_i = DI("sqhdc", [128, 1])

    out_t = nc.dram_tensor("out", [1024, D], F32, kind="ExternalOutput").ap()

    qrt_d = DS("qrt_d", [H, HD, 1024], BF16)
    krt_d = DS("krt_d", [HKV, HD, SQ], BF16)
    v_d = DS("v_d", [HKV, NQB, 128, HD], BF16)
    ot_d = DS("ot_d", [H, 8, HD, 128], BF16)

    dev = {}
    if dev_phase == "B":
        dev["dbg_ot"] = nc.dram_tensor("dbg_ot", [H, 8, HD, 128], BF16, kind="ExternalOutput").ap()
        dev["dbg_stp"] = nc.dram_tensor("dbg_stp", [32, 128, 28 * 8], F32, kind="ExternalOutput").ap()
        dev["dbg_m8"] = nc.dram_tensor("dbg_m8", [32, 128, 128], F32, kind="ExternalOutput").ap()
        dev["dbg_sg"] = nc.dram_tensor("dbg_sg", [128, 8 * 2048], F32, kind="ExternalOutput").ap()
    if dev_phase in ("A", "Araw"):
        dev["dbg_qrt"] = nc.dram_tensor("dbg_qrt", [H, HD, 1024], BF16, kind="ExternalOutput").ap()
        if dev_phase == "A":
            dev["dbg_krt"] = nc.dram_tensor("dbg_krt", [HKV, HD, SQ], BF16, kind="ExternalOutput").ap()
            dev["dbg_v"] = nc.dram_tensor("dbg_v", [HKV, NQB, 128, HD], BF16, kind="ExternalOutput").ap()

    ctx = ExitStack()
    def SB(name, shape, dt):
        return ctx.enter_context(nc.sbuf_tensor(name, list(shape), dt))
    def PS(name, shape, dt):
        return ctx.enter_context(nc.psum_tensor(name, list(shape), dt))

    S = Sched(nc, ctx)
    for s in ("sLD", "sCA", "sW", "sPE", "sVE", "sSC", "sST", "sGP", "sEV"):
        S.sem(s)

    ident_s = SB("ident_s", [128, 128], BF16)
    maskin_s = SB("maskin_s", [128, 8 * 256], F32)
    valid_s = SB("valid_s", [128, 8], F32)
    sel_s = SB("sel_s", [128, 8], F32)
    zq_s = SB("zq_s", [128, 8], F32)
    sl0_s = SB("sl0_s", [128, 8], F32)
    ones_s = SB("ones_s", [1, 128], F32)
    iota8_s = SB("iota8_s", [128, 8], F32)
    selb_s = SB("selb_s", [128, 64], mybir.dt.int32)
    sqhd_s = SB("sqhd_s", [128, 1], F32)
    sinv_s = SB("sinv_s", [128, 1], F32)
    identf_s = SB("identf_s", [128, 128], F32)

    ctxA = ExitStack()
    def SA(name, shape, dt):
        return ctxA.enter_context(nc.sbuf_tensor(name, list(shape), dt))
    def PA(name, shape, dt):
        return ctxA.enter_context(nc.psum_tensor(name, list(shape), dt))
    wpan = SA("wpan", [128, 2 * NDT * 128], BF16)
    hs_bf = SA("hs_bf", [128, 2 * D], BF16)
    qk_tmp = SA("qk_tmp", [128, 2048], F32)
    qk_tmp2 = SA("qk_tmp2", [128, 2048], F32)
    qk_stage = SA("qk_stage", [128, 2048], BF16)
    v_stage = qk_stage
    vn_stage = SA("vn_stage", [128, 2048], BF16)

    ps_pj = PA("ps_pj", [128, 2048], F32)
    ps_tr = PA("ps_tr", [128, 512], BF16)
    cosT_s = SA("cosT_s", [HD, SQ], F32)
    sswapT_s = SA("sswapT_s", [HD, SQ], F32)
    cosTo_s = SA("cosTo_s", [HD, 1024], F32)
    sswapTo_s = SA("sswapTo_s", [HD, 1024], F32)

    for (dname, src) in (("cosT_s", cosT_i), ("sswapT_s", sswapT_i), ("cosTo_s", cosTo_i),
                         ("sswapTo_s", sswapTo_i), ("valid_s", valid_i),
                         ("sel_s", sel_i), ("zq_s", zq_i), ("sl0_s", sl0_i), ("ones_s", ones_i),
                         ("iota8_s", iota8_i), ("sqhd_s", sqhd_i), ("identf_s", ident_i)):
        dst = {"cosT_s": cosT_s, "sswapT_s": sswapT_s, "cosTo_s": cosTo_s,
               "sswapTo_s": sswapTo_s, "valid_s": valid_s,
               "sel_s": sel_s, "zq_s": zq_s, "sl0_s": sl0_s, "ones_s": ones_s,
               "iota8_s": iota8_s, "sqhd_s": sqhd_s, "identf_s": identf_s}[dname]
        S.dma("sync", lambda h, d=dst, s=src: h.dma_start(out=d[:], in_=s), inc="sLD")
    S.dma("gpsimd", lambda h: h.dma_start(out=ident_s[:], in_=ident_i), inc="sCA")
    S.dma("sync", lambda h: h.dma_start(out=selb_s[:], in_=selb_i), inc="sLD")
    S.dma("sync", lambda h: h.dma_start(
              out=maskin_s.rearrange("p (i c) -> p i c", c=256),
              in_=maskin_i.rearrange("i (p c) -> p i c", p=128)), inc="sLD")
    consts_ld = S.after("sLD")
    consts_ca = S.after("sCA")
    S.emit("vector", lambda h: nc.vector.reciprocal(sinv_s[:], sqhd_s[:]),
           waits=[consts_ld], inc="sVE")
    sinv_ready = S.after("sVE")

    def build_hsT(src_ap, ntiles, dst, seg_w, tag, pre=(None, 0)):
        evg = (None, 0)
        for g in range(ntiles // 2):
            for i2 in range(2):
                tt = g * 2 + i2
                w = ([evg] if g > 0 and i2 == 0 else []) + ([pre] if g == 0 and i2 == 0 else [])
                S.dma("gpsimd",
                      lambda h, i2=i2, tt=tt: h.dma_start(out=hs_bf[:, i2 * D:(i2 + 1) * D],
                                                          in_=src_ap[tt * 128:(tt + 1) * 128, :]),
                      waits=w, inc="sCA")
            ca_done = S.after("sCA")
            for dt in range(NDT):
                for i2 in range(2):
                    w = [ca_done, S.after("sSC")] if i2 == 0 else []
                    S.emit("tensor",
                           lambda h, i2=i2, dt=dt: nc.tensor.transpose(
                               ps_tr[:, i2 * 128:(i2 + 1) * 128],
                               hs_bf[:, i2 * D + dt * 128: i2 * D + (dt + 1) * 128], ident_s[:]),
                           waits=w, inc=("sPE" if i2 == 1 else None))
                S.emit("scalar",
                       lambda h, dt=dt, g=g, dst=dst, seg_w=seg_w: nc.scalar.copy(
                           out=dst[:, dt * seg_w + g * 256: dt * seg_w + (g + 1) * 256],
                           in_=ps_tr[:, 0:256]),
                       waits=[S.after("sPE")], inc="sSC")
            evg = S.after("sSC")
        return S.after("sSC")

    ctxAo = ExitStack()
    hsT_own = ctxAo.enter_context(nc.sbuf_tensor("hsT_own", [128, NDT * 1024], BF16))
    hsTo_done = build_hsT(hs_own, 8, hsT_own, 1024, "own")

    def wsrc(kind, idx):
        if kind == "q":
            return WqT[:, idx * 128:(idx + 1) * 128]
        if kind == "k":
            return WkT_i[:, idx * 128:(idx + 1) * 128]
        return WvT_i[:, idx * 128:(idx + 1) * 128]

    pj_free = (None, 0)
    panel_hist = []

    def proj_group(otiles, hsT_src, seg_w, first_ready, t_off=0):
        nonlocal pj_free
        for jj, (kind, idx) in enumerate(otiles):
            slot = jj % 2
            S.dma("gpsimd",
                  lambda h, slot=slot, kind=kind, idx=idx: h.dma_start(
                      out=wpan[:, slot * NDT * 128:(slot + 1) * NDT * 128]
                          .rearrange("p (dt c) -> p dt c", c=128),
                      in_=wsrc(kind, idx).rearrange("(dt p) c -> p dt c", p=128)),
                  waits=([panel_hist[-2]] if len(panel_hist) >= 2 else []),
                  inc="sW")
            w_ready = S.after("sW")
            first_waits = [w_ready, first_ready, pj_free]
            nmm = seg_w // 512
            for dt in range(NDT):
                pan = wpan[:, slot * NDT * 128 + dt * 128: slot * NDT * 128 + (dt + 1) * 128]
                st, fin = (dt == 0), (dt == NDT - 1)
                def mmp(h, pan=pan, dt=dt, st=st, fin=fin, nmm=nmm, seg_w=seg_w, hsT_src=hsT_src):
                    last = None
                    for c in range(nmm):
                        last = nc.tensor.matmul(ps_pj[:, c * 512:(c + 1) * 512], pan,
                                                hsT_src[:, dt * seg_w + c * 512: dt * seg_w + (c + 1) * 512],
                                                start=st, stop=fin)
                    return last
                S.emit("tensor", mmp, waits=(first_waits if dt == 0 else []),
                       inc=("sPE" if fin else None))
            panel_hist.append(S.after("sPE"))
            mm_done = S.after("sPE")

            if kind in ("q", "k"):
                W = seg_w
                ct = cosTo_s if kind == "q" else cosT_s
                sw = sswapTo_s if kind == "q" else sswapT_s
                S.emit("vector", lambda h, W=W, ct=ct, t_off=t_off: nc.vector.tensor_mul(
                           qk_tmp[:, :W], ps_pj[:, :W], ct[:, t_off:t_off + W]),
                       waits=[mm_done, sinv_ready])
                S.emit("vector", lambda h, W=W, sw=sw, t_off=t_off: nc.vector.tensor_mul(
                           qk_tmp2[0:64, :W], ps_pj[64:128, :W], sw[0:64, t_off:t_off + W]))
                S.emit("vector", lambda h, W=W, sw=sw, t_off=t_off: nc.vector.tensor_mul(
                           qk_tmp2[64:128, :W], ps_pj[0:64, :W], sw[64:128, t_off:t_off + W]))
                def rope_add(h, W=W):
                    h.drain()
                    return nc.vector.tensor_add(qk_stage[:, :W], qk_tmp[:, :W], qk_tmp2[:, :W])
                S.emit("vector", rope_add, inc="sVE")
                pj_free = S.after("sVE")
                rope_done = S.after("sVE")
                dst = qrt_d[idx] if kind == "q" else krt_d[idx][:, t_off:t_off + W]
                S.dma("sync", lambda h, dst=dst, W=W: h.dma_start(out=dst, in_=qk_stage[:, :W]),
                      waits=[rope_done], inc="sST")
                park = S.after("sST")
                S.emit("vector", lambda h: nc.vector.engine_nop(), waits=[park])
            else:
                S.emit("scalar", lambda h: nc.scalar.copy(out=v_stage[:], in_=ps_pj[:]),
                       waits=[mm_done, S.after("sST")], inc="sSC")
                pj_free = S.after("sSC")
                ev_done = S.after("sSC")
                nkt4 = seg_w // 512
                for ktg in range(nkt4):
                    for i2 in range(4):
                        kt = ktg * 4 + i2
                        w = [ev_done, S.after("sSC")] if i2 == 0 else []
                        S.emit("tensor",
                               lambda h, kt=kt, i2=i2: nc.tensor.transpose(
                                   ps_tr[:, i2 * 128:(i2 + 1) * 128],
                                   v_stage[:, kt * 128:(kt + 1) * 128], ident_s[:]),
                               waits=w, inc=("sPE" if i2 == 3 else None))
                    S.emit("scalar",
                           lambda h, ktg=ktg: nc.scalar.copy(
                               out=vn_stage[:, ktg * 512:(ktg + 1) * 512], in_=ps_tr[:]),
                           waits=[S.after("sPE")], inc="sSC")
                vt_done = S.after("sSC")
                kt0 = t_off // 128
                S.dma("sync", lambda h, idx=idx, kt0=kt0, seg_w=seg_w: h.dma_start(
                          out=v_d[idx][kt0:kt0 + seg_w // 128].rearrange("kt p c -> p kt c"),
                          in_=vn_stage[:, 0:seg_w].rearrange("p (kt c) -> p kt c", c=128)),
                      waits=[vt_done], inc="sST")
                park = S.after("sST")
                S.emit("scalar", lambda h: nc.scalar.copy(out=v_stage[0:1, 0:1], in_=v_stage[0:1, 0:1]),
                       waits=[park])

    proj_group([("q", i) for i in range(32)], hsT_own, 1024, hsTo_done)
    ctxAo.close()
    hsT = ctxA.enter_context(nc.sbuf_tensor("hsT", [128, NDT * 1024], BF16))
    for half in range(2):
        hsT_h_done = build_hsT(hs[half * 1024:(half + 1) * 1024, :], 8, hsT, 1024, "full%d" % half,
                               pre=S.after("sPE"))
        proj_group([("k", i) for i in range(8)] + [("v", i) for i in range(8)],
                   hsT, 1024, hsT_h_done, t_off=half * 1024)

    phaseA_st = S.after("sST")

    if dev_phase in ("B", "full"):
        ctxA.close()
        ctxB = ExitStack()
        def SBX(name, shape, dt):
            return ctxB.enter_context(nc.sbuf_tensor(name, list(shape), dt))
        def PSX(name, shape, dt):
            return ctxB.enter_context(nc.psum_tensor(name, list(shape), dt))
        SgB = SBX("SgB", [128, 8 * 2048], F32)
        SgS = SBX("SgS", [128, 8 * 1024], F32)
        Ppre2 = SBX("Ppre2", [128, 2 * 2048], F32)
        Bscr = SBX("Bscr", [128, 2048], F32)
        Ascr = SBX("Ascr", [128, 2048], F32)
        Tbs = SBX("Tbs", [128, 2048], BF16)
        Tas = SBX("Tas", [128, 2048], BF16)
        junkb = SBX("junkb", [128, 2048], BF16)
        Pfin2 = SBX("Pfin2", [128, 2 * 2048], BF16)
        PTb = SBX("PTb", [128, 16 * 512], BF16)
        krt2 = SBX("krt2", [128, 2 * 2048], BF16)
        v2 = SBX("v2", [128, 2 * 2 * 2048], BF16)
        qrt2 = SBX("qrt2", [128, 8 * 128], BF16)
        NPL = 28
        stp = SBX("stp", [128, 2 * NPL * 8], F32)
        NPI = 6
        stpi = SBX("stpi", [128, 2 * NPI * 8], mybir.dt.int32)
        W64 = SBX("W64", [128, 64], F32)
        Gb64 = SBX("Gb64", [128, 64], F32)
        Ga64 = SBX("Ga64", [128, 64], F32)
        m8b64 = SBX("m8b64", [128, 64], F32)
        m8a64 = SBX("m8a64", [128, 64], F32)
        negc8 = SBX("negc8", [128, 8], F32)
        ot_stage = SBX("ot_stage", [128, 2 * 512], BF16)
        sinvn_s = SBX("sinvn_s", [128, 1], F32)
        ps_S = [PSX("ps_S0", [128, 512], F32), PSX("ps_S1", [128, 512], F32)]
        ps_pt = [PSX("ps_pt0", [128, 1024], BF16), PSX("ps_pt1", [128, 1024], BF16)]
        ps_av = [PSX("ps_av0", [128, 512], F32), PSX("ps_av1", [128, 512], F32)]

        PL = {n: j for j, n in enumerate(
            ["MU", "SQS", "SD", "T", "SL", "NEGT", "TP", "CP", "C", "RMIN",
             "B0", "VMS", "VMB", "VMA", "SIDE", "IDX", "Z", "ZI", "TLO", "THI",
             "SELB", "X0", "X1", "X2", "X3", "X4", "PAD", "PAD2"])}
        PLI = {n: j for j, n in enumerate(["UP", "DN", "O", "SIDE", "SELB", "OK"])}
        def pli(n, sl):
            j = sl * NPI + PLI[n]
            return stpi[:, j * 8:(j + 1) * 8]
        def pl(n, sl):
            j = sl * NPL + PL[n]
            return stp[:, j * 8:(j + 1) * 8]
        def plc(n, sl, h8):
            j = sl * NPL + PL[n]
            return stp[:, j * 8 + h8: j * 8 + h8 + 1]

        ORDER = [4, 0, 5, 1, 6, 2, 7, 3]
        tiles = [(oct_, i) for oct_ in range(4) for i in ORDER]
        NT = len(tiles)
        def tinfo(k):
            oct_, i = tiles[k]
            return oct_, i, 256 * (i + 1), 2 * (i + 1), k % 2
        ev = [dict() for _ in range(NT)]
        mm_ev = []
        evac_ev = []
        N_CNT = 8
        NVH = 1
        W_LO, W_HI, TGT = float(NREM - 8), float(NREM + 7), 204.5
        V = "vector"

        bar = phaseA_st
        first_b = {e: True for e in ENGS}
        def wb(eng, waits):
            w = [x for x in waits if x is not None]
            if first_b[eng]:
                w.append(bar)
                first_b[eng] = False
            return w
        def ve(fn, waits=(), inc=None):
            S.emit(V, fn, waits, inc)

        S.emit("vector", lambda h: nc.vector.tensor_scalar(
                   sinvn_s[:], sinv_s[:], -1.0, None, op0=ALU.mult),
               waits=wb("vector", [sinv_ready]), inc="sVE")
        S.emit("vector", lambda h: nc.vector.memset(stp[:, :], 0.0))
        S.emit("vector", lambda h: nc.vector.memset(negc8[:, :], -1e29), inc="sVE")
        stp_z = S.after("sVE")

        def Sg_of(k):
            return SgB if k % 2 == 0 else SgS
        def sgw(k):
            return 2048 if k % 2 == 0 else 1024

        def emit_load(u):
            oct_, i, kpad, nkt, sl = tinfo(u)
            kv0 = oct_ * 2
            w0 = [ev[u - 2]["av"]] if u >= 2 else []
            if mm_ev:
                w0.append(mm_ev[-1])
            S.dma("sync", lambda h, oct_=oct_, i=i: h.dma_start(
                      out=qrt2[:, :].rearrange("p (hh q) -> p hh q", q=128),
                      in_=qrt_d[oct_ * 8:(oct_ + 1) * 8, :, i * 128:(i + 1) * 128]
                          .rearrange("hh p q -> p hh q")),
                  waits=wb("sync", w0), inc="sLD")
            for g in range(2):
                S.dma("sync", lambda h, g=g, kv=kv0 + g, kpad=kpad: h.dma_start(
                          out=krt2[:, g * 2048: g * 2048 + kpad],
                          in_=krt_d[kv][:, 0:kpad]),
                      inc="sLD")
                S.dma("sync", lambda h, g=g, kv=kv0 + g, kpad=kpad, nkt=nkt, sl=sl: h.dma_start(
                          out=v2[:, sl * 4096 + g * 2048: sl * 4096 + g * 2048 + kpad]
                              .rearrange("p (kt c) -> p kt c", c=128),
                          in_=v_d[kv][0:nkt].rearrange("kt p c -> p kt c")),
                      inc="sLD")
            ev[u]["ld"] = S.after("sLD")

        def emit_smm(u):
            oct_, i, kpad, nkt, sl = tinfo(u)
            nch = (kpad + 511) // 512
            ev[u]["nch"] = nch
            ev[u]["mm0"] = len(mm_ev)
            sg_free = ev[u - 2]["sgfree"] if u >= 2 else None
            first = True
            for h8 in range(8):
                g = h8 // 4
                for c in range(nch):
                    cw = min(512, kpad - c * 512)
                    gcc = len(mm_ev)
                    pp = gcc % 2
                    wv = []
                    if first:
                        wv = wb("tensor", [ev[u]["ld"], sg_free])
                        first = False
                    if gcc >= 2:
                        wv.append(("sEV", gcc - 1))
                    def mm(h, h8=h8, g=g, c=c, cw=cw, pp=pp):
                        return nc.tensor.matmul(
                            ps_S[pp][:, 0:cw],
                            qrt2[:, h8 * 128:(h8 + 1) * 128],
                            krt2[:, g * 2048 + c * 512: g * 2048 + c * 512 + cw],
                            start=True, stop=True)
                    S.emit("tensor", mm, waits=wv, inc="sPE")
                    mm_ev.append(S.after("sPE"))

        def build_evq(u):
            oct_, i, kpad, nkt, sl = tinfo(u)
            Sg = Sg_of(u); W = sgw(u)
            nch = ev[u]["nch"]
            q = []
            gcc = ev[u]["mm0"]
            for h8 in range(8):
                for c in range(nch):
                    cw = min(512, kpad - c * 512)
                    def evf(h, h8=h8, c=c, cw=cw, pp=gcc % 2, Sg=Sg, W=W):
                        return nc.scalar.copy(
                            out=Sg[:, h8 * W + c * 512: h8 * W + c * 512 + cw],
                            in_=ps_S[pp][:, 0:cw])
                    q.append((evf, [mm_ev[gcc]], "ev", gcc))
                    gcc += 1
                def sqf(h, h8=h8, Sg=Sg, W=W, kpad=kpad, sl=sl):
                    src = Sg[:, h8 * W: h8 * W + kpad].rearrange(
                        "p (a f) -> p a f", f=4)[:, :, 0:1]
                    return nc.scalar.activation(out=junkb[:, 0:kpad // 4], in_=src,
                                                func=AF.Square,
                                                accum_out=plc("SQS", sl, h8))
                def muf(h, h8=h8, Sg=Sg, W=W, kpad=kpad, sl=sl):
                    src = Sg[:, h8 * W: h8 * W + kpad].rearrange(
                        "p (a f) -> p a f", f=4)[:, :, 0:1]
                    return nc.scalar.activation(out=junkb[:, 0:kpad // 4], in_=src,
                                                func=AF.Copy,
                                                accum_out=plc("MU", sl, h8))
                q.append((sqf, [], "st", h8))
                q.append((muf, [], "st", h8))
            ev[u]["evq"] = q
            ev[u]["stat"] = {}

        def drain_evq(u, n=None):
            if u is None or u >= NT:
                return
            qq = ev[u].get("evq")
            if not qq:
                return
            cnt = len(qq) if n is None else min(n, len(qq))
            for _ in range(cnt):
                fn, w, kind, idx = qq.pop(0)
                if kind == "ev":
                    S.emit("scalar", fn, waits=wb("scalar", w), inc="sEV")
                else:
                    S.emit("scalar", fn, waits=wb("scalar", w), inc="sSC")
                    ev[u]["stat"][idx] = S.after("sSC")

        def emit_postload(u):
            oct_, i, kpad, nkt, sl = tinfo(u)
            Sg = Sg_of(u); W = sgw(u)
            if i == 0:
                for h8 in range(8):
                    def rmf(h, h8=h8, Sg=Sg, W=W, kpad=kpad, sl=sl):
                        return nc.vector.tensor_reduce(
                            plc("RMIN", sl, h8), Sg[:, h8 * W: h8 * W + kpad],
                            axis=AX.X, op=ALU.min)
                    ve(rmf, waits=[ev[u]["stat"][h8], stp_z] if h8 == 0 else [ev[u]["stat"][h8]],
                       inc=("sVE" if h8 == 7 else None))
                ev[u]["rmin"] = S.after("sVE")
            for h8 in range(8):
                w = [ev[u]["stat"][h8]]
                if i == 0 and h8 == 0:
                    w.append(ev[u]["rmin"])
                def dcf(h, h8=h8, Sg=Sg, W=W, kpad=kpad, i=i):
                    return nc.vector.tensor_tensor(
                        out=Sg[:, h8 * W + kpad - 256: h8 * W + kpad],
                        in0=Sg[:, h8 * W + kpad - 256: h8 * W + kpad],
                        in1=maskin_s[:, i * 256:(i + 1) * 256], op=ALU.add)
                S.emit("vector", dcf, waits=w,
                       inc=("sVE" if h8 == 7 else None))
            ev[u]["dca"] = S.after("sVE")
            n4 = float(kpad // 4)
            def init1(h, sl=sl, n4=n4):
                nc.gpsimd.tensor_scalar(pl("MU", sl), pl("MU", sl), 1.0 / n4, None, op0=ALU.mult)
                nc.gpsimd.tensor_scalar(pl("SQS", sl), pl("SQS", sl), 1.0 / n4, None, op0=ALU.mult)
                nc.gpsimd.tensor_tensor(pl("X0", sl), pl("MU", sl), pl("MU", sl), op=ALU.mult)
                nc.gpsimd.tensor_tensor(pl("SQS", sl), pl("SQS", sl), pl("X0", sl), op=ALU.subtract)
                return nc.gpsimd.tensor_scalar(pl("SQS", sl), pl("SQS", sl), 1e-6, None, op0=ALU.max)
            S.emit("gpsimd", init1,
                   waits=[ev[u]["stat"][h] for h in range(8)] + [stp_z], inc="sGP")
            i1d = S.after("sGP")
            def sd1(h, sl=sl):
                nc.scalar.activation(pl("X1", sl), pl("SQS", sl), AF.Ln)
                h.drain()
                return nc.scalar.activation(pl("SD", sl), pl("X1", sl), AF.Exp, scale=0.5)
            S.emit("scalar", sd1, waits=[i1d], inc="sSC")
            sdd = S.after("sSC")
            zqc = zq_s[:, i:i + 1]
            slc = sl0_s[:, i:i + 1]
            selc = sel_s[:, i:i + 1]
            def init2(h, sl=sl, zqc=zqc, slc=slc, selc=selc):
                ts, tt = nc.vector.tensor_scalar, nc.vector.tensor_tensor
                ts(pl("X0", sl), pl("SD", sl), zqc, None, op0=ALU.mult)
                ts(pl("SL", sl), pl("SD", sl), slc, None, op0=ALU.mult)
                tt(pl("T", sl), pl("MU", sl), pl("X0", sl), op=ALU.add)
                nc.vector.memset(pl("TLO", sl), -29000.0)
                nc.vector.memset(pl("THI", sl), 29000.0)
                return ts(pl("NEGT", sl), pl("T", sl), -1.0, None, op0=ALU.mult)
            ve(init2, waits=[sdd], inc="sVE")
            ev[u]["init"] = S.after("sVE")

        def emit_counts(k, r):
            oct_, i, kpad, nkt, sl = tinfo(k)
            Sg = Sg_of(k); W = sgw(k)
            w_first = [ev[k]["init"], ev[k]["dca"]] if r == 1 else [ev[k]["upd"]]
            def cv(h, Sg=Sg, kpad=kpad, sl=sl):
                return nc.vector.tensor_scalar(
                    Bscr[:, 0:kpad], Sg[:, 0:kpad], plc("T", sl, 0), None,
                    op0=ALU.is_gt, op1=ALU.add, accum_out=plc("C", sl, 0))
            ve(cv, waits=list(w_first), inc="sVE")
            ev[k]["cv"] = S.after("sVE")
            for h8 in range(1, 8):
                def cs(h, h8=h8, Sg=Sg, W=W, kpad=kpad, sl=sl):
                    return nc.scalar.activation(
                        out=junkb[:, 0:kpad], in_=Sg[:, h8 * W: h8 * W + kpad],
                        func=AF.Sign, bias=plc("NEGT", sl, h8), scale=1.0,
                        accum_out=plc("C", sl, h8))
                S.emit("scalar", cs, waits=(list(w_first) if h8 == 1 else []),
                       inc=("sSC" if h8 == 7 else None))
            ev[k]["cs"] = S.after("sSC")

        def emit_upd(k, r):
            oct_, i, kpad, nkt, sl = tinfo(k)
            def upd(h, sl=sl, r=r, kpad=kpad):
                P = lambda n: pl(n, sl)
                PI = lambda n: pli(n, sl)
                ts, tt, sel = nc.vector.tensor_scalar, nc.vector.tensor_tensor, nc.vector.select
                def pad():
                    nc.vector.tensor_copy(P("PAD"), P("C"))
                cs_cols = stp[:, (sl * NPL + PL["C"]) * 8 + NVH:(sl * NPL + PL["C"]) * 8 + 8]
                ts(cs_cols, cs_cols, float(kpad), 0.5, op0=ALU.add, op1=ALU.mult)
                tt(P("X2"), P("T"), P("TP"), op=ALU.subtract)
                ts(P("X0"), P("C"), TGT, None, op0=ALU.subtract)
                tt(P("X1"), P("C"), P("CP"), op=ALU.subtract)
                ts(PI("UP"), P("C"), float(NREM), None, op0=ALU.is_ge)
                ts(PI("DN"), P("C"), float(NREM), None, op0=ALU.is_lt)
                ts(P("X3"), P("X0"), W_LO - TGT, W_HI - TGT, op0=ALU.max, op1=ALU.min)
                if r >= 2:
                    nc.vector.reciprocal(P("X4"), P("X1"))
                else:
                    pad()
                tt(P("X0"), P("X0"), P("X3"), op=ALU.subtract)
                if r >= 2:
                    tt(P("X2"), P("X2"), P("X4"), op=ALU.mult)
                else:
                    pad()
                sel(P("TLO"), PI("UP"), P("T"), P("TLO"))
                if r >= 2:
                    tt(P("X1"), P("X1"), P("X1"), op=ALU.mult)
                    ts(P("X4"), P("X2"), -1.0, None, op0=ALU.mult)
                    ts(PI("OK"), P("X1"), 0.25, None, op0=ALU.is_ge)
                    tt(P("X2"), P("X2"), P("X4"), op=ALU.max)
                else:
                    pad(); pad(); pad(); pad()
                sel(P("THI"), PI("DN"), P("T"), P("THI"))
                if r >= 2:
                    ts(P("X2"), P("X2"), 1e-5, 1e3, op0=ALU.max, op1=ALU.min)
                else:
                    pad()
                nc.vector.tensor_copy(P("TP"), P("T"))
                nc.vector.tensor_copy(P("CP"), P("C"))
                if r >= 2:
                    sel(P("SL"), PI("OK"), P("X2"), P("SL"))
                else:
                    pad()
                tt(P("X4"), P("TLO"), P("THI"), op=ALU.add)
                tt(P("X0"), P("X0"), P("SL"), op=ALU.mult)
                ts(P("X4"), P("X4"), 0.5, None, op0=ALU.mult)
                tt(P("X3"), P("T"), P("X0"), op=ALU.add)
                pad()
                tt(P("X1"), P("X3"), P("THI"), op=ALU.is_gt)
                tt(P("X0"), P("X3"), P("TLO"), op=ALU.is_lt)
                pad()
                tt(P("X1"), P("X1"), P("X0"), op=ALU.max)
                pad()
                nc.vector.tensor_copy(PI("O"), P("X1"))
                pad()
                sel(P("T"), PI("O"), P("X4"), P("X3"))
                pad()
                return ts(P("NEGT"), P("T"), -1.0, None, op0=ALU.mult)
            ve(upd, waits=[ev[k]["cs"]], inc="sVE")
            ev[k]["upd"] = S.after("sVE")

        def emit_fin(k):
            oct_, i, kpad, nkt, sl = tinfo(k)
            def fin(h, sl=sl, i=i):
                P = lambda n: pl(n, sl)
                PI = lambda n: pli(n, sl)
                ts, tt, sel = nc.vector.tensor_scalar, nc.vector.tensor_tensor, nc.vector.select
                if i == 0:
                    sel(P("X0"), selb_s[:, i * 8:(i + 1) * 8], P("T"), P("RMIN"))
                    nc.vector.tensor_copy(P("PAD"), P("C"))
                    nc.vector.tensor_copy(P("T"), P("X0"))
                    nc.vector.tensor_copy(P("PAD"), P("C"))
                return ts(P("B0"), P("T"), sinvn_s[:, 0:1], None, op0=ALU.mult)
            ve(fin, waits=[ev[k]["cs"], ev[k]["cv"]], inc="sVE")
            ev[k]["fin"] = S.after("sVE")

        def emit_ext(k):
            oct_, i, kpad, nkt, sl = tinfo(k)
            Sg = Sg_of(k); W = sgw(k)
            for h8 in range(8):
                def exc(h, h8=h8, Sg=Sg, W=W, kpad=kpad, sl=sl):
                    ts, stt = nc.vector.tensor_scalar, nc.vector.scalar_tensor_tensor
                    sg = Sg[:, h8 * W: h8 * W + kpad]
                    ts(Tbs[:, 0:kpad], sg, plc("T", sl, h8), -2e30, op0=ALU.is_gt, op1=ALU.mult)
                    ts(Tas[:, 0:kpad], sg, plc("T", sl, h8), -2e30, op0=ALU.is_le, op1=ALU.mult)
                    stt(out=Bscr[:, 0:kpad], in0=Tbs[:, 0:kpad], scalar=1.0,
                        in1=sg, op0=ALU.mult, op1=ALU.add)
                    stt(out=Ascr[:, 0:kpad], in0=Tas[:, 0:kpad], scalar=1.0,
                        in1=sg, op0=ALU.mult, op1=ALU.subtract)
                    nc.vector.max(out=m8b64[:, h8 * 8:(h8 + 1) * 8], in_=Bscr[:, 0:kpad])
                    return nc.vector.max(out=m8a64[:, h8 * 8:(h8 + 1) * 8], in_=Ascr[:, 0:kpad])
                ve(exc, waits=([ev[k]["fin"]] if h8 == 0 else []),
                   inc=("sVE" if h8 == 7 else None))
            def vmb(h, sl=sl, i=i, kpad=kpad):
                P = lambda n: pl(n, sl)
                PI = lambda n: pli(n, sl)
                ts, tt, sel = nc.vector.tensor_scalar, nc.vector.tensor_tensor, nc.vector.select
                def pad():
                    nc.vector.tensor_copy(P("PAD"), P("X4"))
                cs_cols = stp[:, (sl * NPL + PL["C"]) * 8 + NVH:(sl * NPL + PL["C"]) * 8 + 8]
                ts(cs_cols, cs_cols, float(kpad), 0.5, op0=ALU.add, op1=ALU.mult)
                pad()
                ts(P("X4"), P("C"), -0.25, 8388608.0, op0=ALU.add, op1=ALU.add)
                pad()
                ts(P("C"), P("X4"), -8388608.0, None, op0=ALU.add)
                pad()
                ts(P("X0"), P("C"), TGT, None, op0=ALU.subtract)
                ts(PI("SIDE"), P("C"), float(NREM), None, op0=ALU.is_ge)
                ts(P("X1"), P("X0"), -1.0, None, op0=ALU.mult)
                pad()
                tt(P("X0"), P("X0"), P("X1"), op=ALU.max)
                pad()
                ts(P("X0"), P("X0"), -0.5, None, op0=ALU.add)
                pad()
                ts(P("IDX"), P("X0"), 0.0, 7.0, op0=ALU.max, op1=ALU.min)
                pad()
                for h8 in range(8):
                    ts(W64[:, h8 * 8:(h8 + 1) * 8], iota8_s[:], plc("IDX", sl, h8), None,
                       op0=ALU.is_equal)
                pad()
                tt(Gb64[:], W64[:], m8b64[:], op=ALU.mult)
                tt(Ga64[:], W64[:], m8a64[:], op=ALU.mult)
                nc.vector.tensor_reduce(P("VMB"), Gb64.rearrange("p (hh j) -> p hh j", j=8),
                                        axis=AX.X, op=ALU.add)
                nc.vector.tensor_reduce(P("VMA"), Ga64.rearrange("p (hh j) -> p hh j", j=8),
                                        axis=AX.X, op=ALU.add)
                pad()
                ts(P("X1"), P("VMA"), -1.0, None, op0=ALU.mult)
                pad()
                sel(P("VMS"), PI("SIDE"), P("X1"), P("VMB"))
                if i == 0:
                    pad()
                    sel(P("X2"), selb_s[:, i * 8:(i + 1) * 8], P("VMS"), negc8[:])
                    pad()
                    nc.vector.tensor_copy(P("VMS"), P("X2"))
                return nc.vector.engine_nop()
            ve(vmb, waits=[], inc="sVE")
            ev[k]["vm"] = S.after("sVE")
            if dev_phase == "B" and k == 5:
                S.dma("sync", lambda h, Sg=Sg, W=W: h.dma_start(
                          out=dev["dbg_sg"][:, 0:8 * W].rearrange("p (hh c) -> p hh c", c=W),
                          in_=Sg.rearrange("p (hh c) -> p hh c", c=W)),
                      waits=[ev[k]["vm"]], inc="sST")
            if dev_phase == "B":
                S.dma("sync", lambda h, k=k, sl=sl: h.dma_start(
                          out=dev["dbg_stp"][k],
                          in_=stp[:, sl * NPL * 8:(sl + 1) * NPL * 8]),
                      waits=[ev[k]["vm"]], inc="sST")
                S.dma("sync", lambda h, k=k: h.dma_start(
                          out=dev["dbg_m8"][k][:, 0:64], in_=m8b64[:]), inc="sST")
                S.dma("sync", lambda h, k=k: h.dma_start(
                          out=dev["dbg_m8"][k][:, 64:128], in_=m8a64[:]), inc="sST")
                stpk = S.after("sST")
                ve(lambda h: nc.vector.engine_nop(), waits=[stpk], inc="sVE")
            pf_done = {}
            pz_done = {}
            ptbev = {}
            trp_done = {}
            for h8 in range(8):
                def exf(h, h8=h8, Sg=Sg, W=W, kpad=kpad, sl=sl):
                    return nc.scalar.activation(
                        out=Ppre2[:, (h8 % 2) * 2048:(h8 % 2) * 2048 + kpad],
                        in_=Sg[:, h8 * W: h8 * W + kpad], func=AF.Exp,
                        bias=plc("B0", sl, h8), scale=sinv_s[:, 0:1])
                wx = [ev[k]["fin"]]
                if h8 >= 2:
                    wx.append(pf_done[h8 - 2])
                S.emit("scalar", exf, waits=wx, inc="sSC")
                expd = S.after("sSC")
                def pff(h, h8=h8, Sg=Sg, W=W, kpad=kpad, sl=sl):
                    nc.vector.scalar_tensor_tensor(
                        out=Ppre2[:, (h8 % 2) * 2048:(h8 % 2) * 2048 + kpad],
                        in0=Sg[:, h8 * W: h8 * W + kpad], scalar=plc("VMS", sl, h8),
                        in1=Ppre2[:, (h8 % 2) * 2048:(h8 % 2) * 2048 + kpad],
                        op0=ALU.is_ge, op1=ALU.mult, accum_out=plc("Z", sl, h8))
                    nc.vector.tensor_copy(pl("PAD", sl), pl("C", sl))
                    return nc.vector.reciprocal(plc("ZI", sl, h8), plc("Z", sl, h8))
                ve(pff, waits=[expd, ev[k]["vm"]], inc="sVE")
                pf_done[h8] = S.after("sVE")
                def pzf(h, h8=h8, kpad=kpad, sl=sl):
                    return nc.scalar.activation(
                        out=Pfin2[:, (h8 % 2) * 2048:(h8 % 2) * 2048 + kpad],
                        in_=Ppre2[:, (h8 % 2) * 2048:(h8 % 2) * 2048 + kpad],
                        func=AF.Identity, scale=plc("ZI", sl, h8))
                wz = [pf_done[h8]]
                if h8 >= 2:
                    wz.append(ev[k]["trp%d" % (h8 - 2)])
                S.emit("scalar", pzf, waits=wz, inc="sSC")
                pz_done[h8] = S.after("sSC")
                x = h8 % 4
                nbat = (nkt + 7) // 8
                ptbev[h8] = []
                for b in range(nbat):
                    lo = b * 8
                    hi = min(nkt, lo + 8)
                    pb = (h8 * nbat + b) % 2
                    def trf(h, h8=h8, lo=lo, hi=hi, pb=pb, kpad=kpad):
                        last = None
                        for kt in range(lo, hi):
                            last = nc.tensor.transpose(
                                ps_pt[pb][:, (kt - lo) * 128:(kt - lo + 1) * 128],
                                Pfin2[:, (h8 % 2) * 2048 + kt * 128:
                                      (h8 % 2) * 2048 + (kt + 1) * 128],
                                ident_s[:])
                        return last
                    wt = [pz_done[h8]]
                    if len(ptb_hist) >= 2:
                        wt.append(ptb_hist[-2])
                    S.emit("tensor", trf, waits=wt, inc="sPE")
                    trd = S.after("sPE")
                    def tev(h, x=x, lo=lo, hi=hi, pb=pb):
                        return nc.vector.tensor_copy(
                            out=PTb.rearrange("p (kt xx c) -> p kt xx c", xx=4, c=128)[:, lo:hi, x],
                            in_=ps_pt[pb][:, 0:(hi - lo) * 128]
                                .rearrange("p (kt c) -> p kt c", c=128))
                    ve(tev, waits=[trd], inc="sVE")
                    ptb_hist.append(S.after("sVE"))
                    ptbev[h8].append(S.after("sVE"))
                ev[k]["trp%d" % h8] = trd
                if h8 % 4 == 3:
                    g = h8 // 4
                    wav = [e for hh in range(4 * g, 4 * g + 4) for e in ptbev[hh]]
                    if k >= 1 and ("avev%d" % g) in ev[k - 1]:
                        wav.append(ev[k - 1]["avev%d" % g])
                    def avf(h, g=g, nkt=nkt, sl=sl):
                        last = None
                        for kt in range(nkt):
                            last = nc.tensor.matmul(
                                ps_av[g][:, 0:512],
                                v2[:, sl * 4096 + g * 2048 + kt * 128:
                                   sl * 4096 + g * 2048 + (kt + 1) * 128],
                                PTb[:, kt * 512:(kt + 1) * 512],
                                start=(kt == 0), stop=(kt == nkt - 1))
                        return last
                    S.emit("tensor", avf, waits=wav, inc="sPE")
                    avd = S.after("sPE")
                    def aev(h, g=g):
                        return nc.scalar.copy(out=ot_stage[:, g * 512:(g + 1) * 512],
                                              in_=ps_av[g][:, 0:512])
                    S.emit("scalar", aev, waits=[avd], inc="sSC")
                    ev[k]["avev%d" % g] = S.after("sSC")
                    oct2, i2 = tiles[k]
                    hh0 = oct2 * 8 + g * 4
                    S.dma("sync", lambda h, hh0=hh0, i2=i2, g=g: h.dma_start(
                              out=ot_d[hh0:hh0 + 4, i2].rearrange("hh p q -> p hh q"),
                              in_=ot_stage[:, g * 512:(g + 1) * 512]
                                  .rearrange("p (hh q) -> p hh q", q=128)),
                          waits=[ev[k]["avev%d" % g]], inc="sST")
                    if g == 1:
                        ev[k]["av"] = avd
            ev[k]["sgfree"] = pf_done[7]

        ptb_hist = []

        emit_load(0)
        emit_smm(0)
        build_evq(0)
        drain_evq(0, None)
        emit_postload(0)
        for k in range(NT):
            u = k + 1 if k + 1 < NT else None
            if u is not None:
                emit_load(u)
                emit_smm(u)
                build_evq(u)
            for r in range(1, N_CNT + 1):
                emit_counts(k, r)
                if r < N_CNT:
                    emit_upd(k, r)
                drain_evq(u, 4)
            drain_evq(u, None)
            emit_fin(k)
            if u is not None:
                emit_postload(u)
            emit_ext(k)
        phaseB_st = S.after("sST")
        if dev_phase == "B":
            S.dma("sync", lambda h: h.dma_start(out=dev["dbg_ot"], in_=ot_d),
                  waits=[phaseB_st], inc="sST")
            S.emit("vector", lambda h: nc.vector.memset(SgB[:, 0:2048], 0.0), inc="sVE")
            zr2 = S.after("sVE")
            ochunks = out_t.rearrange("(c p) d -> c p d", p=128)
            for c in range(8):
                for hf in range(2):
                    S.dma("sync", lambda h, c=c, hf=hf: h.dma_start(
                              out=ochunks[c][:, hf * 2048:(hf + 1) * 2048], in_=SgB[:, 0:2048]),
                          waits=[zr2], inc="sST")
            S.emit("sync", lambda h: h.nop(), waits=[S.after("sST")])

    if dev_phase == "full":
        ctxB.close()
        first_c = {e: True for e in ENGS}
        def wc(eng, waits):
            w = list(waits)
            if first_c[eng]:
                w.append(phaseB_st)
                first_c[eng] = False
            return w
        ctxC1 = ExitStack()
        wopan = ctxC1.enter_context(nc.sbuf_tensor("wopan", [128, NDT * 1024], BF16))
        otw = ctxC1.enter_context(nc.sbuf_tensor("otw", [128, 32 * 128], BF16))
        x1st = ctxC1.enter_context(nc.sbuf_tensor("x1st", [128, 1024], F32))
        ps_wo = [ctxC1.enter_context(nc.psum_tensor("ps_wo0", [128, 1024], F32)),
                 ctxC1.enter_context(nc.psum_tensor("ps_wo1", [128, 1024], F32))]
        x1_d = DS("x1_d", [8, 128, D], F32)

        otw_guard = (None, 0)
        x1_parks = (None, 0)
        for oq in range(4):
            for dt2 in range(NDT):
                S.dma("gpsimd", lambda h, oq=oq, dt2=dt2: h.dma_start(
                          out=wopan[:, dt2 * 1024:(dt2 + 1) * 1024],
                          in_=WoT_i[dt2 * 128:(dt2 + 1) * 128, oq * 1024:(oq + 1) * 1024]),
                      waits=wc("gpsimd", [S.after("sPE")] if dt2 == 0 and oq > 0 else []),
                      inc="sCA")
            pan_rdy = S.after("sCA")
            for i in range(8):
                S.dma("sync", lambda h, i=i: h.dma_start(
                          out=otw[:, :].rearrange("p (hh c) -> p hh c", c=128),
                          in_=ot_d[:, i].rearrange("hh p c -> p hh c")),
                      waits=wc("sync", [otw_guard]), inc="sLD")
                otl = S.after("sLD")
                buf = i % 2
                def mwo(h, buf=buf, i=i):
                    last = None
                    for dt2 in range(NDT):
                        for c5 in range(2):
                            last = nc.tensor.matmul(
                                ps_wo[buf][:, c5 * 512:(c5 + 1) * 512],
                                otw[:, dt2 * 128:(dt2 + 1) * 128],
                                wopan[:, dt2 * 1024 + c5 * 512: dt2 * 1024 + (c5 + 1) * 512],
                                start=(dt2 == 0), stop=(dt2 == NDT - 1))
                    return last
                S.emit("tensor", mwo, waits=wc("tensor", [pan_rdy, otl, x1_parks]), inc="sPE")
                mwd = S.after("sPE")
                S.emit("scalar", lambda h, buf=buf: nc.scalar.copy(out=x1st[:], in_=ps_wo[buf][:]),
                       waits=wc("scalar", [mwd]), inc="sSC")
                evd = S.after("sSC")
                S.dma("sync", lambda h, i=i, oq=oq: h.dma_start(
                          out=x1_d[i][:, oq * 1024:(oq + 1) * 1024], in_=x1st[:]),
                      waits=[evd], inc="sST")
                x1_parks = S.after("sST")
                otw_guard = mwd
        c1_done = S.after("sST")
        ctxC1.close()

        ctxC2 = ExitStack()
        def SC(name, shape, dt):
            return ctxC2.enter_context(nc.sbuf_tensor(name, list(shape), dt))
        def PC(name, shape, dt):
            return ctxC2.enter_context(nc.psum_tensor(name, list(shape), dt))
        x1l = SC("x1l", [128, D], F32)
        hsr = SC("hsr", [128, D], F32)
        x2f = SC("x2f", [128, D], F32)
        x2b = SC("x2b", [128, D], BF16)
        x2T = SC("x2T", [128, NDT * 128], BF16)
        x3f = SC("x3f", [128, D], F32)
        outf = SC("outf", [128, D], F32)
        routb = SC("routb", [128, NDT * 8], BF16)
        acatb = SC("acatb", [128, NDT * 128], BF16)
        bwcb = SC("bwcb", [128, D], BF16)
        e8f = SC("e8f", [8, 128], F32)
        gatep = SC("gatep", [128, 128], F32)
        gT = SC("gT", [8, 128], F32)
        gevac = SC("gevac", [128, 128], F32)
        htil = SC("htil", [128, 128], BF16)
        lnp = SC("lnp", [128, 16], F32)
        ps_t2 = PC("ps_t2", [128, 1024], BF16)
        ps_rt = PC("ps_rt", [128, 512], F32)
        ps_h = PC("ps_h", [128, 512], F32)
        ps_ge = PC("ps_ge", [128, 512], F32)
        ps_gt = PC("ps_gt", [128, 512], F32)
        ps_ff = PC("ps_ff", [128, 1024], F32)
        def lp(j):
            return lnp[:, j:j + 1]

        S.dma("gpsimd", lambda h: h.dma_start(
                  out=routb[:, :].rearrange("p (dt e) -> p dt e", e=8),
                  in_=routeT_i.rearrange("(dt p) e -> p dt e", p=128)),
              waits=[c1_done], inc="sCA")
        S.dma("gpsimd", lambda h: h.dma_start(
                  out=acatb[:, :].rearrange("p (dt c) -> p dt c", c=128),
                  in_=AcatT_i.rearrange("(dt p) c -> p dt c", p=128)), inc="sCA")
        S.dma("gpsimd", lambda h: h.dma_start(out=bwcb[:], in_=BwcatT_i), inc="sCA")
        S.dma("sync", lambda h: h.dma_start(out=e8f[:], in_=E8_i), waits=[c1_done], inc="sLD")
        cld = S.after("sCA")
        cld2 = S.after("sLD")
        ve(lambda h: nc.vector.memset(gatep[:, :], 0.0), waits=[c1_done], inc="sVE")
        gz = S.after("sVE")

        prev_i_done = (None, 0)
        for i in range(8):
            S.dma("sync", lambda h, i=i: h.dma_start(out=x1l[:], in_=x1_d[i]),
                  waits=[prev_i_done], inc="sLD")
            S.dma("sync", lambda h, i=i: h.dma_start(out=hsr[:], in_=hs_own[i * 128:(i + 1) * 128, :]),
                  inc="sLD")
            lds = S.after("sLD")
            ve(lambda h: nc.vector.tensor_add(x1l[:], x1l[:], hsr[:]),
               waits=[lds, gz], inc="sVE")
            r1d = S.after("sVE")
            def st1(h):
                h.drain()
                nc.scalar.activation(out=x2f[:], in_=x1l[:], func=AF.Copy, accum_out=lp(0))
                h.drain()
                return nc.scalar.activation(out=x2f[:], in_=x1l[:], func=AF.Square, accum_out=lp(1))
            S.emit("scalar", st1, waits=wc("scalar", [r1d]) if first_c["scalar"] else [r1d], inc="sSC")
            st1d = S.after("sSC")
            def ln1t(h):
                nc.vector.tensor_scalar(lp(2), lp(0), 1.0 / D, None, op0=ALU.mult)
                h.drain()
                nc.vector.tensor_scalar(lp(3), lp(1), 1.0 / D, None, op0=ALU.mult)
                h.drain()
                nc.vector.tensor_mul(lp(4), lp(2), lp(2))
                h.drain()
                nc.vector.tensor_sub(lp(3), lp(3), lp(4))
                h.drain()
                return nc.vector.tensor_scalar(lp(3), lp(3), 1e-5, None, op0=ALU.add)
            ve(ln1t, waits=[st1d], inc="sVE")
            ln1td = S.after("sVE")
            def rst1(h):
                h.drain()
                nc.scalar.activation(lp(5), lp(3), AF.Ln)
                h.drain()
                return nc.scalar.activation(lp(6), lp(5), AF.Exp, scale=-0.5)
            S.emit("scalar", rst1, waits=[ln1td], inc="sSC")
            rst1d = S.after("sSC")
            def b2v(h):
                nc.vector.tensor_mul(lp(7), lp(2), lp(6))
                h.drain()
                return nc.vector.tensor_scalar(lp(7), lp(7), -1.0, None, op0=ALU.mult)
            ve(b2v, waits=[rst1d], inc="sVE")
            b2d = S.after("sVE")
            S.emit("scalar", lambda h: nc.scalar.activation(out=x2f[:], in_=x1l[:], func=AF.Identity,
                                                            bias=lp(7), scale=lp(6)),
                   waits=[b2d], inc="sSC")
            x2d = S.after("sSC")
            ve(lambda h: nc.vector.tensor_copy(out=x2b[:], in_=x2f[:]), waits=[x2d], inc="sVE")
            x2bd = S.after("sVE")
            for tg in range(4):
                def trx(h, tg=tg):
                    last = None
                    for j2 in range(8):
                        dt2 = tg * 8 + j2
                        last = nc.tensor.transpose(ps_t2[:, j2 * 128:(j2 + 1) * 128],
                                                   x2b[:, dt2 * 128:(dt2 + 1) * 128], ident_s[:])
                    return last
                S.emit("tensor", trx, waits=wc("tensor", [x2bd, S.after("sSC")]), inc="sPE")
                trxd = S.after("sPE")
                S.emit("scalar", lambda h, tg=tg: nc.scalar.copy(
                           out=x2T[:, tg * 1024:(tg + 1) * 1024], in_=ps_t2[:]),
                       waits=[trxd], inc="sSC")
            x2td = S.after("sSC")
            def mrt(h):
                last = None
                for dt2 in range(NDT):
                    last = nc.tensor.matmul(ps_rt[:, 0:8], x2T[:, dt2 * 128:(dt2 + 1) * 128],
                                            routb[:, dt2 * 8:(dt2 + 1) * 8],
                                            start=(dt2 == 0), stop=(dt2 == NDT - 1))
                return last
            S.emit("tensor", mrt, waits=[x2td, cld, S.after("sVE")], inc="sPE")
            mrtd = S.after("sPE")
            def top1(h):
                nc.vector.tensor_copy(gatep[:, 0:8], ps_rt[:, 0:8])
                h.drain()
                nc.vector.tensor_reduce(lp(8), gatep[:, 0:8], axis=AX.X, op=ALU.max)
                h.drain()
                nc.vector.tensor_scalar(gatep[:, 8:16], gatep[:, 0:8], lp(8), None, op0=ALU.is_ge)
                h.drain()
                nc.vector.memset(gatep[:, 16:24], 0.0)
                h.drain()
                nc.vector.tensor_copy(gatep[:, 17:24], gatep[:, 8:15])
                h.drain()
                nc.vector.tensor_copy(gatep[:, 24:32], gatep[:, 16:24])
                h.drain()
                nc.vector.tensor_add(gatep[:, 26:32], gatep[:, 26:32], gatep[:, 16:22])
                h.drain()
                nc.vector.tensor_copy(gatep[:, 32:40], gatep[:, 24:32])
                h.drain()
                nc.vector.tensor_add(gatep[:, 36:40], gatep[:, 36:40], gatep[:, 24:28])
                h.drain()
                nc.vector.tensor_scalar(gatep[:, 40:48], gatep[:, 32:40], 0.5, None, op0=ALU.is_lt)
                h.drain()
                nc.vector.tensor_mul(gatep[:, 48:56], gatep[:, 8:16], gatep[:, 40:48])
                h.drain()
                nc.vector.tensor_scalar(gatep[:, 56:64], gatep[:, 48:56], lp(8), None, op0=ALU.mult)
                h.drain()
                return nc.vector.tensor_copy(gatep[:, 0:8], gatep[:, 56:64])
            ve(top1, waits=[mrtd], inc="sVE")
            t1d = S.after("sVE")
            def mh(h):
                last = None
                for dt2 in range(NDT):
                    last = nc.tensor.matmul(ps_h[:, 0:128], acatb[:, dt2 * 128:(dt2 + 1) * 128],
                                            x2T[:, dt2 * 128:(dt2 + 1) * 128],
                                            start=(dt2 == 0), stop=(dt2 == NDT - 1))
                return last
            S.emit("tensor", mh, waits=[x2td], inc="sPE")
            mhd = S.after("sPE")
            def mgt(h):
                return nc.tensor.transpose(ps_gt[:, 0:128], gatep[:, 0:128], identf_s[:])
            S.emit("tensor", mgt, waits=[t1d], inc="sPE")
            mgtd = S.after("sPE")
            S.emit("scalar", lambda h: nc.scalar.copy(out=gT[:], in_=ps_gt[0:8, 0:128]),
                   waits=[mgtd], inc="sSC")
            gtd = S.after("sSC")
            def mge(h):
                return nc.tensor.matmul(ps_ge[:, 0:128], e8f[:], gT[:], start=True, stop=True)
            S.emit("tensor", mge, waits=[gtd, cld2], inc="sPE")
            mged = S.after("sPE")
            S.emit("scalar", lambda h: nc.scalar.copy(out=gevac[:], in_=ps_ge[:, 0:128]),
                   waits=[mged], inc="sSC")
            gevd = S.after("sSC")
            ve(lambda h: nc.vector.tensor_mul(htil[:], ps_h[:, 0:128], gevac[:]),
               waits=[mhd, gevd], inc="sVE")
            htd = S.after("sVE")
            for oq2 in range(4):
                def mff(h, oq2=oq2):
                    nc.tensor.matmul(ps_ff[:, 0:512], htil[:],
                                     bwcb[:, oq2 * 1024: oq2 * 1024 + 512], start=True, stop=True)
                    return nc.tensor.matmul(ps_ff[:, 512:1024], htil[:],
                                            bwcb[:, oq2 * 1024 + 512: (oq2 + 1) * 1024],
                                            start=True, stop=True)
                S.emit("tensor", mff, waits=[htd] + ([S.after("sVE")] if oq2 else []), inc="sPE")
                mffd = S.after("sPE")
                def rx3(h, oq2=oq2):
                    return nc.vector.tensor_add(x3f[:, oq2 * 1024:(oq2 + 1) * 1024],
                                                ps_ff[:, 0:1024], x2f[:, oq2 * 1024:(oq2 + 1) * 1024])
                ve(rx3, waits=[mffd], inc="sVE")
            x3d = S.after("sVE")
            def st2(h):
                h.drain()
                nc.scalar.activation(out=outf[:], in_=x3f[:], func=AF.Copy, accum_out=lp(0))
                h.drain()
                return nc.scalar.activation(out=outf[:], in_=x3f[:], func=AF.Square, accum_out=lp(1))
            S.emit("scalar", st2, waits=[x3d], inc="sSC")
            st2d = S.after("sSC")
            ve(ln1t, waits=[st2d], inc="sVE")
            ln2td = S.after("sVE")
            S.emit("scalar", rst1, waits=[ln2td], inc="sSC")
            rst2d = S.after("sSC")
            ve(b2v, waits=[rst2d], inc="sVE")
            b2d2 = S.after("sVE")
            S.emit("scalar", lambda h: nc.scalar.activation(out=outf[:], in_=x3f[:], func=AF.Identity,
                                                            bias=lp(7), scale=lp(6)),
                   waits=[b2d2], inc="sSC")
            od = S.after("sSC")
            S.dma("sync", lambda h, i=i: h.dma_start(out=out_t[i * 128:(i + 1) * 128, :], in_=outf[:]),
                  waits=[od], inc="sST")
            prev_i_done = S.after("sST")
        S.emit("sync", lambda h: h.nop(), waits=[S.after("sST")])
        ctxC2.close()

    if dev_phase in ("A", "Araw"):
        S.dma("sync", lambda h: h.dma_start(out=dev["dbg_qrt"], in_=qrt_d), waits=[phaseA_st], inc="sST")
        if dev_phase == "A":
            S.dma("sync", lambda h: h.dma_start(out=dev["dbg_krt"], in_=krt_d), inc="sST")
            S.dma("sync", lambda h: h.dma_start(out=dev["dbg_v"], in_=v_d), inc="sST")
        S.emit("vector", lambda h: nc.vector.memset(qk_tmp[:, :], 0.0), inc="sVE")
        zr = S.after("sVE")
        ochunks = out_t.rearrange("(c p) d -> c p d", p=128)
        for c in range(8):
            for hf in range(2):
                S.dma("sync", lambda h, c=c, hf=hf: h.dma_start(
                          out=ochunks[c][:, hf * 2048:(hf + 1) * 2048], in_=qk_tmp[:, :]),
                      waits=[zr], inc="sST")
        S.emit("sync", lambda h: h.nop(), waits=[S.after("sST")])

    if dev_phase == "B":
        ctxB.close()
    if dev_phase in ("A", "Araw"):
        ctxA.close()

    with nc.Block() as blk:
        @blk.sync
        def _(h):
            for op in S.ops["sync"]:
                op(h)
        @blk.gpsimd
        def _(h):
            for op in S.ops["gpsimd"]:
                op(h)
        @blk.vector
        def _(h):
            for op in S.ops["vector"]:
                op(h)
        @blk.scalar
        def _(h):
            for op in S.ops["scalar"]:
                op(h)
        @blk.tensor
        def _(h):
            for op in S.ops["tensor"]:
                op(h)

    ctx.close()
    return nc


_PROG_CACHE = {}


def kernel(**inputs):
    wts = host_weights(inputs)
    hs_all = np.asarray(inputs["L0_attn_hidden_states"], dtype=np.float32)
    if "prog" not in _PROG_CACHE:
        _PROG_CACHE["prog"] = build_program(dev_phase="full")
        _PROG_CACHE["consts"] = {p: host_constants(p) for p in (0, 1)}
    nc = _PROG_CACHE["prog"]
    in_maps = []
    for c in range(8):
        b, p = c // 2, c % 2
        owncols = np.concatenate([np.arange(qb * 128, (qb + 1) * 128) for qb in OWN[p]])
        hsb = np.ascontiguousarray(hs_all[b])
        in_maps.append(dict(hs=hsb, hs_own=np.ascontiguousarray(hsb[owncols]),
                            **wts, **_PROG_CACHE["consts"][p]))
    res = run_bass_kernel_spmd(nc, in_maps, core_ids=list(range(8)))
    out = np.empty((B, SQ, D), np.float32)
    for c in range(8):
        b, p = c // 2, c % 2
        r = np.asarray(res.results[c]["out"])
        for i, qb in enumerate(OWN[p]):
            out[b, qb * 128:(qb + 1) * 128, :] = r[i * 128:(i + 1) * 128, :]
    return out


# version 03
# speedup vs baseline: 1.0359x; 1.0359x over previous
import math
import numpy as np
import ml_dtypes
import concourse.bass as bass
import concourse.mybir as mybir
from concourse.bass_utils import run_bass_kernel_spmd
from contextlib import ExitStack

F32 = mybir.dt.float32
BF16 = mybir.dt.bfloat16
AF = mybir.ActivationFunctionType
ALU = mybir.AluOpType
AX = mybir.AxisListType
BF = ml_dtypes.bfloat16

B, SQ, D, H, HKV, HD = 4, 2048, 4096, 32, 8, 128
NQB, NDT = SQ // 128, D // 128
NREM = 205
NEG30 = -1.0e30
SQRT_HD = float(np.float32(np.sqrt(np.float32(HD))))
OWN = {0: [0, 3, 4, 7, 8, 11, 12, 15], 1: [1, 2, 5, 6, 9, 10, 13, 14]}
PAIR_OFF = {0: (0, 384), 1: (128, 256)}
N_IT = 8
TARGET = 200.5

ENGS = ("sync", "gpsimd", "vector", "scalar", "tensor")


class Sched:
    def __init__(self, nc, ctx):
        self.nc = nc
        self.ops = {e: [] for e in ENGS}
        self.ctr = {}
        self.ctx = ctx

    def sem(self, name):
        s = self.ctx.enter_context(self.nc.semaphore(name))
        self.ctr[name] = [s, 0]

    def val(self, name):
        return self.ctr[name][1]

    def after(self, name):
        return (name, self.ctr[name][1])

    def emit(self, eng, fn, waits=(), inc=None, n=1):
        if not hasattr(self, "log"):
            self.log = []
        self.log.append((eng, len(self.ops[eng]), [w for w in waits if w is not None],
                         inc, n, getattr(fn, "__qualname__", str(fn))))
        if inc is not None:
            self.ctr[inc][1] += n
        sem = self.ctr[inc][0] if inc is not None else None
        wl = [(self.ctr[w][0], v) for (w, v) in waits if v > 0]
        def op(h, fn=fn, wl=wl, sem=sem, n=n):
            for (s, v) in wl:
                h.wait_ge(s, v)
            ins = fn(h)
            if sem is not None:
                ins.then_inc(sem, n)
        self.ops[eng].append(op)

    def dma(self, eng, fn, waits=(), inc=None):
        self.emit(eng, fn, waits, inc, n=16)


def host_constants(parity):
    inv_freq = (1.0 / (np.float32(10000.0) ** (np.arange(0, HD, 2, dtype=np.float32) / np.float32(HD)))).astype(np.float32)
    t = np.arange(SQ, dtype=np.float32)
    freqs = np.outer(t, inv_freq).astype(np.float32)
    emb = np.concatenate([freqs, freqs], axis=-1)
    cos = np.cos(emb).astype(np.float32)
    sin = np.sin(emb).astype(np.float32)
    cosT = np.ascontiguousarray(cos.T)
    sswapT = np.ascontiguousarray(sin.T).copy()
    sswapT[:64, :] = -sswapT[:64, :]
    own = OWN[parity]
    owncols = np.concatenate([np.arange(qb * 128, (qb + 1) * 128) for qb in own])
    cosT_own = np.ascontiguousarray(cosT[:, owncols])
    sswapT_own = np.ascontiguousarray(sswapT[:, owncols])
    ident = np.eye(128, dtype=np.float32)
    causal_add = np.where(np.arange(128)[:, None] >= np.arange(128)[None, :], 0.0,
                          np.float32(NEG30)).astype(np.float32)
    valid = np.stack([qb * 128 + np.arange(128) + 1 for qb in own], axis=1).astype(np.float32)
    sel = (valid > NREM).astype(np.float32)
    from statistics import NormalDist
    nd = NormalDist()
    frac = np.clip(1.0 - NREM / np.maximum(valid, NREM + 1.0), 1e-6, 1 - 1e-6)
    zq = np.vectorize(nd.inv_cdf)(frac).astype(np.float32)
    phi = (np.exp(-0.5 * zq * zq) / np.sqrt(2 * np.pi)).astype(np.float32)
    sl0 = (1.0 / np.maximum(valid * phi, 1.0)).astype(np.float32)
    iota8 = np.tile(np.arange(8, dtype=np.float32)[None, :], (128, 1)).astype(np.float32)
    maskin = np.zeros((8, 128, 256), np.float32)
    for i, qb in enumerate(own):
        kpad = 256 * (i + 1)
        kcols = np.arange(kpad - 256, kpad)
        qpos = qb * 128 + np.arange(128)
        maskin[i] = np.where(kcols[None, :] <= qpos[:, None], 0.0, np.float32(NEG30))
    selb = np.repeat(sel.astype(np.int32)[:, :, None], 8, axis=2).reshape(128, 64)
    return dict(cosT=cosT, sswapT=sswapT, cosT_own=cosT_own, sswapT_own=sswapT_own,
                identc=ident, validc=valid, selc=sel, selbc=selb, zqc=zq, sl0c=sl0,
                iota8c=iota8, onesc=np.ones((1, 128), np.float32),
                sqhdc=np.full((128, 1), SQRT_HD, np.float32),
                maskin=maskin.reshape(8, 128 * 256))


def host_weights(inputs):
    WqT = np.ascontiguousarray(np.asarray(inputs["L0_attn_Wq"]).T)
    WkT = np.ascontiguousarray(np.asarray(inputs["L0_attn_Wk"]).T)
    WvT = np.ascontiguousarray(np.asarray(inputs["L0_attn_Wv"]).T)
    WoT = np.ascontiguousarray(np.asarray(inputs["L0_attn_Wo"]).T)
    routeT = np.ascontiguousarray(np.asarray(inputs["L0_ffn_route_w"]).T)
    A = np.asarray(inputs["L0_ffn_A"])
    AcatT = np.ascontiguousarray(A.reshape(128, D).T)
    Bw = np.asarray(inputs["L0_ffn_Bw"])
    BwcatT = np.ascontiguousarray(Bw.transpose(0, 2, 1).reshape(128, D))
    E8 = np.zeros((8, 128), dtype=np.float32)
    for e in range(8):
        E8[e, e * 16:(e + 1) * 16] = 1.0
    return dict(WqT=WqT, WkT=WkT, WvT=WvT, WoT=WoT, routeT=routeT,
                AcatT=AcatT, BwcatT=BwcatT, E8=E8)


def build_program(dev_phase="full"):
    own = OWN[0]
    nc = bass.Bass("TRN2", target_bir_lowering=False)

    def DI(name, shape, dt=F32):
        return nc.dram_tensor(name, list(shape), dt, kind="ExternalInput").ap()
    def DS(name, shape, dt):
        return nc.dram_tensor(name, list(shape), dt, kind="Internal").ap()

    hs = DI("hs", [SQ, D])
    hs_own = DI("hs_own", [1024, D])
    WqT = DI("WqT", [D, D]); WkT_i = DI("WkT", [D, HKV * HD]); WvT_i = DI("WvT", [D, HKV * HD])
    WoT_i = DI("WoT", [D, D])
    routeT_i = DI("routeT", [D, 8]); AcatT_i = DI("AcatT", [D, 128]); BwcatT_i = DI("BwcatT", [128, D])
    E8_i = DI("E8", [8, 128])
    cosT_i = DI("cosT", [HD, SQ]); sswapT_i = DI("sswapT", [HD, SQ])
    cosTo_i = DI("cosT_own", [HD, 1024]); sswapTo_i = DI("sswapT_own", [HD, 1024])
    ident_i = DI("identc", [128, 128])
    maskin_i = DI("maskin", [8, 128 * 256])
    valid_i = DI("validc", [128, 8]); sel_i = DI("selc", [128, 8])
    selb_i = nc.dram_tensor("selbc", [128, 64], mybir.dt.int32, kind="ExternalInput").ap()
    zq_i = DI("zqc", [128, 8]); sl0_i = DI("sl0c", [128, 8])
    ones_i = DI("onesc", [1, 128]); iota8_i = DI("iota8c", [128, 8])
    sqhd_i = DI("sqhdc", [128, 1])

    out_t = nc.dram_tensor("out", [1024, D], F32, kind="ExternalOutput").ap()

    qrt_d = DS("qrt_d", [H, HD, 1024], BF16)
    krt_d = DS("krt_d", [HKV, HD, SQ], BF16)
    v_d = DS("v_d", [HKV, NQB, 128, HD], BF16)
    ot_d = DS("ot_d", [H, 8, HD, 128], BF16)

    dev = {}
    if dev_phase == "B":
        dev["dbg_ot"] = nc.dram_tensor("dbg_ot", [H, 8, HD, 128], BF16, kind="ExternalOutput").ap()
        dev["dbg_stp"] = nc.dram_tensor("dbg_stp", [32, 128, 28 * 8], F32, kind="ExternalOutput").ap()
        dev["dbg_m8"] = nc.dram_tensor("dbg_m8", [32, 128, 128], F32, kind="ExternalOutput").ap()
        dev["dbg_sg"] = nc.dram_tensor("dbg_sg", [128, 8 * 2048], F32, kind="ExternalOutput").ap()
    if dev_phase in ("A", "Araw"):
        dev["dbg_qrt"] = nc.dram_tensor("dbg_qrt", [H, HD, 1024], BF16, kind="ExternalOutput").ap()
        if dev_phase == "A":
            dev["dbg_krt"] = nc.dram_tensor("dbg_krt", [HKV, HD, SQ], BF16, kind="ExternalOutput").ap()
            dev["dbg_v"] = nc.dram_tensor("dbg_v", [HKV, NQB, 128, HD], BF16, kind="ExternalOutput").ap()

    ctx = ExitStack()
    def SB(name, shape, dt):
        return ctx.enter_context(nc.sbuf_tensor(name, list(shape), dt))
    def PS(name, shape, dt):
        return ctx.enter_context(nc.psum_tensor(name, list(shape), dt))

    S = Sched(nc, ctx)
    for s in ("sLD", "sCA", "sW", "sPE", "sVE", "sSC", "sST", "sGP", "sEV", "sSG", "sAV"):
        S.sem(s)

    ident_s = SB("ident_s", [128, 128], BF16)
    maskin_s = SB("maskin_s", [128, 8 * 256], F32)
    valid_s = SB("valid_s", [128, 8], F32)
    sel_s = SB("sel_s", [128, 8], F32)
    zq_s = SB("zq_s", [128, 8], F32)
    sl0_s = SB("sl0_s", [128, 8], F32)
    ones_s = SB("ones_s", [1, 128], F32)
    iota8_s = SB("iota8_s", [128, 8], F32)
    selb_s = SB("selb_s", [128, 64], mybir.dt.int32)
    sqhd_s = SB("sqhd_s", [128, 1], F32)
    sinv_s = SB("sinv_s", [128, 1], F32)
    identf_s = SB("identf_s", [128, 128], F32)

    ctxA = ExitStack()
    def SA(name, shape, dt):
        return ctxA.enter_context(nc.sbuf_tensor(name, list(shape), dt))
    def PA(name, shape, dt):
        return ctxA.enter_context(nc.psum_tensor(name, list(shape), dt))
    wpan = SA("wpan", [128, 2 * NDT * 128], BF16)
    hs_bf = SA("hs_bf", [128, 2 * D], BF16)
    qk_tmp = SA("qk_tmp", [128, 2048], F32)
    qk_tmp2 = SA("qk_tmp2", [128, 2048], F32)
    qk_stage = SA("qk_stage", [128, 2048], BF16)
    v_stage = qk_stage
    vn_stage = SA("vn_stage", [128, 2048], BF16)

    ps_pj = PA("ps_pj", [128, 2048], F32)
    ps_tr = PA("ps_tr", [128, 512], BF16)
    cosT_s = SA("cosT_s", [HD, SQ], F32)
    sswapT_s = SA("sswapT_s", [HD, SQ], F32)
    cosTo_s = SA("cosTo_s", [HD, 1024], F32)
    sswapTo_s = SA("sswapTo_s", [HD, 1024], F32)

    for (dname, src) in (("cosT_s", cosT_i), ("sswapT_s", sswapT_i), ("cosTo_s", cosTo_i),
                         ("sswapTo_s", sswapTo_i), ("valid_s", valid_i),
                         ("sel_s", sel_i), ("zq_s", zq_i), ("sl0_s", sl0_i), ("ones_s", ones_i),
                         ("iota8_s", iota8_i), ("sqhd_s", sqhd_i), ("identf_s", ident_i)):
        dst = {"cosT_s": cosT_s, "sswapT_s": sswapT_s, "cosTo_s": cosTo_s,
               "sswapTo_s": sswapTo_s, "valid_s": valid_s,
               "sel_s": sel_s, "zq_s": zq_s, "sl0_s": sl0_s, "ones_s": ones_s,
               "iota8_s": iota8_s, "sqhd_s": sqhd_s, "identf_s": identf_s}[dname]
        S.dma("sync", lambda h, d=dst, s=src: h.dma_start(out=d[:], in_=s), inc="sLD")
    S.dma("gpsimd", lambda h: h.dma_start(out=ident_s[:], in_=ident_i), inc="sCA")
    S.dma("sync", lambda h: h.dma_start(out=selb_s[:], in_=selb_i), inc="sLD")
    S.dma("sync", lambda h: h.dma_start(
              out=maskin_s.rearrange("p (i c) -> p i c", c=256),
              in_=maskin_i.rearrange("i (p c) -> p i c", p=128)), inc="sLD")
    consts_ld = S.after("sLD")
    consts_ca = S.after("sCA")
    S.emit("vector", lambda h: nc.vector.reciprocal(sinv_s[:], sqhd_s[:]),
           waits=[consts_ld], inc="sVE")
    sinv_ready = S.after("sVE")

    def build_hsT(src_ap, ntiles, dst, seg_w, tag, pre=(None, 0)):
        evg = (None, 0)
        for g in range(ntiles // 2):
            for i2 in range(2):
                tt = g * 2 + i2
                w = ([evg] if g > 0 and i2 == 0 else []) + ([pre] if g == 0 and i2 == 0 else [])
                S.dma("gpsimd",
                      lambda h, i2=i2, tt=tt: h.dma_start(out=hs_bf[:, i2 * D:(i2 + 1) * D],
                                                          in_=src_ap[tt * 128:(tt + 1) * 128, :]),
                      waits=w, inc="sCA")
            ca_done = S.after("sCA")
            for dt in range(NDT):
                for i2 in range(2):
                    w = [ca_done, S.after("sSC")] if i2 == 0 else []
                    S.emit("tensor",
                           lambda h, i2=i2, dt=dt: nc.tensor.transpose(
                               ps_tr[:, i2 * 128:(i2 + 1) * 128],
                               hs_bf[:, i2 * D + dt * 128: i2 * D + (dt + 1) * 128], ident_s[:]),
                           waits=w, inc=("sPE" if i2 == 1 else None))
                S.emit("scalar",
                       lambda h, dt=dt, g=g, dst=dst, seg_w=seg_w: nc.scalar.copy(
                           out=dst[:, dt * seg_w + g * 256: dt * seg_w + (g + 1) * 256],
                           in_=ps_tr[:, 0:256]),
                       waits=[S.after("sPE")], inc="sSC")
            evg = S.after("sSC")
        return S.after("sSC")

    ctxAo = ExitStack()
    hsT_own = ctxAo.enter_context(nc.sbuf_tensor("hsT_own", [128, NDT * 1024], BF16))
    hsTo_done = build_hsT(hs_own, 8, hsT_own, 1024, "own")

    def wsrc(kind, idx):
        if kind == "q":
            return WqT[:, idx * 128:(idx + 1) * 128]
        if kind == "k":
            return WkT_i[:, idx * 128:(idx + 1) * 128]
        return WvT_i[:, idx * 128:(idx + 1) * 128]

    pj_free = (None, 0)
    panel_hist = []

    def proj_group(otiles, hsT_src, seg_w, first_ready, t_off=0):
        nonlocal pj_free
        for jj, (kind, idx) in enumerate(otiles):
            slot = jj % 2
            S.dma("gpsimd",
                  lambda h, slot=slot, kind=kind, idx=idx: h.dma_start(
                      out=wpan[:, slot * NDT * 128:(slot + 1) * NDT * 128]
                          .rearrange("p (dt c) -> p dt c", c=128),
                      in_=wsrc(kind, idx).rearrange("(dt p) c -> p dt c", p=128)),
                  waits=([panel_hist[-2]] if len(panel_hist) >= 2 else []),
                  inc="sW")
            w_ready = S.after("sW")
            first_waits = [w_ready, first_ready, pj_free]
            nmm = seg_w // 512
            for dt in range(NDT):
                pan = wpan[:, slot * NDT * 128 + dt * 128: slot * NDT * 128 + (dt + 1) * 128]
                st, fin = (dt == 0), (dt == NDT - 1)
                def mmp(h, pan=pan, dt=dt, st=st, fin=fin, nmm=nmm, seg_w=seg_w, hsT_src=hsT_src):
                    last = None
                    for c in range(nmm):
                        last = nc.tensor.matmul(ps_pj[:, c * 512:(c + 1) * 512], pan,
                                                hsT_src[:, dt * seg_w + c * 512: dt * seg_w + (c + 1) * 512],
                                                start=st, stop=fin)
                    return last
                S.emit("tensor", mmp, waits=(first_waits if dt == 0 else []),
                       inc=("sPE" if fin else None))
            panel_hist.append(S.after("sPE"))
            mm_done = S.after("sPE")

            if kind in ("q", "k"):
                W = seg_w
                ct = cosTo_s if kind == "q" else cosT_s
                sw = sswapTo_s if kind == "q" else sswapT_s
                S.emit("vector", lambda h, W=W, ct=ct, t_off=t_off: nc.vector.tensor_mul(
                           qk_tmp[:, :W], ps_pj[:, :W], ct[:, t_off:t_off + W]),
                       waits=[mm_done, sinv_ready])
                S.emit("vector", lambda h, W=W, sw=sw, t_off=t_off: nc.vector.tensor_mul(
                           qk_tmp2[0:64, :W], ps_pj[64:128, :W], sw[0:64, t_off:t_off + W]))
                S.emit("vector", lambda h, W=W, sw=sw, t_off=t_off: nc.vector.tensor_mul(
                           qk_tmp2[64:128, :W], ps_pj[0:64, :W], sw[64:128, t_off:t_off + W]))
                def rope_add(h, W=W):
                    h.drain()
                    return nc.vector.tensor_add(qk_stage[:, :W], qk_tmp[:, :W], qk_tmp2[:, :W])
                S.emit("vector", rope_add, inc="sVE")
                pj_free = S.after("sVE")
                rope_done = S.after("sVE")
                dst = qrt_d[idx] if kind == "q" else krt_d[idx][:, t_off:t_off + W]
                S.dma("sync", lambda h, dst=dst, W=W: h.dma_start(out=dst, in_=qk_stage[:, :W]),
                      waits=[rope_done], inc="sST")
                park = S.after("sST")
                S.emit("vector", lambda h: nc.vector.engine_nop(), waits=[park])
            else:
                S.emit("scalar", lambda h: nc.scalar.copy(out=v_stage[:], in_=ps_pj[:]),
                       waits=[mm_done, S.after("sST")], inc="sSC")
                pj_free = S.after("sSC")
                ev_done = S.after("sSC")
                nkt4 = seg_w // 512
                for ktg in range(nkt4):
                    for i2 in range(4):
                        kt = ktg * 4 + i2
                        w = [ev_done, S.after("sSC")] if i2 == 0 else []
                        S.emit("tensor",
                               lambda h, kt=kt, i2=i2: nc.tensor.transpose(
                                   ps_tr[:, i2 * 128:(i2 + 1) * 128],
                                   v_stage[:, kt * 128:(kt + 1) * 128], ident_s[:]),
                               waits=w, inc=("sPE" if i2 == 3 else None))
                    S.emit("scalar",
                           lambda h, ktg=ktg: nc.scalar.copy(
                               out=vn_stage[:, ktg * 512:(ktg + 1) * 512], in_=ps_tr[:]),
                           waits=[S.after("sPE")], inc="sSC")
                vt_done = S.after("sSC")
                kt0 = t_off // 128
                S.dma("sync", lambda h, idx=idx, kt0=kt0, seg_w=seg_w: h.dma_start(
                          out=v_d[idx][kt0:kt0 + seg_w // 128].rearrange("kt p c -> p kt c"),
                          in_=vn_stage[:, 0:seg_w].rearrange("p (kt c) -> p kt c", c=128)),
                      waits=[vt_done], inc="sST")
                park = S.after("sST")
                S.emit("scalar", lambda h: nc.scalar.copy(out=v_stage[0:1, 0:1], in_=v_stage[0:1, 0:1]),
                       waits=[park])

    proj_group([("q", i) for i in range(32)], hsT_own, 1024, hsTo_done)
    ctxAo.close()
    hsT = ctxA.enter_context(nc.sbuf_tensor("hsT", [128, NDT * 1024], BF16))
    for half in range(2):
        hsT_h_done = build_hsT(hs[half * 1024:(half + 1) * 1024, :], 8, hsT, 1024, "full%d" % half,
                               pre=S.after("sPE"))
        proj_group([("k", i) for i in range(8)] + [("v", i) for i in range(8)],
                   hsT, 1024, hsT_h_done, t_off=half * 1024)

    phaseA_st = S.after("sST")

    if dev_phase in ("B", "full"):
        ctxA.close()
        ctxB = ExitStack()
        def SBX(name, shape, dt):
            return ctxB.enter_context(nc.sbuf_tensor(name, list(shape), dt))
        def PSX(name, shape, dt):
            return ctxB.enter_context(nc.psum_tensor(name, list(shape), dt))
        SgB = SBX("SgB", [128, 8 * 2048], F32)
        SgS = SBX("SgS", [128, 8 * 1024], F32)
        Ppre2 = SBX("Ppre2", [128, 2 * 2048], F32)
        Bscr = SBX("Bscr", [128, 2048], F32)
        Ascr = SBX("Ascr", [128, 2048], F32)
        Tbs = SBX("Tbs", [128, 2048], BF16)
        Tas = SBX("Tas", [128, 2048], BF16)
        junkb = SBX("junkb", [128, 2048], BF16)
        Pfin2 = SBX("Pfin2", [128, 2 * 2048], BF16)
        PTb = SBX("PTb", [128, 16 * 512], BF16)
        krt2 = SBX("krt2", [128, 2 * 2048], BF16)
        v2 = SBX("v2", [128, 2 * 2 * 2048], BF16)
        qrt2 = SBX("qrt2", [128, 8 * 128], BF16)
        NPL = 28
        stp = SBX("stp", [128, 2 * NPL * 8], F32)
        NPI = 6
        stpi = SBX("stpi", [128, 2 * NPI * 8], mybir.dt.int32)
        W64 = SBX("W64", [128, 64], F32)
        Gb64 = SBX("Gb64", [128, 64], F32)
        Ga64 = SBX("Ga64", [128, 64], F32)
        m8b64 = SBX("m8b64", [128, 64], F32)
        m8a64 = SBX("m8a64", [128, 64], F32)
        negc8 = SBX("negc8", [128, 8], F32)
        ot_stage = SBX("ot_stage", [128, 2 * 512], BF16)
        sinvn_s = SBX("sinvn_s", [128, 1], F32)
        ps_S = [PSX("ps_S0", [128, 512], F32), PSX("ps_S1", [128, 512], F32)]
        ps_pt = [PSX("ps_pt0", [128, 1024], BF16), PSX("ps_pt1", [128, 1024], BF16)]
        ps_av = [PSX("ps_av0", [128, 512], F32), PSX("ps_av1", [128, 512], F32)]

        PL = {n: j for j, n in enumerate(
            ["MU", "SQS", "SD", "T", "SL", "NEGT", "TP", "CP", "C", "RMIN",
             "B0", "VMS", "VMB", "VMA", "SIDE", "IDX", "Z", "ZI", "TLO", "THI",
             "SELB", "X0", "X1", "X2", "X3", "X4", "PAD", "PAD2"])}
        PLI = {n: j for j, n in enumerate(["UP", "DN", "O", "SIDE", "SELB", "OK"])}
        def pli(n, sl):
            j = sl * NPI + PLI[n]
            return stpi[:, j * 8:(j + 1) * 8]
        def pl(n, sl):
            j = sl * NPL + PL[n]
            return stp[:, j * 8:(j + 1) * 8]
        def plc(n, sl, h8):
            j = sl * NPL + PL[n]
            return stp[:, j * 8 + h8: j * 8 + h8 + 1]

        ORDER = [4, 0, 5, 1, 6, 2, 7, 3]
        tiles = [(oct_, i) for oct_ in range(4) for i in ORDER]
        NT = len(tiles)
        def tinfo(k):
            oct_, i = tiles[k]
            return oct_, i, 256 * (i + 1), 2 * (i + 1), k % 2
        ev = [dict() for _ in range(NT)]
        mm_ev = []
        evac_ev = []
        N_CNT = 8
        NVH = 1
        INTERLEAVE_EXC = True
        PIPE_DEEP = True
        W_LO, W_HI, TGT = float(NREM - 8), float(NREM + 7), 204.5
        V = "vector"

        bar = phaseA_st
        first_b = {e: True for e in ENGS}
        def wb(eng, waits):
            w = [x for x in waits if x is not None]
            if first_b[eng]:
                w.append(bar)
                first_b[eng] = False
            return w
        def ve(fn, waits=(), inc=None):
            S.emit(V, fn, waits, inc)

        S.emit("vector", lambda h: nc.vector.tensor_scalar(
                   sinvn_s[:], sinv_s[:], -1.0, None, op0=ALU.mult),
               waits=wb("vector", [sinv_ready]), inc="sVE")
        S.emit("vector", lambda h: nc.vector.memset(stp[:, :], 0.0))
        S.emit("vector", lambda h: nc.vector.memset(negc8[:, :], -1e29), inc="sVE")
        stp_z = S.after("sVE")

        def Sg_of(k):
            return SgB if k % 2 == 0 else SgS
        def sgw(k):
            return 2048 if k % 2 == 0 else 1024

        def emit_load(u):
            oct_, i, kpad, nkt, sl = tinfo(u)
            kv0 = oct_ * 2
            w0 = [("sAV", u - 1)] if u >= 2 else []
            if mm_ev:
                w0.append(mm_ev[-1])
            S.dma("sync", lambda h, oct_=oct_, i=i: h.dma_start(
                      out=qrt2[:, :].rearrange("p (hh q) -> p hh q", q=128),
                      in_=qrt_d[oct_ * 8:(oct_ + 1) * 8, :, i * 128:(i + 1) * 128]
                          .rearrange("hh p q -> p hh q")),
                  waits=wb("sync", w0), inc="sLD")
            for g in range(2):
                S.dma("sync", lambda h, g=g, kv=kv0 + g, kpad=kpad: h.dma_start(
                          out=krt2[:, g * 2048: g * 2048 + kpad],
                          in_=krt_d[kv][:, 0:kpad]),
                      inc="sLD")
                S.dma("sync", lambda h, g=g, kv=kv0 + g, kpad=kpad, nkt=nkt, sl=sl: h.dma_start(
                          out=v2[:, sl * 4096 + g * 2048: sl * 4096 + g * 2048 + kpad]
                              .rearrange("p (kt c) -> p kt c", c=128),
                          in_=v_d[kv][0:nkt].rearrange("kt p c -> p kt c")),
                      inc="sLD")
            ev[u]["ld"] = S.after("sLD")

        def emit_smm(u):
            oct_, i, kpad, nkt, sl = tinfo(u)
            nch = (kpad + 511) // 512
            ev[u]["nch"] = nch
            ev[u]["mm0"] = len(mm_ev)
            sg_free = ("sSG", u - 1) if u >= 2 else None
            first = True
            for h8 in range(8):
                g = h8 // 4
                for c in range(nch):
                    cw = min(512, kpad - c * 512)
                    gcc = len(mm_ev)
                    pp = gcc % 2
                    wv = []
                    if first:
                        wv = wb("tensor", [ev[u]["ld"], sg_free])
                        first = False
                    if gcc >= 2:
                        wv.append(("sEV", gcc - 1))
                    def mm(h, h8=h8, g=g, c=c, cw=cw, pp=pp):
                        return nc.tensor.matmul(
                            ps_S[pp][:, 0:cw],
                            qrt2[:, h8 * 128:(h8 + 1) * 128],
                            krt2[:, g * 2048 + c * 512: g * 2048 + c * 512 + cw],
                            start=True, stop=True)
                    S.emit("tensor", mm, waits=wv, inc="sPE")
                    mm_ev.append(S.after("sPE"))

        def build_evq(u):
            oct_, i, kpad, nkt, sl = tinfo(u)
            Sg = Sg_of(u); W = sgw(u)
            nch = ev[u]["nch"]
            q = []
            gcc = ev[u]["mm0"]
            for h8 in range(8):
                for c in range(nch):
                    cw = min(512, kpad - c * 512)
                    def evf(h, h8=h8, c=c, cw=cw, pp=gcc % 2, Sg=Sg, W=W):
                        return nc.scalar.copy(
                            out=Sg[:, h8 * W + c * 512: h8 * W + c * 512 + cw],
                            in_=ps_S[pp][:, 0:cw])
                    q.append((evf, [mm_ev[gcc]], "ev", gcc))
                    gcc += 1
                def sqf(h, h8=h8, Sg=Sg, W=W, kpad=kpad, sl=sl):
                    src = Sg[:, h8 * W: h8 * W + kpad].rearrange(
                        "p (a f) -> p a f", f=4)[:, :, 0:1]
                    return nc.scalar.activation(out=junkb[:, 0:kpad // 4], in_=src,
                                                func=AF.Square,
                                                accum_out=plc("SQS", sl, h8))
                def muf(h, h8=h8, Sg=Sg, W=W, kpad=kpad, sl=sl):
                    src = Sg[:, h8 * W: h8 * W + kpad].rearrange(
                        "p (a f) -> p a f", f=4)[:, :, 0:1]
                    return nc.scalar.activation(out=junkb[:, 0:kpad // 4], in_=src,
                                                func=AF.Copy,
                                                accum_out=plc("MU", sl, h8))
                q.append((sqf, [], "st", h8))
                q.append((muf, [], "st", h8))
            ev[u]["evq"] = q
            ev[u]["stat"] = {}

        def drain_evq(u, n=None):
            if u is None or u >= NT:
                return
            qq = ev[u].get("evq")
            if not qq:
                return
            cnt = len(qq) if n is None else min(n, len(qq))
            for _ in range(cnt):
                fn, w, kind, idx = qq.pop(0)
                if kind == "ev":
                    S.emit("scalar", fn, waits=wb("scalar", w), inc="sEV")
                else:
                    S.emit("scalar", fn, waits=wb("scalar", w), inc="sSC")
                    ev[u]["stat"][idx] = S.after("sSC")

        def emit_postload(u):
            oct_, i, kpad, nkt, sl = tinfo(u)
            Sg = Sg_of(u); W = sgw(u)
            if i == 0:
                for h8 in range(8):
                    def rmf(h, h8=h8, Sg=Sg, W=W, kpad=kpad, sl=sl):
                        return nc.vector.tensor_reduce(
                            plc("RMIN", sl, h8), Sg[:, h8 * W: h8 * W + kpad],
                            axis=AX.X, op=ALU.min)
                    ve(rmf, waits=[ev[u]["stat"][h8], stp_z] if h8 == 0 else [ev[u]["stat"][h8]],
                       inc=("sVE" if h8 == 7 else None))
                ev[u]["rmin"] = S.after("sVE")
            for h8 in range(8):
                w = [ev[u]["stat"][h8]]
                if i == 0 and h8 == 0:
                    w.append(ev[u]["rmin"])
                def dcf(h, h8=h8, Sg=Sg, W=W, kpad=kpad, i=i):
                    return nc.vector.tensor_tensor(
                        out=Sg[:, h8 * W + kpad - 256: h8 * W + kpad],
                        in0=Sg[:, h8 * W + kpad - 256: h8 * W + kpad],
                        in1=maskin_s[:, i * 256:(i + 1) * 256], op=ALU.add)
                S.emit("vector", dcf, waits=w,
                       inc=("sVE" if h8 == 7 else None))
            ev[u]["dca"] = S.after("sVE")
            n4 = float(kpad // 4)
            def init1(h, sl=sl, n4=n4):
                nc.gpsimd.tensor_scalar(pl("MU", sl), pl("MU", sl), 1.0 / n4, None, op0=ALU.mult)
                nc.gpsimd.tensor_scalar(pl("SQS", sl), pl("SQS", sl), 1.0 / n4, None, op0=ALU.mult)
                nc.gpsimd.tensor_tensor(pl("X0", sl), pl("MU", sl), pl("MU", sl), op=ALU.mult)
                nc.gpsimd.tensor_tensor(pl("SQS", sl), pl("SQS", sl), pl("X0", sl), op=ALU.subtract)
                return nc.gpsimd.tensor_scalar(pl("SQS", sl), pl("SQS", sl), 1e-6, None, op0=ALU.max)
            S.emit("gpsimd", init1,
                   waits=[ev[u]["stat"][h] for h in range(8)] + [stp_z], inc="sGP")
            i1d = S.after("sGP")
            def sd1(h, sl=sl):
                nc.scalar.activation(pl("X1", sl), pl("SQS", sl), AF.Ln)
                h.drain()
                return nc.scalar.activation(pl("SD", sl), pl("X1", sl), AF.Exp, scale=0.5)
            S.emit("scalar", sd1, waits=[i1d], inc="sSC")
            sdd = S.after("sSC")
            zqc = zq_s[:, i:i + 1]
            slc = sl0_s[:, i:i + 1]
            selc = sel_s[:, i:i + 1]
            def init2(h, sl=sl, zqc=zqc, slc=slc, selc=selc):
                ts, tt = nc.vector.tensor_scalar, nc.vector.tensor_tensor
                ts(pl("X0", sl), pl("SD", sl), zqc, None, op0=ALU.mult)
                ts(pl("SL", sl), pl("SD", sl), slc, None, op0=ALU.mult)
                tt(pl("T", sl), pl("MU", sl), pl("X0", sl), op=ALU.add)
                nc.vector.memset(pl("TLO", sl), -29000.0)
                nc.vector.memset(pl("THI", sl), 29000.0)
                return ts(pl("NEGT", sl), pl("T", sl), -1.0, None, op0=ALU.mult)
            ve(init2, waits=[sdd], inc="sVE")
            ev[u]["init"] = S.after("sVE")

        def emit_counts(k, r):
            oct_, i, kpad, nkt, sl = tinfo(k)
            Sg = Sg_of(k); W = sgw(k)
            w_first = [ev[k]["init"], ev[k]["dca"]] if r == 1 else [ev[k]["upd"]]
            def cv(h, Sg=Sg, kpad=kpad, sl=sl):
                return nc.vector.tensor_scalar(
                    Bscr[:, 0:kpad], Sg[:, 0:kpad], plc("T", sl, 0), None,
                    op0=ALU.is_gt, op1=ALU.add, accum_out=plc("C", sl, 0))
            ve(cv, waits=list(w_first), inc="sVE")
            ev[k]["cv"] = S.after("sVE")
            for h8 in range(1, 8):
                def cs(h, h8=h8, Sg=Sg, W=W, kpad=kpad, sl=sl):
                    return nc.scalar.activation(
                        out=junkb[:, 0:kpad], in_=Sg[:, h8 * W: h8 * W + kpad],
                        func=AF.Sign, bias=plc("NEGT", sl, h8), scale=1.0,
                        accum_out=plc("C", sl, h8))
                S.emit("scalar", cs, waits=(list(w_first) if h8 == NVH else []),
                       inc=("sSC" if h8 == 7 else None))
            ev[k]["cs"] = S.after("sSC")

        def emit_upd(k, r):
            oct_, i, kpad, nkt, sl = tinfo(k)
            def upd(h, sl=sl, r=r, kpad=kpad):
                P = lambda n: pl(n, sl)
                PI = lambda n: pli(n, sl)
                ts, tt, sel = nc.vector.tensor_scalar, nc.vector.tensor_tensor, nc.vector.select
                def pad():
                    nc.vector.tensor_copy(P("PAD"), P("C"))
                cs_cols = stp[:, (sl * NPL + PL["C"]) * 8 + NVH:(sl * NPL + PL["C"]) * 8 + 8]
                ts(cs_cols, cs_cols, float(kpad), 0.5, op0=ALU.add, op1=ALU.mult)
                tt(P("X2"), P("T"), P("TP"), op=ALU.subtract)
                ts(P("X0"), P("C"), TGT, None, op0=ALU.subtract)
                tt(P("X1"), P("C"), P("CP"), op=ALU.subtract)
                ts(PI("UP"), P("C"), float(NREM), None, op0=ALU.is_ge)
                ts(PI("DN"), P("C"), float(NREM), None, op0=ALU.is_lt)
                ts(P("X3"), P("X0"), W_LO - TGT, W_HI - TGT, op0=ALU.max, op1=ALU.min)
                if r >= 2:
                    nc.vector.reciprocal(P("X4"), P("X1"))
                else:
                    pad()
                tt(P("X0"), P("X0"), P("X3"), op=ALU.subtract)
                if r >= 2:
                    tt(P("X2"), P("X2"), P("X4"), op=ALU.mult)
                else:
                    pad()
                sel(P("TLO"), PI("UP"), P("T"), P("TLO"))
                if r >= 2:
                    tt(P("X1"), P("X1"), P("X1"), op=ALU.mult)
                    ts(P("X4"), P("X2"), -1.0, None, op0=ALU.mult)
                    ts(PI("OK"), P("X1"), 0.25, None, op0=ALU.is_ge)
                    tt(P("X2"), P("X2"), P("X4"), op=ALU.max)
                else:
                    pad(); pad(); pad(); pad()
                sel(P("THI"), PI("DN"), P("T"), P("THI"))
                if r >= 2:
                    ts(P("X2"), P("X2"), 1e-5, 1e3, op0=ALU.max, op1=ALU.min)
                else:
                    pad()
                nc.vector.tensor_copy(P("TP"), P("T"))
                nc.vector.tensor_copy(P("CP"), P("C"))
                if r >= 2:
                    sel(P("SL"), PI("OK"), P("X2"), P("SL"))
                else:
                    pad()
                tt(P("X4"), P("TLO"), P("THI"), op=ALU.add)
                tt(P("X0"), P("X0"), P("SL"), op=ALU.mult)
                ts(P("X4"), P("X4"), 0.5, None, op0=ALU.mult)
                tt(P("X3"), P("T"), P("X0"), op=ALU.add)
                pad()
                tt(P("X1"), P("X3"), P("THI"), op=ALU.is_gt)
                tt(P("X0"), P("X3"), P("TLO"), op=ALU.is_lt)
                pad()
                tt(P("X1"), P("X1"), P("X0"), op=ALU.max)
                pad()
                nc.vector.tensor_copy(PI("O"), P("X1"))
                pad()
                sel(P("T"), PI("O"), P("X4"), P("X3"))
                pad()
                return ts(P("NEGT"), P("T"), -1.0, None, op0=ALU.mult)
            ve(upd, waits=[ev[k]["cs"]], inc="sVE")
            ev[k]["upd"] = S.after("sVE")

        def emit_fin(k):
            oct_, i, kpad, nkt, sl = tinfo(k)
            def fin(h, sl=sl, i=i):
                P = lambda n: pl(n, sl)
                PI = lambda n: pli(n, sl)
                ts, tt, sel = nc.vector.tensor_scalar, nc.vector.tensor_tensor, nc.vector.select
                if i == 0:
                    sel(P("X0"), selb_s[:, i * 8:(i + 1) * 8], P("T"), P("RMIN"))
                    nc.vector.tensor_copy(P("PAD"), P("C"))
                    nc.vector.tensor_copy(P("T"), P("X0"))
                    nc.vector.tensor_copy(P("PAD"), P("C"))
                return ts(P("B0"), P("T"), sinvn_s[:, 0:1], None, op0=ALU.mult)
            ve(fin, waits=[ev[k]["cs"], ev[k]["cv"]], inc="sVE")
            ev[k]["fin"] = S.after("sVE")

        def emit_ext_exc(k, h8):
            oct_, i, kpad, nkt, sl = tinfo(k)
            Sg = Sg_of(k); W = sgw(k)
            def exc(h, h8=h8, Sg=Sg, W=W, kpad=kpad, sl=sl):
                ts, stt = nc.vector.tensor_scalar, nc.vector.scalar_tensor_tensor
                sg = Sg[:, h8 * W: h8 * W + kpad]
                ts(Tbs[:, 0:kpad], sg, plc("T", sl, h8), -2e30, op0=ALU.is_gt, op1=ALU.mult)
                ts(Tas[:, 0:kpad], sg, plc("T", sl, h8), -2e30, op0=ALU.is_le, op1=ALU.mult)
                stt(out=Bscr[:, 0:kpad], in0=Tbs[:, 0:kpad], scalar=1.0,
                    in1=sg, op0=ALU.mult, op1=ALU.add)
                stt(out=Ascr[:, 0:kpad], in0=Tas[:, 0:kpad], scalar=1.0,
                    in1=sg, op0=ALU.mult, op1=ALU.subtract)
                nc.vector.max(out=m8b64[:, h8 * 8:(h8 + 1) * 8], in_=Bscr[:, 0:kpad])
                return nc.vector.max(out=m8a64[:, h8 * 8:(h8 + 1) * 8], in_=Ascr[:, 0:kpad])
            ve(exc, waits=([ev[k]["fin"]] if h8 == 0 else []),
               inc=("sVE" if h8 == 7 else None))

        def emit_ext(k):
            oct_, i, kpad, nkt, sl = tinfo(k)
            Sg = Sg_of(k); W = sgw(k)
            def vmb(h, sl=sl, i=i, kpad=kpad):
                P = lambda n: pl(n, sl)
                PI = lambda n: pli(n, sl)
                ts, tt, sel = nc.vector.tensor_scalar, nc.vector.tensor_tensor, nc.vector.select
                def pad():
                    nc.vector.tensor_copy(P("PAD"), P("X4"))
                cs_cols = stp[:, (sl * NPL + PL["C"]) * 8 + NVH:(sl * NPL + PL["C"]) * 8 + 8]
                ts(cs_cols, cs_cols, float(kpad), 0.5, op0=ALU.add, op1=ALU.mult)
                pad()
                ts(P("X4"), P("C"), -0.25, 8388608.0, op0=ALU.add, op1=ALU.add)
                pad()
                ts(P("C"), P("X4"), -8388608.0, None, op0=ALU.add)
                pad()
                ts(P("X0"), P("C"), TGT, None, op0=ALU.subtract)
                ts(PI("SIDE"), P("C"), float(NREM), None, op0=ALU.is_ge)
                ts(P("X1"), P("X0"), -1.0, None, op0=ALU.mult)
                pad()
                tt(P("X0"), P("X0"), P("X1"), op=ALU.max)
                pad()
                ts(P("X0"), P("X0"), -0.5, None, op0=ALU.add)
                pad()
                ts(P("IDX"), P("X0"), 0.0, 7.0, op0=ALU.max, op1=ALU.min)
                pad()
                for h8 in range(8):
                    ts(W64[:, h8 * 8:(h8 + 1) * 8], iota8_s[:], plc("IDX", sl, h8), None,
                       op0=ALU.is_equal)
                pad()
                tt(Gb64[:], W64[:], m8b64[:], op=ALU.mult)
                tt(Ga64[:], W64[:], m8a64[:], op=ALU.mult)
                nc.vector.tensor_reduce(P("VMB"), Gb64.rearrange("p (hh j) -> p hh j", j=8),
                                        axis=AX.X, op=ALU.add)
                nc.vector.tensor_reduce(P("VMA"), Ga64.rearrange("p (hh j) -> p hh j", j=8),
                                        axis=AX.X, op=ALU.add)
                pad()
                ts(P("X1"), P("VMA"), -1.0, None, op0=ALU.mult)
                pad()
                sel(P("VMS"), PI("SIDE"), P("X1"), P("VMB"))
                if i == 0:
                    pad()
                    sel(P("X2"), selb_s[:, i * 8:(i + 1) * 8], P("VMS"), negc8[:])
                    pad()
                    nc.vector.tensor_copy(P("VMS"), P("X2"))
                return nc.vector.engine_nop()
            ve(vmb, waits=[], inc="sVE")
            ev[k]["vm"] = S.after("sVE")
            if dev_phase == "B" and k == 5:
                S.dma("sync", lambda h, Sg=Sg, W=W: h.dma_start(
                          out=dev["dbg_sg"][:, 0:8 * W].rearrange("p (hh c) -> p hh c", c=W),
                          in_=Sg.rearrange("p (hh c) -> p hh c", c=W)),
                      waits=[ev[k]["vm"]], inc="sST")
            if dev_phase == "B":
                S.dma("sync", lambda h, k=k, sl=sl: h.dma_start(
                          out=dev["dbg_stp"][k],
                          in_=stp[:, sl * NPL * 8:(sl + 1) * NPL * 8]),
                      waits=[ev[k]["vm"]], inc="sST")
                S.dma("sync", lambda h, k=k: h.dma_start(
                          out=dev["dbg_m8"][k][:, 0:64], in_=m8b64[:]), inc="sST")
                S.dma("sync", lambda h, k=k: h.dma_start(
                          out=dev["dbg_m8"][k][:, 64:128], in_=m8a64[:]), inc="sST")
            pf_done = {}
            pz_done = {}
            ptbev = {}
            trp_done = {}
            for h8 in range(8):
                def exf(h, h8=h8, Sg=Sg, W=W, kpad=kpad, sl=sl):
                    return nc.scalar.activation(
                        out=Ppre2[:, (h8 % 2) * 2048:(h8 % 2) * 2048 + kpad],
                        in_=Sg[:, h8 * W: h8 * W + kpad], func=AF.Exp,
                        bias=plc("B0", sl, h8), scale=sinv_s[:, 0:1])
                wx = [ev[k]["fin"]]
                if h8 >= 2:
                    wx.append(pf_done[h8 - 2])
                S.emit("scalar", exf, waits=wx, inc="sSC")
                expd = S.after("sSC")
                def pff(h, h8=h8, Sg=Sg, W=W, kpad=kpad, sl=sl):
                    nc.vector.scalar_tensor_tensor(
                        out=Ppre2[:, (h8 % 2) * 2048:(h8 % 2) * 2048 + kpad],
                        in0=Sg[:, h8 * W: h8 * W + kpad], scalar=plc("VMS", sl, h8),
                        in1=Ppre2[:, (h8 % 2) * 2048:(h8 % 2) * 2048 + kpad],
                        op0=ALU.is_ge, op1=ALU.mult, accum_out=plc("Z", sl, h8))
                    nc.vector.tensor_copy(pl("PAD", sl), pl("C", sl))
                    return nc.vector.reciprocal(plc("ZI", sl, h8), plc("Z", sl, h8))
                ve(pff, waits=[expd, ev[k]["vm"]], inc="sVE")
                pf_done[h8] = S.after("sVE")
                def pzf(h, h8=h8, kpad=kpad, sl=sl):
                    return nc.scalar.activation(
                        out=Pfin2[:, (h8 % 2) * 2048:(h8 % 2) * 2048 + kpad],
                        in_=Ppre2[:, (h8 % 2) * 2048:(h8 % 2) * 2048 + kpad],
                        func=AF.Identity, scale=plc("ZI", sl, h8))
                wz = [pf_done[h8]]
                if h8 >= 2:
                    wz.append(ev[k]["trp%d" % (h8 - 2)])
                S.emit("scalar", pzf, waits=wz, inc="sSC")
                pz_done[h8] = S.after("sSC")
                x = h8 % 4
                nbat = (nkt + 7) // 8
                ptbev[h8] = []
                for b in range(nbat):
                    lo = b * 8
                    hi = min(nkt, lo + 8)
                    pb = (h8 * nbat + b) % 2
                    def trf(h, h8=h8, lo=lo, hi=hi, pb=pb, kpad=kpad):
                        last = None
                        for kt in range(lo, hi):
                            last = nc.tensor.transpose(
                                ps_pt[pb][:, (kt - lo) * 128:(kt - lo + 1) * 128],
                                Pfin2[:, (h8 % 2) * 2048 + kt * 128:
                                      (h8 % 2) * 2048 + (kt + 1) * 128],
                                ident_s[:])
                        return last
                    wt = [pz_done[h8]]
                    if len(ptb_hist) >= 2:
                        wt.append(ptb_hist[-2])
                    S.emit("tensor", trf, waits=wt, inc="sPE")
                    trd = S.after("sPE")
                    def tev(h, x=x, lo=lo, hi=hi, pb=pb):
                        return nc.vector.tensor_copy(
                            out=PTb.rearrange("p (kt xx c) -> p kt xx c", xx=4, c=128)[:, lo:hi, x],
                            in_=ps_pt[pb][:, 0:(hi - lo) * 128]
                                .rearrange("p (kt c) -> p kt c", c=128))
                    ve(tev, waits=[trd], inc="sVE")
                    ptb_hist.append(S.after("sVE"))
                    ptbev[h8].append(S.after("sVE"))
                ev[k]["trp%d" % h8] = trd
                if h8 % 4 == 3:
                    g = h8 // 4
                    wav = [e for hh in range(4 * g, 4 * g + 4) for e in ptbev[hh]]
                    if k >= 1 and ("avev%d" % g) in ev[k - 1]:
                        wav.append(ev[k - 1]["avev%d" % g])
                    def avf(h, g=g, nkt=nkt, sl=sl):
                        last = None
                        for kt in range(nkt):
                            last = nc.tensor.matmul(
                                ps_av[g][:, 0:512],
                                v2[:, sl * 4096 + g * 2048 + kt * 128:
                                   sl * 4096 + g * 2048 + (kt + 1) * 128],
                                PTb[:, kt * 512:(kt + 1) * 512],
                                start=(kt == 0), stop=(kt == nkt - 1))
                        return last
                    S.emit("tensor", avf, waits=wav, inc="sPE")
                    avd = S.after("sPE")
                    def aev(h, g=g):
                        return nc.scalar.copy(out=ot_stage[:, g * 512:(g + 1) * 512],
                                              in_=ps_av[g][:, 0:512])
                    S.emit("scalar", aev, waits=[avd], inc="sSC")
                    ev[k]["avev%d" % g] = S.after("sSC")
                    oct2, i2 = tiles[k]
                    hh0 = oct2 * 8 + g * 4
                    S.dma("sync", lambda h, hh0=hh0, i2=i2, g=g: h.dma_start(
                              out=ot_d[hh0:hh0 + 4, i2].rearrange("hh p q -> p hh q"),
                              in_=ot_stage[:, g * 512:(g + 1) * 512]
                                  .rearrange("p (hh q) -> p hh q", q=128)),
                          waits=[ev[k]["avev%d" % g]], inc="sST")
                    if g == 1:
                        S.emit("vector", lambda h: nc.vector.engine_nop(),
                               waits=[avd], inc="sAV")
            ev[k]["sgfree"] = pf_done[7]
            S.emit("vector", lambda h: nc.vector.engine_nop(),
                   waits=[pf_done[7]], inc="sSG")

        ptb_hist = []

        emit_load(0)
        emit_smm(0)
        build_evq(0)
        drain_evq(0, None)
        emit_postload(0)
        if not PIPE_DEEP:
            for k in range(NT):
                u = k + 1 if k + 1 < NT else None
                if u is not None:
                    emit_load(u)
                    emit_smm(u)
                    build_evq(u)
                for r in range(1, N_CNT + 1):
                    emit_counts(k, r)
                    if r < N_CNT:
                        emit_upd(k, r)
                    drain_evq(u, 4)
                drain_evq(u, None)
                emit_fin(k)
                if u is not None:
                    emit_postload(u)
                for hh8 in range(8):
                    emit_ext_exc(k, hh8)
                emit_ext(k)
        else:
            for k in range(NT):
                u = k + 1 if k + 1 < NT else None
                if u is not None:
                    emit_load(u)
                for r in range(1, N_CNT + 1):
                    emit_counts(k, r)
                    if k >= 1 and INTERLEAVE_EXC:
                        emit_ext_exc(k - 1, r - 1)
                    if r < N_CNT:
                        emit_upd(k, r)
                emit_fin(k)
                if k >= 1:
                    if not INTERLEAVE_EXC:
                        for hh8 in range(8):
                            emit_ext_exc(k - 1, hh8)
                    emit_ext(k - 1)
                if u is not None:
                    emit_smm(u)
                    build_evq(u)
                    drain_evq(u, None)
                    emit_postload(u)
            for h8 in range(8):
                emit_ext_exc(NT - 1, h8)
            emit_ext(NT - 1)
        phaseB_st = S.after("sST")
        if dev_phase == "B":
            S.dma("sync", lambda h: h.dma_start(out=dev["dbg_ot"], in_=ot_d),
                  waits=[phaseB_st], inc="sST")
            S.emit("vector", lambda h: nc.vector.memset(SgB[:, 0:2048], 0.0), inc="sVE")
            zr2 = S.after("sVE")
            ochunks = out_t.rearrange("(c p) d -> c p d", p=128)
            for c in range(8):
                for hf in range(2):
                    S.dma("sync", lambda h, c=c, hf=hf: h.dma_start(
                              out=ochunks[c][:, hf * 2048:(hf + 1) * 2048], in_=SgB[:, 0:2048]),
                          waits=[zr2], inc="sST")
            S.emit("sync", lambda h: h.nop(), waits=[S.after("sST")])

    if dev_phase == "full":
        ctxB.close()
        first_c = {e: True for e in ENGS}
        def wc(eng, waits):
            w = list(waits)
            if first_c[eng]:
                w.append(phaseB_st)
                first_c[eng] = False
            return w
        ctxC1 = ExitStack()
        wopan = ctxC1.enter_context(nc.sbuf_tensor("wopan", [128, NDT * 1024], BF16))
        otw = ctxC1.enter_context(nc.sbuf_tensor("otw", [128, 32 * 128], BF16))
        x1st = ctxC1.enter_context(nc.sbuf_tensor("x1st", [128, 1024], F32))
        ps_wo = [ctxC1.enter_context(nc.psum_tensor("ps_wo0", [128, 1024], F32)),
                 ctxC1.enter_context(nc.psum_tensor("ps_wo1", [128, 1024], F32))]
        x1_d = DS("x1_d", [8, 128, D], F32)

        otw_guard = (None, 0)
        x1_parks = (None, 0)
        for oq in range(4):
            for dt2 in range(NDT):
                S.dma("gpsimd", lambda h, oq=oq, dt2=dt2: h.dma_start(
                          out=wopan[:, dt2 * 1024:(dt2 + 1) * 1024],
                          in_=WoT_i[dt2 * 128:(dt2 + 1) * 128, oq * 1024:(oq + 1) * 1024]),
                      waits=wc("gpsimd", [S.after("sPE")] if dt2 == 0 and oq > 0 else []),
                      inc="sCA")
            pan_rdy = S.after("sCA")
            for i in range(8):
                S.dma("sync", lambda h, i=i: h.dma_start(
                          out=otw[:, :].rearrange("p (hh c) -> p hh c", c=128),
                          in_=ot_d[:, i].rearrange("hh p c -> p hh c")),
                      waits=wc("sync", [otw_guard]), inc="sLD")
                otl = S.after("sLD")
                buf = i % 2
                def mwo(h, buf=buf, i=i):
                    last = None
                    for dt2 in range(NDT):
                        for c5 in range(2):
                            last = nc.tensor.matmul(
                                ps_wo[buf][:, c5 * 512:(c5 + 1) * 512],
                                otw[:, dt2 * 128:(dt2 + 1) * 128],
                                wopan[:, dt2 * 1024 + c5 * 512: dt2 * 1024 + (c5 + 1) * 512],
                                start=(dt2 == 0), stop=(dt2 == NDT - 1))
                    return last
                S.emit("tensor", mwo, waits=wc("tensor", [pan_rdy, otl, x1_parks]), inc="sPE")
                mwd = S.after("sPE")
                S.emit("scalar", lambda h, buf=buf: nc.scalar.copy(out=x1st[:], in_=ps_wo[buf][:]),
                       waits=wc("scalar", [mwd]), inc="sSC")
                evd = S.after("sSC")
                S.dma("sync", lambda h, i=i, oq=oq: h.dma_start(
                          out=x1_d[i][:, oq * 1024:(oq + 1) * 1024], in_=x1st[:]),
                      waits=[evd], inc="sST")
                x1_parks = S.after("sST")
                otw_guard = mwd
        c1_done = S.after("sST")
        ctxC1.close()

        ctxC2 = ExitStack()
        def SC(name, shape, dt):
            return ctxC2.enter_context(nc.sbuf_tensor(name, list(shape), dt))
        def PC(name, shape, dt):
            return ctxC2.enter_context(nc.psum_tensor(name, list(shape), dt))
        x1l = SC("x1l", [128, D], F32)
        hsr = SC("hsr", [128, D], F32)
        x2f = SC("x2f", [128, D], F32)
        x2b = SC("x2b", [128, D], BF16)
        x2T = SC("x2T", [128, NDT * 128], BF16)
        x3f = SC("x3f", [128, D], F32)
        outf = SC("outf", [128, D], F32)
        routb = SC("routb", [128, NDT * 8], BF16)
        acatb = SC("acatb", [128, NDT * 128], BF16)
        bwcb = SC("bwcb", [128, D], BF16)
        e8f = SC("e8f", [8, 128], F32)
        gatep = SC("gatep", [128, 128], F32)
        gT = SC("gT", [8, 128], F32)
        gevac = SC("gevac", [128, 128], F32)
        htil = SC("htil", [128, 128], BF16)
        lnp = SC("lnp", [128, 16], F32)
        ps_t2 = PC("ps_t2", [128, 1024], BF16)
        ps_rt = PC("ps_rt", [128, 512], F32)
        ps_h = PC("ps_h", [128, 512], F32)
        ps_ge = PC("ps_ge", [128, 512], F32)
        ps_gt = PC("ps_gt", [128, 512], F32)
        ps_ff = PC("ps_ff", [128, 1024], F32)
        def lp(j):
            return lnp[:, j:j + 1]

        S.dma("gpsimd", lambda h: h.dma_start(
                  out=routb[:, :].rearrange("p (dt e) -> p dt e", e=8),
                  in_=routeT_i.rearrange("(dt p) e -> p dt e", p=128)),
              waits=[c1_done], inc="sCA")
        S.dma("gpsimd", lambda h: h.dma_start(
                  out=acatb[:, :].rearrange("p (dt c) -> p dt c", c=128),
                  in_=AcatT_i.rearrange("(dt p) c -> p dt c", p=128)), inc="sCA")
        S.dma("gpsimd", lambda h: h.dma_start(out=bwcb[:], in_=BwcatT_i), inc="sCA")
        S.dma("sync", lambda h: h.dma_start(out=e8f[:], in_=E8_i), waits=[c1_done], inc="sLD")
        cld = S.after("sCA")
        cld2 = S.after("sLD")
        ve(lambda h: nc.vector.memset(gatep[:, :], 0.0), waits=[c1_done], inc="sVE")
        gz = S.after("sVE")

        prev_i_done = (None, 0)
        for i in range(8):
            S.dma("sync", lambda h, i=i: h.dma_start(out=x1l[:], in_=x1_d[i]),
                  waits=[prev_i_done], inc="sLD")
            S.dma("sync", lambda h, i=i: h.dma_start(out=hsr[:], in_=hs_own[i * 128:(i + 1) * 128, :]),
                  inc="sLD")
            lds = S.after("sLD")
            ve(lambda h: nc.vector.tensor_add(x1l[:], x1l[:], hsr[:]),
               waits=[lds, gz], inc="sVE")
            r1d = S.after("sVE")
            def st1(h):
                h.drain()
                nc.scalar.activation(out=x2f[:], in_=x1l[:], func=AF.Copy, accum_out=lp(0))
                h.drain()
                return nc.scalar.activation(out=x2f[:], in_=x1l[:], func=AF.Square, accum_out=lp(1))
            S.emit("scalar", st1, waits=wc("scalar", [r1d]) if first_c["scalar"] else [r1d], inc="sSC")
            st1d = S.after("sSC")
            def ln1t(h):
                nc.vector.tensor_scalar(lp(2), lp(0), 1.0 / D, None, op0=ALU.mult)
                h.drain()
                nc.vector.tensor_scalar(lp(3), lp(1), 1.0 / D, None, op0=ALU.mult)
                h.drain()
                nc.vector.tensor_mul(lp(4), lp(2), lp(2))
                h.drain()
                nc.vector.tensor_sub(lp(3), lp(3), lp(4))
                h.drain()
                return nc.vector.tensor_scalar(lp(3), lp(3), 1e-5, None, op0=ALU.add)
            ve(ln1t, waits=[st1d], inc="sVE")
            ln1td = S.after("sVE")
            def rst1(h):
                h.drain()
                nc.scalar.activation(lp(5), lp(3), AF.Ln)
                h.drain()
                return nc.scalar.activation(lp(6), lp(5), AF.Exp, scale=-0.5)
            S.emit("scalar", rst1, waits=[ln1td], inc="sSC")
            rst1d = S.after("sSC")
            def b2v(h):
                nc.vector.tensor_mul(lp(7), lp(2), lp(6))
                h.drain()
                return nc.vector.tensor_scalar(lp(7), lp(7), -1.0, None, op0=ALU.mult)
            ve(b2v, waits=[rst1d], inc="sVE")
            b2d = S.after("sVE")
            S.emit("scalar", lambda h: nc.scalar.activation(out=x2f[:], in_=x1l[:], func=AF.Identity,
                                                            bias=lp(7), scale=lp(6)),
                   waits=[b2d], inc="sSC")
            x2d = S.after("sSC")
            ve(lambda h: nc.vector.tensor_copy(out=x2b[:], in_=x2f[:]), waits=[x2d], inc="sVE")
            x2bd = S.after("sVE")
            for tg in range(4):
                def trx(h, tg=tg):
                    last = None
                    for j2 in range(8):
                        dt2 = tg * 8 + j2
                        last = nc.tensor.transpose(ps_t2[:, j2 * 128:(j2 + 1) * 128],
                                                   x2b[:, dt2 * 128:(dt2 + 1) * 128], ident_s[:])
                    return last
                S.emit("tensor", trx, waits=wc("tensor", [x2bd, S.after("sSC")]), inc="sPE")
                trxd = S.after("sPE")
                S.emit("scalar", lambda h, tg=tg: nc.scalar.copy(
                           out=x2T[:, tg * 1024:(tg + 1) * 1024], in_=ps_t2[:]),
                       waits=[trxd], inc="sSC")
            x2td = S.after("sSC")
            def mrt(h):
                last = None
                for dt2 in range(NDT):
                    last = nc.tensor.matmul(ps_rt[:, 0:8], x2T[:, dt2 * 128:(dt2 + 1) * 128],
                                            routb[:, dt2 * 8:(dt2 + 1) * 8],
                                            start=(dt2 == 0), stop=(dt2 == NDT - 1))
                return last
            S.emit("tensor", mrt, waits=[x2td, cld, S.after("sVE")], inc="sPE")
            mrtd = S.after("sPE")
            def top1(h):
                nc.vector.tensor_copy(gatep[:, 0:8], ps_rt[:, 0:8])
                h.drain()
                nc.vector.tensor_reduce(lp(8), gatep[:, 0:8], axis=AX.X, op=ALU.max)
                h.drain()
                nc.vector.tensor_scalar(gatep[:, 8:16], gatep[:, 0:8], lp(8), None, op0=ALU.is_ge)
                h.drain()
                nc.vector.memset(gatep[:, 16:24], 0.0)
                h.drain()
                nc.vector.tensor_copy(gatep[:, 17:24], gatep[:, 8:15])
                h.drain()
                nc.vector.tensor_copy(gatep[:, 24:32], gatep[:, 16:24])
                h.drain()
                nc.vector.tensor_add(gatep[:, 26:32], gatep[:, 26:32], gatep[:, 16:22])
                h.drain()
                nc.vector.tensor_copy(gatep[:, 32:40], gatep[:, 24:32])
                h.drain()
                nc.vector.tensor_add(gatep[:, 36:40], gatep[:, 36:40], gatep[:, 24:28])
                h.drain()
                nc.vector.tensor_scalar(gatep[:, 40:48], gatep[:, 32:40], 0.5, None, op0=ALU.is_lt)
                h.drain()
                nc.vector.tensor_mul(gatep[:, 48:56], gatep[:, 8:16], gatep[:, 40:48])
                h.drain()
                nc.vector.tensor_scalar(gatep[:, 56:64], gatep[:, 48:56], lp(8), None, op0=ALU.mult)
                h.drain()
                return nc.vector.tensor_copy(gatep[:, 0:8], gatep[:, 56:64])
            ve(top1, waits=[mrtd], inc="sVE")
            t1d = S.after("sVE")
            def mh(h):
                last = None
                for dt2 in range(NDT):
                    last = nc.tensor.matmul(ps_h[:, 0:128], acatb[:, dt2 * 128:(dt2 + 1) * 128],
                                            x2T[:, dt2 * 128:(dt2 + 1) * 128],
                                            start=(dt2 == 0), stop=(dt2 == NDT - 1))
                return last
            S.emit("tensor", mh, waits=[x2td], inc="sPE")
            mhd = S.after("sPE")
            def mgt(h):
                return nc.tensor.transpose(ps_gt[:, 0:128], gatep[:, 0:128], identf_s[:])
            S.emit("tensor", mgt, waits=[t1d], inc="sPE")
            mgtd = S.after("sPE")
            S.emit("scalar", lambda h: nc.scalar.copy(out=gT[:], in_=ps_gt[0:8, 0:128]),
                   waits=[mgtd], inc="sSC")
            gtd = S.after("sSC")
            def mge(h):
                return nc.tensor.matmul(ps_ge[:, 0:128], e8f[:], gT[:], start=True, stop=True)
            S.emit("tensor", mge, waits=[gtd, cld2], inc="sPE")
            mged = S.after("sPE")
            S.emit("scalar", lambda h: nc.scalar.copy(out=gevac[:], in_=ps_ge[:, 0:128]),
                   waits=[mged], inc="sSC")
            gevd = S.after("sSC")
            ve(lambda h: nc.vector.tensor_mul(htil[:], ps_h[:, 0:128], gevac[:]),
               waits=[mhd, gevd], inc="sVE")
            htd = S.after("sVE")
            for oq2 in range(4):
                def mff(h, oq2=oq2):
                    nc.tensor.matmul(ps_ff[:, 0:512], htil[:],
                                     bwcb[:, oq2 * 1024: oq2 * 1024 + 512], start=True, stop=True)
                    return nc.tensor.matmul(ps_ff[:, 512:1024], htil[:],
                                            bwcb[:, oq2 * 1024 + 512: (oq2 + 1) * 1024],
                                            start=True, stop=True)
                S.emit("tensor", mff, waits=[htd] + ([S.after("sVE")] if oq2 else []), inc="sPE")
                mffd = S.after("sPE")
                def rx3(h, oq2=oq2):
                    return nc.vector.tensor_add(x3f[:, oq2 * 1024:(oq2 + 1) * 1024],
                                                ps_ff[:, 0:1024], x2f[:, oq2 * 1024:(oq2 + 1) * 1024])
                ve(rx3, waits=[mffd], inc="sVE")
            x3d = S.after("sVE")
            def st2(h):
                h.drain()
                nc.scalar.activation(out=outf[:], in_=x3f[:], func=AF.Copy, accum_out=lp(0))
                h.drain()
                return nc.scalar.activation(out=outf[:], in_=x3f[:], func=AF.Square, accum_out=lp(1))
            S.emit("scalar", st2, waits=[x3d], inc="sSC")
            st2d = S.after("sSC")
            ve(ln1t, waits=[st2d], inc="sVE")
            ln2td = S.after("sVE")
            S.emit("scalar", rst1, waits=[ln2td], inc="sSC")
            rst2d = S.after("sSC")
            ve(b2v, waits=[rst2d], inc="sVE")
            b2d2 = S.after("sVE")
            S.emit("scalar", lambda h: nc.scalar.activation(out=outf[:], in_=x3f[:], func=AF.Identity,
                                                            bias=lp(7), scale=lp(6)),
                   waits=[b2d2], inc="sSC")
            od = S.after("sSC")
            S.dma("sync", lambda h, i=i: h.dma_start(out=out_t[i * 128:(i + 1) * 128, :], in_=outf[:]),
                  waits=[od], inc="sST")
            prev_i_done = S.after("sST")
        S.emit("sync", lambda h: h.nop(), waits=[S.after("sST")])
        ctxC2.close()

    if dev_phase in ("A", "Araw"):
        S.dma("sync", lambda h: h.dma_start(out=dev["dbg_qrt"], in_=qrt_d), waits=[phaseA_st], inc="sST")
        if dev_phase == "A":
            S.dma("sync", lambda h: h.dma_start(out=dev["dbg_krt"], in_=krt_d), inc="sST")
            S.dma("sync", lambda h: h.dma_start(out=dev["dbg_v"], in_=v_d), inc="sST")
        S.emit("vector", lambda h: nc.vector.memset(qk_tmp[:, :], 0.0), inc="sVE")
        zr = S.after("sVE")
        ochunks = out_t.rearrange("(c p) d -> c p d", p=128)
        for c in range(8):
            for hf in range(2):
                S.dma("sync", lambda h, c=c, hf=hf: h.dma_start(
                          out=ochunks[c][:, hf * 2048:(hf + 1) * 2048], in_=qk_tmp[:, :]),
                      waits=[zr], inc="sST")
        S.emit("sync", lambda h: h.nop(), waits=[S.after("sST")])

    if dev_phase == "B":
        ctxB.close()
    if dev_phase in ("A", "Araw"):
        ctxA.close()

    with nc.Block() as blk:
        @blk.sync
        def _(h):
            for op in S.ops["sync"]:
                op(h)
        @blk.gpsimd
        def _(h):
            for op in S.ops["gpsimd"]:
                op(h)
        @blk.vector
        def _(h):
            for op in S.ops["vector"]:
                op(h)
        @blk.scalar
        def _(h):
            for op in S.ops["scalar"]:
                op(h)
        @blk.tensor
        def _(h):
            for op in S.ops["tensor"]:
                op(h)

    ctx.close()
    return nc


_PROG_CACHE = {}


def kernel(**inputs):
    wts = host_weights(inputs)
    hs_all = np.asarray(inputs["L0_attn_hidden_states"], dtype=np.float32)
    if "prog" not in _PROG_CACHE:
        _PROG_CACHE["prog"] = build_program(dev_phase="full")
        _PROG_CACHE["consts"] = {p: host_constants(p) for p in (0, 1)}
    nc = _PROG_CACHE["prog"]
    in_maps = []
    for c in range(8):
        b, p = c // 2, c % 2
        owncols = np.concatenate([np.arange(qb * 128, (qb + 1) * 128) for qb in OWN[p]])
        hsb = np.ascontiguousarray(hs_all[b])
        in_maps.append(dict(hs=hsb, hs_own=np.ascontiguousarray(hsb[owncols]),
                            **wts, **_PROG_CACHE["consts"][p]))
    res = run_bass_kernel_spmd(nc, in_maps, core_ids=list(range(8)))
    out = np.empty((B, SQ, D), np.float32)
    for c in range(8):
        b, p = c // 2, c % 2
        r = np.asarray(res.results[c]["out"])
        for i, qb in enumerate(OWN[p]):
            out[b, qb * 128:(qb + 1) * 128, :] = r[i * 128:(i + 1) * 128, :]
    return out


# version 04
# speedup vs baseline: 1.0727x; 1.0355x over previous
import math
import numpy as np
import ml_dtypes
import concourse.bass as bass
import concourse.mybir as mybir
from concourse.bass_utils import run_bass_kernel_spmd
from contextlib import ExitStack

F32 = mybir.dt.float32
BF16 = mybir.dt.bfloat16
AF = mybir.ActivationFunctionType
ALU = mybir.AluOpType
AX = mybir.AxisListType
BF = ml_dtypes.bfloat16

B, SQ, D, H, HKV, HD = 4, 2048, 4096, 32, 8, 128
NQB, NDT = SQ // 128, D // 128
NREM = 205
NEG30 = -1.0e30
SQRT_HD = float(np.float32(np.sqrt(np.float32(HD))))
OWN = {0: [0, 3, 4, 7, 8, 11, 12, 15], 1: [1, 2, 5, 6, 9, 10, 13, 14]}
PAIR_OFF = {0: (0, 384), 1: (128, 256)}
N_IT = 8
TARGET = 200.5

ENGS = ("sync", "gpsimd", "vector", "scalar", "tensor")


class Sched:
    def __init__(self, nc, ctx):
        self.nc = nc
        self.ops = {e: [] for e in ENGS}
        self.ctr = {}
        self.ctx = ctx

    def sem(self, name):
        s = self.ctx.enter_context(self.nc.semaphore(name))
        self.ctr[name] = [s, 0]

    def val(self, name):
        return self.ctr[name][1]

    def after(self, name):
        return (name, self.ctr[name][1])

    def emit(self, eng, fn, waits=(), inc=None, n=1):
        if not hasattr(self, "log"):
            self.log = []
        self.log.append((eng, len(self.ops[eng]), [w for w in waits if w is not None],
                         inc, n, getattr(fn, "__qualname__", str(fn))))
        if inc is not None:
            self.ctr[inc][1] += n
        sem = self.ctr[inc][0] if inc is not None else None
        wl = [(self.ctr[w][0], v) for (w, v) in waits if v > 0]
        def op(h, fn=fn, wl=wl, sem=sem, n=n):
            for (s, v) in wl:
                h.wait_ge(s, v)
            ins = fn(h)
            if sem is not None:
                ins.then_inc(sem, n)
        self.ops[eng].append(op)

    def dma(self, eng, fn, waits=(), inc=None):
        self.emit(eng, fn, waits, inc, n=16)


def host_constants(parity):
    inv_freq = (1.0 / (np.float32(10000.0) ** (np.arange(0, HD, 2, dtype=np.float32) / np.float32(HD)))).astype(np.float32)
    t = np.arange(SQ, dtype=np.float32)
    freqs = np.outer(t, inv_freq).astype(np.float32)
    emb = np.concatenate([freqs, freqs], axis=-1)
    cos = np.cos(emb).astype(np.float32)
    sin = np.sin(emb).astype(np.float32)
    cosT = np.ascontiguousarray(cos.T)
    sswapT = np.ascontiguousarray(sin.T).copy()
    sswapT[:64, :] = -sswapT[:64, :]
    own = OWN[parity]
    owncols = np.concatenate([np.arange(qb * 128, (qb + 1) * 128) for qb in own])
    cosT_own = np.ascontiguousarray(cosT[:, owncols])
    sswapT_own = np.ascontiguousarray(sswapT[:, owncols])
    ident = np.eye(128, dtype=np.float32)
    causal_add = np.where(np.arange(128)[:, None] >= np.arange(128)[None, :], 0.0,
                          np.float32(NEG30)).astype(np.float32)
    valid = np.stack([qb * 128 + np.arange(128) + 1 for qb in own], axis=1).astype(np.float32)
    sel = (valid > NREM).astype(np.float32)
    from statistics import NormalDist
    nd = NormalDist()
    frac = np.clip(1.0 - NREM / np.maximum(valid, NREM + 1.0), 1e-6, 1 - 1e-6)
    zq = np.vectorize(nd.inv_cdf)(frac).astype(np.float32)
    phi = (np.exp(-0.5 * zq * zq) / np.sqrt(2 * np.pi)).astype(np.float32)
    sl0 = (1.0 / np.maximum(valid * phi, 1.0)).astype(np.float32)
    iota8 = np.tile(np.arange(8, dtype=np.float32)[None, :], (128, 1)).astype(np.float32)
    maskin = np.zeros((8, 128, 256), np.float32)
    for i, qb in enumerate(own):
        kpad = 256 * (i + 1)
        kcols = np.arange(kpad - 256, kpad)
        qpos = qb * 128 + np.arange(128)
        maskin[i] = np.where(kcols[None, :] <= qpos[:, None], 0.0, np.float32(NEG30))
    selb = np.repeat(sel.astype(np.int32)[:, :, None], 8, axis=2).reshape(128, 64)
    return dict(cosT=cosT, sswapT=sswapT, cosT_own=cosT_own, sswapT_own=sswapT_own,
                identc=ident, validc=valid, selc=sel, selbc=selb, zqc=zq, sl0c=sl0,
                iota8c=iota8, onesc=np.ones((1, 128), np.float32),
                sqhdc=np.full((128, 1), SQRT_HD, np.float32),
                maskin=maskin.reshape(8, 128 * 256))


def host_weights(inputs):
    WqT = np.ascontiguousarray(np.asarray(inputs["L0_attn_Wq"]).T)
    WkT = np.ascontiguousarray(np.asarray(inputs["L0_attn_Wk"]).T)
    WvT = np.ascontiguousarray(np.asarray(inputs["L0_attn_Wv"]).T)
    WoT = np.ascontiguousarray(np.asarray(inputs["L0_attn_Wo"]).T)
    routeT = np.ascontiguousarray(np.asarray(inputs["L0_ffn_route_w"]).T)
    A = np.asarray(inputs["L0_ffn_A"])
    AcatT = np.ascontiguousarray(A.reshape(128, D).T)
    Bw = np.asarray(inputs["L0_ffn_Bw"])
    BwcatT = np.ascontiguousarray(Bw.transpose(0, 2, 1).reshape(128, D))
    E8 = np.zeros((8, 128), dtype=np.float32)
    for e in range(8):
        E8[e, e * 16:(e + 1) * 16] = 1.0
    return dict(WqT=WqT, WkT=WkT, WvT=WvT, WoT=WoT, routeT=routeT,
                AcatT=AcatT, BwcatT=BwcatT, E8=E8)


def build_program(dev_phase="full"):
    own = OWN[0]
    nc = bass.Bass("TRN2", target_bir_lowering=False)

    def DI(name, shape, dt=F32):
        return nc.dram_tensor(name, list(shape), dt, kind="ExternalInput").ap()
    def DS(name, shape, dt):
        return nc.dram_tensor(name, list(shape), dt, kind="Internal").ap()

    hs = DI("hs", [SQ, D])
    hs_own = DI("hs_own", [1024, D])
    WqT = DI("WqT", [D, D]); WkT_i = DI("WkT", [D, HKV * HD]); WvT_i = DI("WvT", [D, HKV * HD])
    WoT_i = DI("WoT", [D, D])
    routeT_i = DI("routeT", [D, 8]); AcatT_i = DI("AcatT", [D, 128]); BwcatT_i = DI("BwcatT", [128, D])
    E8_i = DI("E8", [8, 128])
    cosT_i = DI("cosT", [HD, SQ]); sswapT_i = DI("sswapT", [HD, SQ])
    cosTo_i = DI("cosT_own", [HD, 1024]); sswapTo_i = DI("sswapT_own", [HD, 1024])
    ident_i = DI("identc", [128, 128])
    maskin_i = DI("maskin", [8, 128 * 256])
    valid_i = DI("validc", [128, 8]); sel_i = DI("selc", [128, 8])
    selb_i = nc.dram_tensor("selbc", [128, 64], mybir.dt.int32, kind="ExternalInput").ap()
    zq_i = DI("zqc", [128, 8]); sl0_i = DI("sl0c", [128, 8])
    ones_i = DI("onesc", [1, 128]); iota8_i = DI("iota8c", [128, 8])
    sqhd_i = DI("sqhdc", [128, 1])

    out_t = nc.dram_tensor("out", [1024, D], F32, kind="ExternalOutput").ap()

    qrt_d = DS("qrt_d", [H, HD, 1024], BF16)
    krt_d = DS("krt_d", [HKV, HD, SQ], BF16)
    v_d = DS("v_d", [HKV, NQB, 128, HD], BF16)
    ot_d = DS("ot_d", [H, 8, HD, 128], BF16)

    dev = {}
    if dev_phase == "B":
        dev["dbg_ot"] = nc.dram_tensor("dbg_ot", [H, 8, HD, 128], BF16, kind="ExternalOutput").ap()
        dev["dbg_stp"] = nc.dram_tensor("dbg_stp", [32, 128, 28 * 8], F32, kind="ExternalOutput").ap()
        dev["dbg_m8"] = nc.dram_tensor("dbg_m8", [32, 128, 128], F32, kind="ExternalOutput").ap()
        dev["dbg_sg"] = nc.dram_tensor("dbg_sg", [128, 8 * 2048], F32, kind="ExternalOutput").ap()
    if dev_phase in ("A", "Araw"):
        dev["dbg_qrt"] = nc.dram_tensor("dbg_qrt", [H, HD, 1024], BF16, kind="ExternalOutput").ap()
        if dev_phase == "A":
            dev["dbg_krt"] = nc.dram_tensor("dbg_krt", [HKV, HD, SQ], BF16, kind="ExternalOutput").ap()
            dev["dbg_v"] = nc.dram_tensor("dbg_v", [HKV, NQB, 128, HD], BF16, kind="ExternalOutput").ap()

    ctx = ExitStack()
    def SB(name, shape, dt):
        return ctx.enter_context(nc.sbuf_tensor(name, list(shape), dt))
    def PS(name, shape, dt):
        return ctx.enter_context(nc.psum_tensor(name, list(shape), dt))

    S = Sched(nc, ctx)
    for s in ("sLD", "sCA", "sW", "sPE", "sVE", "sSC", "sST", "sGP", "sEV", "sSG", "sAV"):
        S.sem(s)

    ident_s = SB("ident_s", [128, 128], BF16)
    maskin_s = SB("maskin_s", [128, 8 * 256], F32)
    valid_s = SB("valid_s", [128, 8], F32)
    sel_s = SB("sel_s", [128, 8], F32)
    zq_s = SB("zq_s", [128, 8], F32)
    sl0_s = SB("sl0_s", [128, 8], F32)
    ones_s = SB("ones_s", [1, 128], F32)
    iota8_s = SB("iota8_s", [128, 8], F32)
    selb_s = SB("selb_s", [128, 64], mybir.dt.int32)
    sqhd_s = SB("sqhd_s", [128, 1], F32)
    sinv_s = SB("sinv_s", [128, 1], F32)
    identf_s = SB("identf_s", [128, 128], F32)

    ctxA = ExitStack()
    def SA(name, shape, dt):
        return ctxA.enter_context(nc.sbuf_tensor(name, list(shape), dt))
    def PA(name, shape, dt):
        return ctxA.enter_context(nc.psum_tensor(name, list(shape), dt))
    wpan = SA("wpan", [128, 2 * NDT * 128], BF16)
    hs_bf = SA("hs_bf", [128, 2 * D], BF16)
    qk_tmp = SA("qk_tmp", [128, 2048], F32)
    qk_tmp2 = SA("qk_tmp2", [128, 2048], F32)
    qk_stage = SA("qk_stage", [128, 2048], BF16)
    v_stage = qk_stage
    vn_stage = SA("vn_stage", [128, 2048], BF16)

    ps_pj2 = [PA("ps_pj0", [128, 1024], F32), PA("ps_pj1", [128, 1024], F32)]
    ps_tr = PA("ps_tr", [128, 512], BF16)
    cosT_s = SA("cosT_s", [HD, SQ], F32)
    sswapT_s = SA("sswapT_s", [HD, SQ], F32)
    cosTo_s = SA("cosTo_s", [HD, 1024], F32)
    sswapTo_s = SA("sswapTo_s", [HD, 1024], F32)

    for (dname, src) in (("cosT_s", cosT_i), ("sswapT_s", sswapT_i), ("cosTo_s", cosTo_i),
                         ("sswapTo_s", sswapTo_i), ("valid_s", valid_i),
                         ("sel_s", sel_i), ("zq_s", zq_i), ("sl0_s", sl0_i), ("ones_s", ones_i),
                         ("iota8_s", iota8_i), ("sqhd_s", sqhd_i), ("identf_s", ident_i)):
        dst = {"cosT_s": cosT_s, "sswapT_s": sswapT_s, "cosTo_s": cosTo_s,
               "sswapTo_s": sswapTo_s, "valid_s": valid_s,
               "sel_s": sel_s, "zq_s": zq_s, "sl0_s": sl0_s, "ones_s": ones_s,
               "iota8_s": iota8_s, "sqhd_s": sqhd_s, "identf_s": identf_s}[dname]
        S.dma("sync", lambda h, d=dst, s=src: h.dma_start(out=d[:], in_=s), inc="sLD")
    S.dma("gpsimd", lambda h: h.dma_start(out=ident_s[:], in_=ident_i), inc="sCA")
    S.dma("sync", lambda h: h.dma_start(out=selb_s[:], in_=selb_i), inc="sLD")
    S.dma("sync", lambda h: h.dma_start(
              out=maskin_s.rearrange("p (i c) -> p i c", c=256),
              in_=maskin_i.rearrange("i (p c) -> p i c", p=128)), inc="sLD")
    consts_ld = S.after("sLD")
    consts_ca = S.after("sCA")
    S.emit("vector", lambda h: nc.vector.reciprocal(sinv_s[:], sqhd_s[:]),
           waits=[consts_ld], inc="sVE")
    sinv_ready = S.after("sVE")

    def build_hsT(src_ap, ntiles, dst, seg_w, tag, pre=(None, 0)):
        evg = (None, 0)
        for g in range(ntiles // 2):
            for i2 in range(2):
                tt = g * 2 + i2
                w = ([evg] if g > 0 and i2 == 0 else []) + ([pre] if g == 0 and i2 == 0 else [])
                S.dma("gpsimd",
                      lambda h, i2=i2, tt=tt: h.dma_start(out=hs_bf[:, i2 * D:(i2 + 1) * D],
                                                          in_=src_ap[tt * 128:(tt + 1) * 128, :]),
                      waits=w, inc="sCA")
            ca_done = S.after("sCA")
            for dt in range(NDT):
                for i2 in range(2):
                    w = [ca_done, S.after("sSC")] if i2 == 0 else []
                    S.emit("tensor",
                           lambda h, i2=i2, dt=dt: nc.tensor.transpose(
                               ps_tr[:, i2 * 128:(i2 + 1) * 128],
                               hs_bf[:, i2 * D + dt * 128: i2 * D + (dt + 1) * 128], ident_s[:]),
                           waits=w, inc=("sPE" if i2 == 1 else None))
                S.emit("scalar",
                       lambda h, dt=dt, g=g, dst=dst, seg_w=seg_w: nc.scalar.copy(
                           out=dst[:, dt * seg_w + g * 256: dt * seg_w + (g + 1) * 256],
                           in_=ps_tr[:, 0:256]),
                       waits=[S.after("sPE")], inc="sSC")
            evg = S.after("sSC")
        return S.after("sSC")

    ctxAo = ExitStack()
    hsT_own = ctxAo.enter_context(nc.sbuf_tensor("hsT_own", [128, NDT * 1024], BF16))
    hsTo_done = build_hsT(hs_own, 8, hsT_own, 1024, "own")

    def wsrc(kind, idx):
        if kind == "q":
            return WqT[:, idx * 128:(idx + 1) * 128]
        if kind == "k":
            return WkT_i[:, idx * 128:(idx + 1) * 128]
        return WvT_i[:, idx * 128:(idx + 1) * 128]

    pj_free = {0: (None, 0), 1: (None, 0)}
    panel_hist = []

    def proj_group(otiles, hsT_src, seg_w, first_ready, t_off=0):
        for jj, (kind, idx) in enumerate(otiles):
            slot = jj % 2
            S.dma("gpsimd",
                  lambda h, slot=slot, kind=kind, idx=idx: h.dma_start(
                      out=wpan[:, slot * NDT * 128:(slot + 1) * NDT * 128]
                          .rearrange("p (dt c) -> p dt c", c=128),
                      in_=wsrc(kind, idx).rearrange("(dt p) c -> p dt c", p=128)),
                  waits=([panel_hist[-2]] if len(panel_hist) >= 2 else []),
                  inc="sW")
            w_ready = S.after("sW")
            first_waits = [w_ready, first_ready, pj_free[slot]]
            nmm = seg_w // 512
            pj = ps_pj2[slot]
            for dt in range(NDT):
                pan = wpan[:, slot * NDT * 128 + dt * 128: slot * NDT * 128 + (dt + 1) * 128]
                st, fin = (dt == 0), (dt == NDT - 1)
                def mmp(h, pan=pan, dt=dt, st=st, fin=fin, nmm=nmm, seg_w=seg_w,
                        hsT_src=hsT_src, pj=pj):
                    last = None
                    for c in range(nmm):
                        last = nc.tensor.matmul(pj[:, c * 512:(c + 1) * 512], pan,
                                                hsT_src[:, dt * seg_w + c * 512: dt * seg_w + (c + 1) * 512],
                                                start=st, stop=fin)
                    return last
                S.emit("tensor", mmp, waits=(first_waits if dt == 0 else []),
                       inc=("sPE" if fin else None))
            panel_hist.append(S.after("sPE"))
            mm_done = S.after("sPE")

            if kind in ("q", "k"):
                W = seg_w
                ct = cosTo_s if kind == "q" else cosT_s
                sw = sswapTo_s if kind == "q" else sswapT_s
                S.emit("vector", lambda h, W=W, ct=ct, t_off=t_off, pj=pj: nc.vector.tensor_mul(
                           qk_tmp[:, :W], pj[:, :W], ct[:, t_off:t_off + W]),
                       waits=[mm_done, sinv_ready])
                S.emit("vector", lambda h, W=W, sw=sw, t_off=t_off, pj=pj: nc.vector.tensor_mul(
                           qk_tmp2[0:64, :W], pj[64:128, :W], sw[0:64, t_off:t_off + W]))
                S.emit("vector", lambda h, W=W, sw=sw, t_off=t_off, pj=pj: nc.vector.tensor_mul(
                           qk_tmp2[64:128, :W], pj[0:64, :W], sw[64:128, t_off:t_off + W]))
                def rope_add(h, W=W):
                    h.drain()
                    return nc.vector.tensor_add(qk_stage[:, :W], qk_tmp[:, :W], qk_tmp2[:, :W])
                S.emit("vector", rope_add, inc="sVE")
                pj_free[slot] = S.after("sVE")
                rope_done = S.after("sVE")
                dst = qrt_d[idx] if kind == "q" else krt_d[idx][:, t_off:t_off + W]
                S.dma("sync", lambda h, dst=dst, W=W: h.dma_start(out=dst, in_=qk_stage[:, :W]),
                      waits=[rope_done], inc="sST")
                park = S.after("sST")
                S.emit("vector", lambda h: nc.vector.engine_nop(), waits=[park])
            else:
                S.emit("scalar", lambda h, pj=pj, seg_w=seg_w: nc.scalar.copy(
                           out=v_stage[:, 0:seg_w], in_=pj[:, 0:seg_w]),
                       waits=[mm_done, S.after("sST")], inc="sSC")
                pj_free[slot] = S.after("sSC")
                ev_done = S.after("sSC")
                nkt4 = seg_w // 512
                for ktg in range(nkt4):
                    for i2 in range(4):
                        kt = ktg * 4 + i2
                        w = [ev_done, S.after("sSC")] if i2 == 0 else []
                        S.emit("tensor",
                               lambda h, kt=kt, i2=i2: nc.tensor.transpose(
                                   ps_tr[:, i2 * 128:(i2 + 1) * 128],
                                   v_stage[:, kt * 128:(kt + 1) * 128], ident_s[:]),
                               waits=w, inc=("sPE" if i2 == 3 else None))
                    S.emit("scalar",
                           lambda h, ktg=ktg: nc.scalar.copy(
                               out=vn_stage[:, ktg * 512:(ktg + 1) * 512], in_=ps_tr[:]),
                           waits=[S.after("sPE")], inc="sSC")
                vt_done = S.after("sSC")
                kt0 = t_off // 128
                S.dma("sync", lambda h, idx=idx, kt0=kt0, seg_w=seg_w: h.dma_start(
                          out=v_d[idx][kt0:kt0 + seg_w // 128].rearrange("kt p c -> p kt c"),
                          in_=vn_stage[:, 0:seg_w].rearrange("p (kt c) -> p kt c", c=128)),
                      waits=[vt_done], inc="sST")
                park = S.after("sST")
                S.emit("scalar", lambda h: nc.scalar.copy(out=v_stage[0:1, 0:1], in_=v_stage[0:1, 0:1]),
                       waits=[park])

    proj_group([("q", i) for i in range(32)], hsT_own, 1024, hsTo_done)
    ctxAo.close()
    hsT = ctxA.enter_context(nc.sbuf_tensor("hsT", [128, NDT * 1024], BF16))
    for half in range(2):
        hsT_h_done = build_hsT(hs[half * 1024:(half + 1) * 1024, :], 8, hsT, 1024, "full%d" % half,
                               pre=S.after("sPE"))
        proj_group([("k", i) for i in range(8)] + [("v", i) for i in range(8)],
                   hsT, 1024, hsT_h_done, t_off=half * 1024)

    phaseA_st = S.after("sST")

    if dev_phase in ("B", "full"):
        ctxA.close()
        ctxB = ExitStack()
        def SBX(name, shape, dt):
            return ctxB.enter_context(nc.sbuf_tensor(name, list(shape), dt))
        def PSX(name, shape, dt):
            return ctxB.enter_context(nc.psum_tensor(name, list(shape), dt))
        SgB = SBX("SgB", [128, 8 * 2048], F32)
        SgS = SBX("SgS", [128, 8 * 1024], F32)
        Ppre2 = SBX("Ppre2", [128, 2 * 2048], F32)
        Bscr = SBX("Bscr", [128, 2048], F32)
        Ascr = SBX("Ascr", [128, 2048], F32)
        Tbs = SBX("Tbs", [128, 2048], BF16)
        Tas = SBX("Tas", [128, 2048], BF16)
        junkb = SBX("junkb", [128, 2048], BF16)
        Pfin2 = SBX("Pfin2", [128, 2 * 2048], BF16)
        PTb = SBX("PTb", [128, 16 * 512], BF16)
        krt2 = SBX("krt2", [128, 2 * 2048], BF16)
        v2 = SBX("v2", [128, 2 * 2 * 2048], BF16)
        qrt2 = SBX("qrt2", [128, 8 * 128], BF16)
        NPL = 28
        stp = SBX("stp", [128, 2 * NPL * 8], F32)
        NPI = 6
        stpi = SBX("stpi", [128, 2 * NPI * 8], mybir.dt.int32)
        W64 = SBX("W64", [128, 64], F32)
        Gb64 = SBX("Gb64", [128, 64], F32)
        Ga64 = SBX("Ga64", [128, 64], F32)
        m8b64 = SBX("m8b64", [128, 64], F32)
        m8a64 = SBX("m8a64", [128, 64], F32)
        negc8 = SBX("negc8", [128, 8], F32)
        ot_stage = SBX("ot_stage", [128, 2 * 512], BF16)
        sinvn_s = SBX("sinvn_s", [128, 1], F32)
        ps_S = [PSX("ps_S0", [128, 512], F32), PSX("ps_S1", [128, 512], F32)]
        ps_pt = [PSX("ps_pt0", [128, 1024], BF16), PSX("ps_pt1", [128, 1024], BF16)]
        ps_av = [PSX("ps_av0", [128, 512], F32), PSX("ps_av1", [128, 512], F32)]

        PL = {n: j for j, n in enumerate(
            ["MU", "SQS", "SD", "T", "SL", "NEGT", "TP", "CP", "C", "RMIN",
             "B0", "VMS", "VMB", "VMA", "SIDE", "IDX", "Z", "ZI", "TLO", "THI",
             "SELB", "X0", "X1", "X2", "X3", "X4", "PAD", "PAD2"])}
        PLI = {n: j for j, n in enumerate(["UP", "DN", "O", "SIDE", "SELB", "OK"])}
        def pli(n, sl):
            j = sl * NPI + PLI[n]
            return stpi[:, j * 8:(j + 1) * 8]
        def pl(n, sl):
            j = sl * NPL + PL[n]
            return stp[:, j * 8:(j + 1) * 8]
        def plc(n, sl, h8):
            j = sl * NPL + PL[n]
            return stp[:, j * 8 + h8: j * 8 + h8 + 1]

        ORDER = [4, 0, 5, 1, 6, 2, 7, 3]
        tiles = [(oct_, i) for oct_ in range(4) for i in ORDER]
        NT = len(tiles)
        def tinfo(k):
            oct_, i = tiles[k]
            return oct_, i, 256 * (i + 1), 2 * (i + 1), k % 2
        ev = [dict() for _ in range(NT)]
        mm_ev = []
        evac_ev = []
        N_CNT = 8
        NVH = 1
        INTERLEAVE_EXC = True
        PIPE_DEEP = True
        W_LO, W_HI, TGT = float(NREM - 8), float(NREM + 7), 204.5
        V = "vector"

        bar = phaseA_st
        first_b = {e: True for e in ENGS}
        def wb(eng, waits):
            w = [x for x in waits if x is not None]
            if first_b[eng]:
                w.append(bar)
                first_b[eng] = False
            return w
        def ve(fn, waits=(), inc=None):
            S.emit(V, fn, waits, inc)

        S.emit("vector", lambda h: nc.vector.tensor_scalar(
                   sinvn_s[:], sinv_s[:], -1.0, None, op0=ALU.mult),
               waits=wb("vector", [sinv_ready]), inc="sVE")
        S.emit("vector", lambda h: nc.vector.memset(stp[:, :], 0.0))
        S.emit("vector", lambda h: nc.vector.memset(negc8[:, :], -1e29), inc="sVE")
        stp_z = S.after("sVE")

        def Sg_of(k):
            return SgB if k % 2 == 0 else SgS
        def sgw(k):
            return 2048 if k % 2 == 0 else 1024

        def emit_load(u):
            oct_, i, kpad, nkt, sl = tinfo(u)
            kv0 = oct_ * 2
            w0 = [("sAV", u - 1)] if u >= 2 else []
            if mm_ev:
                w0.append(mm_ev[-1])
            S.dma("sync", lambda h, oct_=oct_, i=i: h.dma_start(
                      out=qrt2[:, :].rearrange("p (hh q) -> p hh q", q=128),
                      in_=qrt_d[oct_ * 8:(oct_ + 1) * 8, :, i * 128:(i + 1) * 128]
                          .rearrange("hh p q -> p hh q")),
                  waits=wb("sync", w0), inc="sLD")
            for g in range(2):
                S.dma("sync", lambda h, g=g, kv=kv0 + g, kpad=kpad: h.dma_start(
                          out=krt2[:, g * 2048: g * 2048 + kpad],
                          in_=krt_d[kv][:, 0:kpad]),
                      inc="sLD")
                S.dma("sync", lambda h, g=g, kv=kv0 + g, kpad=kpad, nkt=nkt, sl=sl: h.dma_start(
                          out=v2[:, sl * 4096 + g * 2048: sl * 4096 + g * 2048 + kpad]
                              .rearrange("p (kt c) -> p kt c", c=128),
                          in_=v_d[kv][0:nkt].rearrange("kt p c -> p kt c")),
                      inc="sLD")
            ev[u]["ld"] = S.after("sLD")

        def emit_smm(u):
            oct_, i, kpad, nkt, sl = tinfo(u)
            nch = (kpad + 511) // 512
            ev[u]["nch"] = nch
            ev[u]["mm0"] = len(mm_ev)
            sg_free = ("sSG", u - 1) if u >= 2 else None
            first = True
            for h8 in range(8):
                g = h8 // 4
                for c in range(nch):
                    cw = min(512, kpad - c * 512)
                    gcc = len(mm_ev)
                    pp = gcc % 2
                    wv = []
                    if first:
                        wv = wb("tensor", [ev[u]["ld"], sg_free])
                        first = False
                    if gcc >= 2:
                        wv.append(("sEV", gcc - 1))
                    def mm(h, h8=h8, g=g, c=c, cw=cw, pp=pp):
                        return nc.tensor.matmul(
                            ps_S[pp][:, 0:cw],
                            qrt2[:, h8 * 128:(h8 + 1) * 128],
                            krt2[:, g * 2048 + c * 512: g * 2048 + c * 512 + cw],
                            start=True, stop=True)
                    S.emit("tensor", mm, waits=wv, inc="sPE")
                    mm_ev.append(S.after("sPE"))

        def build_evq(u):
            oct_, i, kpad, nkt, sl = tinfo(u)
            Sg = Sg_of(u); W = sgw(u)
            nch = ev[u]["nch"]
            q = []
            gcc = ev[u]["mm0"]
            for h8 in range(8):
                for c in range(nch):
                    cw = min(512, kpad - c * 512)
                    def evf(h, h8=h8, c=c, cw=cw, pp=gcc % 2, Sg=Sg, W=W):
                        return nc.scalar.copy(
                            out=Sg[:, h8 * W + c * 512: h8 * W + c * 512 + cw],
                            in_=ps_S[pp][:, 0:cw])
                    q.append((evf, [mm_ev[gcc]], "ev", gcc))
                    gcc += 1
                def sqf(h, h8=h8, Sg=Sg, W=W, kpad=kpad, sl=sl):
                    src = Sg[:, h8 * W: h8 * W + kpad].rearrange(
                        "p (a f) -> p a f", f=4)[:, :, 0:1]
                    return nc.scalar.activation(out=junkb[:, 0:kpad // 4], in_=src,
                                                func=AF.Square,
                                                accum_out=plc("SQS", sl, h8))
                def muf(h, h8=h8, Sg=Sg, W=W, kpad=kpad, sl=sl):
                    src = Sg[:, h8 * W: h8 * W + kpad].rearrange(
                        "p (a f) -> p a f", f=4)[:, :, 0:1]
                    return nc.scalar.activation(out=junkb[:, 0:kpad // 4], in_=src,
                                                func=AF.Copy,
                                                accum_out=plc("MU", sl, h8))
                q.append((sqf, [], "st", h8))
                q.append((muf, [], "st", h8))
            ev[u]["evq"] = q
            ev[u]["stat"] = {}

        def drain_evq(u, n=None):
            if u is None or u >= NT:
                return
            qq = ev[u].get("evq")
            if not qq:
                return
            cnt = len(qq) if n is None else min(n, len(qq))
            for _ in range(cnt):
                fn, w, kind, idx = qq.pop(0)
                if kind == "ev":
                    S.emit("scalar", fn, waits=wb("scalar", w), inc="sEV")
                else:
                    S.emit("scalar", fn, waits=wb("scalar", w), inc="sSC")
                    ev[u]["stat"][idx] = S.after("sSC")

        def emit_postload(u):
            oct_, i, kpad, nkt, sl = tinfo(u)
            Sg = Sg_of(u); W = sgw(u)
            if i == 0:
                for h8 in range(8):
                    def rmf(h, h8=h8, Sg=Sg, W=W, kpad=kpad, sl=sl):
                        return nc.vector.tensor_reduce(
                            plc("RMIN", sl, h8), Sg[:, h8 * W: h8 * W + kpad],
                            axis=AX.X, op=ALU.min)
                    ve(rmf, waits=[ev[u]["stat"][h8], stp_z] if h8 == 0 else [ev[u]["stat"][h8]],
                       inc=("sVE" if h8 == 7 else None))
                ev[u]["rmin"] = S.after("sVE")
            for h8 in range(8):
                w = [ev[u]["stat"][h8]]
                if i == 0 and h8 == 0:
                    w.append(ev[u]["rmin"])
                def dcf(h, h8=h8, Sg=Sg, W=W, kpad=kpad, i=i):
                    return nc.vector.tensor_tensor(
                        out=Sg[:, h8 * W + kpad - 256: h8 * W + kpad],
                        in0=Sg[:, h8 * W + kpad - 256: h8 * W + kpad],
                        in1=maskin_s[:, i * 256:(i + 1) * 256], op=ALU.add)
                S.emit("vector", dcf, waits=w,
                       inc=("sVE" if h8 == 7 else None))
            ev[u]["dca"] = S.after("sVE")
            n4 = float(kpad // 4)
            def init1(h, sl=sl, n4=n4):
                nc.gpsimd.tensor_scalar(pl("MU", sl), pl("MU", sl), 1.0 / n4, None, op0=ALU.mult)
                nc.gpsimd.tensor_scalar(pl("SQS", sl), pl("SQS", sl), 1.0 / n4, None, op0=ALU.mult)
                nc.gpsimd.tensor_tensor(pl("X0", sl), pl("MU", sl), pl("MU", sl), op=ALU.mult)
                nc.gpsimd.tensor_tensor(pl("SQS", sl), pl("SQS", sl), pl("X0", sl), op=ALU.subtract)
                return nc.gpsimd.tensor_scalar(pl("SQS", sl), pl("SQS", sl), 1e-6, None, op0=ALU.max)
            S.emit("gpsimd", init1,
                   waits=[ev[u]["stat"][h] for h in range(8)] + [stp_z], inc="sGP")
            i1d = S.after("sGP")
            def sd1(h, sl=sl):
                nc.scalar.activation(pl("X1", sl), pl("SQS", sl), AF.Ln)
                h.drain()
                return nc.scalar.activation(pl("SD", sl), pl("X1", sl), AF.Exp, scale=0.5)
            S.emit("scalar", sd1, waits=[i1d], inc="sSC")
            sdd = S.after("sSC")
            zqc = zq_s[:, i:i + 1]
            slc = sl0_s[:, i:i + 1]
            selc = sel_s[:, i:i + 1]
            def init2(h, sl=sl, zqc=zqc, slc=slc, selc=selc):
                ts, tt = nc.vector.tensor_scalar, nc.vector.tensor_tensor
                ts(pl("X0", sl), pl("SD", sl), zqc, None, op0=ALU.mult)
                ts(pl("SL", sl), pl("SD", sl), slc, None, op0=ALU.mult)
                tt(pl("T", sl), pl("MU", sl), pl("X0", sl), op=ALU.add)
                nc.vector.memset(pl("TLO", sl), -29000.0)
                nc.vector.memset(pl("THI", sl), 29000.0)
                return ts(pl("NEGT", sl), pl("T", sl), -1.0, None, op0=ALU.mult)
            ve(init2, waits=[sdd], inc="sVE")
            ev[u]["init"] = S.after("sVE")

        def emit_counts(k, r):
            oct_, i, kpad, nkt, sl = tinfo(k)
            Sg = Sg_of(k); W = sgw(k)
            w_first = [ev[k]["init"], ev[k]["dca"]] if r == 1 else [ev[k]["upd"]]
            def cv(h, Sg=Sg, kpad=kpad, sl=sl):
                return nc.vector.tensor_scalar(
                    Bscr[:, 0:kpad], Sg[:, 0:kpad], plc("T", sl, 0), None,
                    op0=ALU.is_gt, op1=ALU.add, accum_out=plc("C", sl, 0))
            ve(cv, waits=list(w_first), inc="sVE")
            ev[k]["cv"] = S.after("sVE")
            for h8 in range(1, 8):
                def cs(h, h8=h8, Sg=Sg, W=W, kpad=kpad, sl=sl):
                    return nc.scalar.activation(
                        out=junkb[:, 0:kpad], in_=Sg[:, h8 * W: h8 * W + kpad],
                        func=AF.Sign, bias=plc("NEGT", sl, h8), scale=1.0,
                        accum_out=plc("C", sl, h8))
                S.emit("scalar", cs, waits=(list(w_first) if h8 == NVH else []),
                       inc=("sSC" if h8 == 7 else None))
            ev[k]["cs"] = S.after("sSC")

        def emit_upd(k, r):
            oct_, i, kpad, nkt, sl = tinfo(k)
            def upd(h, sl=sl, r=r, kpad=kpad):
                P = lambda n: pl(n, sl)
                PI = lambda n: pli(n, sl)
                ts, tt, sel = nc.vector.tensor_scalar, nc.vector.tensor_tensor, nc.vector.select
                def pad():
                    nc.vector.tensor_copy(P("PAD"), P("C"))
                cs_cols = stp[:, (sl * NPL + PL["C"]) * 8 + NVH:(sl * NPL + PL["C"]) * 8 + 8]
                ts(cs_cols, cs_cols, float(kpad), 0.5, op0=ALU.add, op1=ALU.mult)
                tt(P("X2"), P("T"), P("TP"), op=ALU.subtract)
                ts(P("X0"), P("C"), TGT, None, op0=ALU.subtract)
                tt(P("X1"), P("C"), P("CP"), op=ALU.subtract)
                ts(PI("UP"), P("C"), float(NREM), None, op0=ALU.is_ge)
                ts(PI("DN"), P("C"), float(NREM), None, op0=ALU.is_lt)
                ts(P("X3"), P("X0"), W_LO - TGT, W_HI - TGT, op0=ALU.max, op1=ALU.min)
                if r >= 2:
                    nc.vector.reciprocal(P("X4"), P("X1"))
                else:
                    pad()
                tt(P("X0"), P("X0"), P("X3"), op=ALU.subtract)
                if r >= 2:
                    tt(P("X2"), P("X2"), P("X4"), op=ALU.mult)
                else:
                    pad()
                sel(P("TLO"), PI("UP"), P("T"), P("TLO"))
                if r >= 2:
                    tt(P("X1"), P("X1"), P("X1"), op=ALU.mult)
                    ts(P("X4"), P("X2"), -1.0, None, op0=ALU.mult)
                    ts(PI("OK"), P("X1"), 0.25, None, op0=ALU.is_ge)
                    tt(P("X2"), P("X2"), P("X4"), op=ALU.max)
                else:
                    pad(); pad(); pad(); pad()
                sel(P("THI"), PI("DN"), P("T"), P("THI"))
                if r >= 2:
                    ts(P("X2"), P("X2"), 1e-5, 1e3, op0=ALU.max, op1=ALU.min)
                else:
                    pad()
                nc.vector.tensor_copy(P("TP"), P("T"))
                nc.vector.tensor_copy(P("CP"), P("C"))
                if r >= 2:
                    sel(P("SL"), PI("OK"), P("X2"), P("SL"))
                else:
                    pad()
                tt(P("X4"), P("TLO"), P("THI"), op=ALU.add)
                tt(P("X0"), P("X0"), P("SL"), op=ALU.mult)
                ts(P("X4"), P("X4"), 0.5, None, op0=ALU.mult)
                tt(P("X3"), P("T"), P("X0"), op=ALU.add)
                pad()
                tt(P("X1"), P("X3"), P("THI"), op=ALU.is_gt)
                tt(P("X0"), P("X3"), P("TLO"), op=ALU.is_lt)
                pad()
                tt(P("X1"), P("X1"), P("X0"), op=ALU.max)
                pad()
                nc.vector.tensor_copy(PI("O"), P("X1"))
                pad()
                sel(P("T"), PI("O"), P("X4"), P("X3"))
                pad()
                return ts(P("NEGT"), P("T"), -1.0, None, op0=ALU.mult)
            ve(upd, waits=[ev[k]["cs"]], inc="sVE")
            ev[k]["upd"] = S.after("sVE")

        def emit_fin(k):
            oct_, i, kpad, nkt, sl = tinfo(k)
            def fin(h, sl=sl, i=i):
                P = lambda n: pl(n, sl)
                PI = lambda n: pli(n, sl)
                ts, tt, sel = nc.vector.tensor_scalar, nc.vector.tensor_tensor, nc.vector.select
                if i == 0:
                    sel(P("X0"), selb_s[:, i * 8:(i + 1) * 8], P("T"), P("RMIN"))
                    nc.vector.tensor_copy(P("PAD"), P("C"))
                    nc.vector.tensor_copy(P("T"), P("X0"))
                    nc.vector.tensor_copy(P("PAD"), P("C"))
                return ts(P("B0"), P("T"), sinvn_s[:, 0:1], None, op0=ALU.mult)
            ve(fin, waits=[ev[k]["cs"], ev[k]["cv"]], inc="sVE")
            ev[k]["fin"] = S.after("sVE")

        def emit_ext_exc(k, h8):
            oct_, i, kpad, nkt, sl = tinfo(k)
            Sg = Sg_of(k); W = sgw(k)
            def exc(h, h8=h8, Sg=Sg, W=W, kpad=kpad, sl=sl):
                ts, stt = nc.vector.tensor_scalar, nc.vector.scalar_tensor_tensor
                sg = Sg[:, h8 * W: h8 * W + kpad]
                ts(Tbs[:, 0:kpad], sg, plc("T", sl, h8), -2e30, op0=ALU.is_gt, op1=ALU.mult)
                ts(Tas[:, 0:kpad], sg, plc("T", sl, h8), -2e30, op0=ALU.is_le, op1=ALU.mult)
                stt(out=Bscr[:, 0:kpad], in0=Tbs[:, 0:kpad], scalar=1.0,
                    in1=sg, op0=ALU.mult, op1=ALU.add)
                stt(out=Ascr[:, 0:kpad], in0=Tas[:, 0:kpad], scalar=1.0,
                    in1=sg, op0=ALU.mult, op1=ALU.subtract)
                nc.vector.max(out=m8b64[:, h8 * 8:(h8 + 1) * 8], in_=Bscr[:, 0:kpad])
                return nc.vector.max(out=m8a64[:, h8 * 8:(h8 + 1) * 8], in_=Ascr[:, 0:kpad])
            ve(exc, waits=([ev[k]["fin"]] if h8 == 0 else []),
               inc=("sVE" if h8 == 7 else None))

        def emit_ext(k):
            oct_, i, kpad, nkt, sl = tinfo(k)
            Sg = Sg_of(k); W = sgw(k)
            def vmb(h, sl=sl, i=i, kpad=kpad):
                P = lambda n: pl(n, sl)
                PI = lambda n: pli(n, sl)
                ts, tt, sel = nc.vector.tensor_scalar, nc.vector.tensor_tensor, nc.vector.select
                def pad():
                    nc.vector.tensor_copy(P("PAD"), P("X4"))
                cs_cols = stp[:, (sl * NPL + PL["C"]) * 8 + NVH:(sl * NPL + PL["C"]) * 8 + 8]
                ts(cs_cols, cs_cols, float(kpad), 0.5, op0=ALU.add, op1=ALU.mult)
                pad()
                ts(P("X4"), P("C"), -0.25, 8388608.0, op0=ALU.add, op1=ALU.add)
                pad()
                ts(P("C"), P("X4"), -8388608.0, None, op0=ALU.add)
                pad()
                ts(P("X0"), P("C"), TGT, None, op0=ALU.subtract)
                ts(PI("SIDE"), P("C"), float(NREM), None, op0=ALU.is_ge)
                ts(P("X1"), P("X0"), -1.0, None, op0=ALU.mult)
                pad()
                tt(P("X0"), P("X0"), P("X1"), op=ALU.max)
                pad()
                ts(P("X0"), P("X0"), -0.5, None, op0=ALU.add)
                pad()
                ts(P("IDX"), P("X0"), 0.0, 7.0, op0=ALU.max, op1=ALU.min)
                pad()
                for h8 in range(8):
                    ts(W64[:, h8 * 8:(h8 + 1) * 8], iota8_s[:], plc("IDX", sl, h8), None,
                       op0=ALU.is_equal)
                pad()
                tt(Gb64[:], W64[:], m8b64[:], op=ALU.mult)
                tt(Ga64[:], W64[:], m8a64[:], op=ALU.mult)
                nc.vector.tensor_reduce(P("VMB"), Gb64.rearrange("p (hh j) -> p hh j", j=8),
                                        axis=AX.X, op=ALU.add)
                nc.vector.tensor_reduce(P("VMA"), Ga64.rearrange("p (hh j) -> p hh j", j=8),
                                        axis=AX.X, op=ALU.add)
                pad()
                ts(P("X1"), P("VMA"), -1.0, None, op0=ALU.mult)
                pad()
                sel(P("VMS"), PI("SIDE"), P("X1"), P("VMB"))
                if i == 0:
                    pad()
                    sel(P("X2"), selb_s[:, i * 8:(i + 1) * 8], P("VMS"), negc8[:])
                    pad()
                    nc.vector.tensor_copy(P("VMS"), P("X2"))
                return nc.vector.engine_nop()
            ve(vmb, waits=[], inc="sVE")
            ev[k]["vm"] = S.after("sVE")
            if dev_phase == "B" and k == 5:
                S.dma("sync", lambda h, Sg=Sg, W=W: h.dma_start(
                          out=dev["dbg_sg"][:, 0:8 * W].rearrange("p (hh c) -> p hh c", c=W),
                          in_=Sg.rearrange("p (hh c) -> p hh c", c=W)),
                      waits=[ev[k]["vm"]], inc="sST")
            if dev_phase == "B":
                S.dma("sync", lambda h, k=k, sl=sl: h.dma_start(
                          out=dev["dbg_stp"][k],
                          in_=stp[:, sl * NPL * 8:(sl + 1) * NPL * 8]),
                      waits=[ev[k]["vm"]], inc="sST")
                S.dma("sync", lambda h, k=k: h.dma_start(
                          out=dev["dbg_m8"][k][:, 0:64], in_=m8b64[:]), inc="sST")
                S.dma("sync", lambda h, k=k: h.dma_start(
                          out=dev["dbg_m8"][k][:, 64:128], in_=m8a64[:]), inc="sST")
            pf_done = {}
            pz_done = {}
            ptbev = {}
            trp_done = {}
            for h8 in range(8):
                def exf(h, h8=h8, Sg=Sg, W=W, kpad=kpad, sl=sl):
                    return nc.scalar.activation(
                        out=Ppre2[:, (h8 % 2) * 2048:(h8 % 2) * 2048 + kpad],
                        in_=Sg[:, h8 * W: h8 * W + kpad], func=AF.Exp,
                        bias=plc("B0", sl, h8), scale=sinv_s[:, 0:1])
                wx = [ev[k]["fin"]]
                if h8 >= 2:
                    wx.append(pf_done[h8 - 2])
                S.emit("scalar", exf, waits=wx, inc="sSC")
                expd = S.after("sSC")
                def pff(h, h8=h8, Sg=Sg, W=W, kpad=kpad, sl=sl):
                    nc.vector.scalar_tensor_tensor(
                        out=Ppre2[:, (h8 % 2) * 2048:(h8 % 2) * 2048 + kpad],
                        in0=Sg[:, h8 * W: h8 * W + kpad], scalar=plc("VMS", sl, h8),
                        in1=Ppre2[:, (h8 % 2) * 2048:(h8 % 2) * 2048 + kpad],
                        op0=ALU.is_ge, op1=ALU.mult, accum_out=plc("Z", sl, h8))
                    nc.vector.tensor_copy(pl("PAD", sl), pl("C", sl))
                    return nc.vector.reciprocal(plc("ZI", sl, h8), plc("Z", sl, h8))
                ve(pff, waits=[expd, ev[k]["vm"]], inc="sVE")
                pf_done[h8] = S.after("sVE")
                def pzf(h, h8=h8, kpad=kpad, sl=sl):
                    return nc.scalar.activation(
                        out=Pfin2[:, (h8 % 2) * 2048:(h8 % 2) * 2048 + kpad],
                        in_=Ppre2[:, (h8 % 2) * 2048:(h8 % 2) * 2048 + kpad],
                        func=AF.Identity, scale=plc("ZI", sl, h8))
                wz = [pf_done[h8]]
                if h8 >= 2:
                    wz.append(ev[k]["trp%d" % (h8 - 2)])
                S.emit("scalar", pzf, waits=wz, inc="sSC")
                pz_done[h8] = S.after("sSC")
                x = h8 % 4
                nbat = (nkt + 7) // 8
                ptbev[h8] = []
                for b in range(nbat):
                    lo = b * 8
                    hi = min(nkt, lo + 8)
                    pb = (h8 * nbat + b) % 2
                    def trf(h, h8=h8, lo=lo, hi=hi, pb=pb, kpad=kpad):
                        last = None
                        for kt in range(lo, hi):
                            last = nc.tensor.transpose(
                                ps_pt[pb][:, (kt - lo) * 128:(kt - lo + 1) * 128],
                                Pfin2[:, (h8 % 2) * 2048 + kt * 128:
                                      (h8 % 2) * 2048 + (kt + 1) * 128],
                                ident_s[:])
                        return last
                    wt = [pz_done[h8]]
                    if len(ptb_hist) >= 2:
                        wt.append(ptb_hist[-2])
                    S.emit("tensor", trf, waits=wt, inc="sPE")
                    trd = S.after("sPE")
                    def tev(h, x=x, lo=lo, hi=hi, pb=pb):
                        return nc.vector.tensor_copy(
                            out=PTb.rearrange("p (kt xx c) -> p kt xx c", xx=4, c=128)[:, lo:hi, x],
                            in_=ps_pt[pb][:, 0:(hi - lo) * 128]
                                .rearrange("p (kt c) -> p kt c", c=128))
                    ve(tev, waits=[trd], inc="sVE")
                    ptb_hist.append(S.after("sVE"))
                    ptbev[h8].append(S.after("sVE"))
                ev[k]["trp%d" % h8] = trd
                if h8 % 4 == 3:
                    g = h8 // 4
                    wav = [e for hh in range(4 * g, 4 * g + 4) for e in ptbev[hh]]
                    if k >= 1 and ("avev%d" % g) in ev[k - 1]:
                        wav.append(ev[k - 1]["avev%d" % g])
                    def avf(h, g=g, nkt=nkt, sl=sl):
                        last = None
                        for kt in range(nkt):
                            last = nc.tensor.matmul(
                                ps_av[g][:, 0:512],
                                v2[:, sl * 4096 + g * 2048 + kt * 128:
                                   sl * 4096 + g * 2048 + (kt + 1) * 128],
                                PTb[:, kt * 512:(kt + 1) * 512],
                                start=(kt == 0), stop=(kt == nkt - 1))
                        return last
                    S.emit("tensor", avf, waits=wav, inc="sPE")
                    avd = S.after("sPE")
                    def aev(h, g=g):
                        return nc.scalar.copy(out=ot_stage[:, g * 512:(g + 1) * 512],
                                              in_=ps_av[g][:, 0:512])
                    S.emit("scalar", aev, waits=[avd], inc="sSC")
                    ev[k]["avev%d" % g] = S.after("sSC")
                    oct2, i2 = tiles[k]
                    hh0 = oct2 * 8 + g * 4
                    S.dma("sync", lambda h, hh0=hh0, i2=i2, g=g: h.dma_start(
                              out=ot_d[hh0:hh0 + 4, i2].rearrange("hh p q -> p hh q"),
                              in_=ot_stage[:, g * 512:(g + 1) * 512]
                                  .rearrange("p (hh q) -> p hh q", q=128)),
                          waits=[ev[k]["avev%d" % g]], inc="sST")
                    if g == 1:
                        S.emit("vector", lambda h: nc.vector.engine_nop(),
                               waits=[avd], inc="sAV")
            ev[k]["sgfree"] = pf_done[7]
            S.emit("vector", lambda h: nc.vector.engine_nop(),
                   waits=[pf_done[7]], inc="sSG")

        ptb_hist = []

        emit_load(0)
        emit_smm(0)
        build_evq(0)
        drain_evq(0, None)
        emit_postload(0)
        if not PIPE_DEEP:
            for k in range(NT):
                u = k + 1 if k + 1 < NT else None
                if u is not None:
                    emit_load(u)
                    emit_smm(u)
                    build_evq(u)
                for r in range(1, N_CNT + 1):
                    emit_counts(k, r)
                    if r < N_CNT:
                        emit_upd(k, r)
                    drain_evq(u, 4)
                drain_evq(u, None)
                emit_fin(k)
                if u is not None:
                    emit_postload(u)
                for hh8 in range(8):
                    emit_ext_exc(k, hh8)
                emit_ext(k)
        else:
            for k in range(NT):
                u = k + 1 if k + 1 < NT else None
                if u is not None:
                    emit_load(u)
                for r in range(1, N_CNT + 1):
                    emit_counts(k, r)
                    if k >= 1 and INTERLEAVE_EXC:
                        emit_ext_exc(k - 1, r - 1)
                    if r < N_CNT:
                        emit_upd(k, r)
                emit_fin(k)
                if k >= 1:
                    if not INTERLEAVE_EXC:
                        for hh8 in range(8):
                            emit_ext_exc(k - 1, hh8)
                    emit_ext(k - 1)
                if u is not None:
                    emit_smm(u)
                    build_evq(u)
                    drain_evq(u, None)
                    emit_postload(u)
            for h8 in range(8):
                emit_ext_exc(NT - 1, h8)
            emit_ext(NT - 1)
        phaseB_st = S.after("sST")
        if dev_phase == "B":
            S.dma("sync", lambda h: h.dma_start(out=dev["dbg_ot"], in_=ot_d),
                  waits=[phaseB_st], inc="sST")
            S.emit("vector", lambda h: nc.vector.memset(SgB[:, 0:2048], 0.0), inc="sVE")
            zr2 = S.after("sVE")
            ochunks = out_t.rearrange("(c p) d -> c p d", p=128)
            for c in range(8):
                for hf in range(2):
                    S.dma("sync", lambda h, c=c, hf=hf: h.dma_start(
                              out=ochunks[c][:, hf * 2048:(hf + 1) * 2048], in_=SgB[:, 0:2048]),
                          waits=[zr2], inc="sST")
            S.emit("sync", lambda h: h.nop(), waits=[S.after("sST")])

    if dev_phase == "full":
        ctxB.close()
        first_c = {e: True for e in ENGS}
        def wc(eng, waits):
            w = list(waits)
            if first_c[eng]:
                w.append(phaseB_st)
                first_c[eng] = False
            return w
        ctxC1 = ExitStack()
        wopan = ctxC1.enter_context(nc.sbuf_tensor("wopan", [128, NDT * 1024], BF16))
        otw = ctxC1.enter_context(nc.sbuf_tensor("otw", [128, 32 * 128], BF16))
        x1st = ctxC1.enter_context(nc.sbuf_tensor("x1st", [128, 1024], F32))
        ps_wo = [ctxC1.enter_context(nc.psum_tensor("ps_wo0", [128, 1024], F32)),
                 ctxC1.enter_context(nc.psum_tensor("ps_wo1", [128, 1024], F32))]
        x1_d = DS("x1_d", [8, 128, D], F32)

        otw_guard = (None, 0)
        x1_parks = (None, 0)
        for oq in range(4):
            for dt2 in range(NDT):
                S.dma("gpsimd", lambda h, oq=oq, dt2=dt2: h.dma_start(
                          out=wopan[:, dt2 * 1024:(dt2 + 1) * 1024],
                          in_=WoT_i[dt2 * 128:(dt2 + 1) * 128, oq * 1024:(oq + 1) * 1024]),
                      waits=wc("gpsimd", [S.after("sPE")] if dt2 == 0 and oq > 0 else []),
                      inc="sCA")
            pan_rdy = S.after("sCA")
            for i in range(8):
                S.dma("sync", lambda h, i=i: h.dma_start(
                          out=otw[:, :].rearrange("p (hh c) -> p hh c", c=128),
                          in_=ot_d[:, i].rearrange("hh p c -> p hh c")),
                      waits=wc("sync", [otw_guard]), inc="sLD")
                otl = S.after("sLD")
                buf = i % 2
                def mwo(h, buf=buf, i=i):
                    last = None
                    for dt2 in range(NDT):
                        for c5 in range(2):
                            last = nc.tensor.matmul(
                                ps_wo[buf][:, c5 * 512:(c5 + 1) * 512],
                                otw[:, dt2 * 128:(dt2 + 1) * 128],
                                wopan[:, dt2 * 1024 + c5 * 512: dt2 * 1024 + (c5 + 1) * 512],
                                start=(dt2 == 0), stop=(dt2 == NDT - 1))
                    return last
                S.emit("tensor", mwo, waits=wc("tensor", [pan_rdy, otl, x1_parks]), inc="sPE")
                mwd = S.after("sPE")
                S.emit("scalar", lambda h, buf=buf: nc.scalar.copy(out=x1st[:], in_=ps_wo[buf][:]),
                       waits=wc("scalar", [mwd]), inc="sSC")
                evd = S.after("sSC")
                S.dma("sync", lambda h, i=i, oq=oq: h.dma_start(
                          out=x1_d[i][:, oq * 1024:(oq + 1) * 1024], in_=x1st[:]),
                      waits=[evd], inc="sST")
                x1_parks = S.after("sST")
                otw_guard = mwd
        c1_done = S.after("sST")
        ctxC1.close()

        ctxC2 = ExitStack()
        def SC(name, shape, dt):
            return ctxC2.enter_context(nc.sbuf_tensor(name, list(shape), dt))
        def PC(name, shape, dt):
            return ctxC2.enter_context(nc.psum_tensor(name, list(shape), dt))
        x1l = SC("x1l", [128, D], F32)
        hsr = SC("hsr", [128, D], F32)
        x2f = SC("x2f", [128, D], F32)
        x2b = SC("x2b", [128, D], BF16)
        x2T = SC("x2T", [128, NDT * 128], BF16)
        x3f = SC("x3f", [128, D], F32)
        outf = SC("outf", [128, D], F32)
        routb = SC("routb", [128, NDT * 8], BF16)
        acatb = SC("acatb", [128, NDT * 128], BF16)
        bwcb = SC("bwcb", [128, D], BF16)
        e8f = SC("e8f", [8, 128], F32)
        gatep = SC("gatep", [128, 128], F32)
        gT = SC("gT", [8, 128], F32)
        gevac = SC("gevac", [128, 128], F32)
        htil = SC("htil", [128, 128], BF16)
        lnp = SC("lnp", [128, 16], F32)
        ps_t2 = PC("ps_t2", [128, 1024], BF16)
        ps_rt = PC("ps_rt", [128, 512], F32)
        ps_h = PC("ps_h", [128, 512], F32)
        ps_ge = PC("ps_ge", [128, 512], F32)
        ps_gt = PC("ps_gt", [128, 512], F32)
        ps_ff = PC("ps_ff", [128, 1024], F32)
        def lp(j):
            return lnp[:, j:j + 1]

        S.dma("gpsimd", lambda h: h.dma_start(
                  out=routb[:, :].rearrange("p (dt e) -> p dt e", e=8),
                  in_=routeT_i.rearrange("(dt p) e -> p dt e", p=128)),
              waits=[c1_done], inc="sCA")
        S.dma("gpsimd", lambda h: h.dma_start(
                  out=acatb[:, :].rearrange("p (dt c) -> p dt c", c=128),
                  in_=AcatT_i.rearrange("(dt p) c -> p dt c", p=128)), inc="sCA")
        S.dma("gpsimd", lambda h: h.dma_start(out=bwcb[:], in_=BwcatT_i), inc="sCA")
        S.dma("sync", lambda h: h.dma_start(out=e8f[:], in_=E8_i), waits=[c1_done], inc="sLD")
        cld = S.after("sCA")
        cld2 = S.after("sLD")
        ve(lambda h: nc.vector.memset(gatep[:, :], 0.0), waits=[c1_done], inc="sVE")
        gz = S.after("sVE")

        prev_i_done = (None, 0)
        for i in range(8):
            S.dma("sync", lambda h, i=i: h.dma_start(out=x1l[:], in_=x1_d[i]),
                  waits=[prev_i_done], inc="sLD")
            S.dma("sync", lambda h, i=i: h.dma_start(out=hsr[:], in_=hs_own[i * 128:(i + 1) * 128, :]),
                  inc="sLD")
            lds = S.after("sLD")
            ve(lambda h: nc.vector.tensor_add(x1l[:], x1l[:], hsr[:]),
               waits=[lds, gz], inc="sVE")
            r1d = S.after("sVE")
            def st1(h):
                h.drain()
                nc.scalar.activation(out=x2f[:], in_=x1l[:], func=AF.Copy, accum_out=lp(0))
                h.drain()
                return nc.scalar.activation(out=x2f[:], in_=x1l[:], func=AF.Square, accum_out=lp(1))
            S.emit("scalar", st1, waits=wc("scalar", [r1d]) if first_c["scalar"] else [r1d], inc="sSC")
            st1d = S.after("sSC")
            def ln1t(h):
                nc.vector.tensor_scalar(lp(2), lp(0), 1.0 / D, None, op0=ALU.mult)
                h.drain()
                nc.vector.tensor_scalar(lp(3), lp(1), 1.0 / D, None, op0=ALU.mult)
                h.drain()
                nc.vector.tensor_mul(lp(4), lp(2), lp(2))
                h.drain()
                nc.vector.tensor_sub(lp(3), lp(3), lp(4))
                h.drain()
                return nc.vector.tensor_scalar(lp(3), lp(3), 1e-5, None, op0=ALU.add)
            ve(ln1t, waits=[st1d], inc="sVE")
            ln1td = S.after("sVE")
            def rst1(h):
                h.drain()
                nc.scalar.activation(lp(5), lp(3), AF.Ln)
                h.drain()
                return nc.scalar.activation(lp(6), lp(5), AF.Exp, scale=-0.5)
            S.emit("scalar", rst1, waits=[ln1td], inc="sSC")
            rst1d = S.after("sSC")
            def b2v(h):
                nc.vector.tensor_mul(lp(7), lp(2), lp(6))
                h.drain()
                return nc.vector.tensor_scalar(lp(7), lp(7), -1.0, None, op0=ALU.mult)
            ve(b2v, waits=[rst1d], inc="sVE")
            b2d = S.after("sVE")
            S.emit("scalar", lambda h: nc.scalar.activation(out=x2f[:], in_=x1l[:], func=AF.Identity,
                                                            bias=lp(7), scale=lp(6)),
                   waits=[b2d], inc="sSC")
            x2d = S.after("sSC")
            ve(lambda h: nc.vector.tensor_copy(out=x2b[:], in_=x2f[:]), waits=[x2d], inc="sVE")
            x2bd = S.after("sVE")
            for tg in range(4):
                def trx(h, tg=tg):
                    last = None
                    for j2 in range(8):
                        dt2 = tg * 8 + j2
                        last = nc.tensor.transpose(ps_t2[:, j2 * 128:(j2 + 1) * 128],
                                                   x2b[:, dt2 * 128:(dt2 + 1) * 128], ident_s[:])
                    return last
                S.emit("tensor", trx, waits=wc("tensor", [x2bd, S.after("sSC")]), inc="sPE")
                trxd = S.after("sPE")
                S.emit("scalar", lambda h, tg=tg: nc.scalar.copy(
                           out=x2T[:, tg * 1024:(tg + 1) * 1024], in_=ps_t2[:]),
                       waits=[trxd], inc="sSC")
            x2td = S.after("sSC")
            def mrt(h):
                last = None
                for dt2 in range(NDT):
                    last = nc.tensor.matmul(ps_rt[:, 0:8], x2T[:, dt2 * 128:(dt2 + 1) * 128],
                                            routb[:, dt2 * 8:(dt2 + 1) * 8],
                                            start=(dt2 == 0), stop=(dt2 == NDT - 1))
                return last
            S.emit("tensor", mrt, waits=[x2td, cld, S.after("sVE")], inc="sPE")
            mrtd = S.after("sPE")
            def top1(h):
                nc.vector.tensor_copy(gatep[:, 0:8], ps_rt[:, 0:8])
                h.drain()
                nc.vector.tensor_reduce(lp(8), gatep[:, 0:8], axis=AX.X, op=ALU.max)
                h.drain()
                nc.vector.tensor_scalar(gatep[:, 8:16], gatep[:, 0:8], lp(8), None, op0=ALU.is_ge)
                h.drain()
                nc.vector.memset(gatep[:, 16:24], 0.0)
                h.drain()
                nc.vector.tensor_copy(gatep[:, 17:24], gatep[:, 8:15])
                h.drain()
                nc.vector.tensor_copy(gatep[:, 24:32], gatep[:, 16:24])
                h.drain()
                nc.vector.tensor_add(gatep[:, 26:32], gatep[:, 26:32], gatep[:, 16:22])
                h.drain()
                nc.vector.tensor_copy(gatep[:, 32:40], gatep[:, 24:32])
                h.drain()
                nc.vector.tensor_add(gatep[:, 36:40], gatep[:, 36:40], gatep[:, 24:28])
                h.drain()
                nc.vector.tensor_scalar(gatep[:, 40:48], gatep[:, 32:40], 0.5, None, op0=ALU.is_lt)
                h.drain()
                nc.vector.tensor_mul(gatep[:, 48:56], gatep[:, 8:16], gatep[:, 40:48])
                h.drain()
                nc.vector.tensor_scalar(gatep[:, 56:64], gatep[:, 48:56], lp(8), None, op0=ALU.mult)
                h.drain()
                return nc.vector.tensor_copy(gatep[:, 0:8], gatep[:, 56:64])
            ve(top1, waits=[mrtd], inc="sVE")
            t1d = S.after("sVE")
            def mh(h):
                last = None
                for dt2 in range(NDT):
                    last = nc.tensor.matmul(ps_h[:, 0:128], acatb[:, dt2 * 128:(dt2 + 1) * 128],
                                            x2T[:, dt2 * 128:(dt2 + 1) * 128],
                                            start=(dt2 == 0), stop=(dt2 == NDT - 1))
                return last
            S.emit("tensor", mh, waits=[x2td], inc="sPE")
            mhd = S.after("sPE")
            def mgt(h):
                return nc.tensor.transpose(ps_gt[:, 0:128], gatep[:, 0:128], identf_s[:])
            S.emit("tensor", mgt, waits=[t1d], inc="sPE")
            mgtd = S.after("sPE")
            S.emit("scalar", lambda h: nc.scalar.copy(out=gT[:], in_=ps_gt[0:8, 0:128]),
                   waits=[mgtd], inc="sSC")
            gtd = S.after("sSC")
            def mge(h):
                return nc.tensor.matmul(ps_ge[:, 0:128], e8f[:], gT[:], start=True, stop=True)
            S.emit("tensor", mge, waits=[gtd, cld2], inc="sPE")
            mged = S.after("sPE")
            S.emit("scalar", lambda h: nc.scalar.copy(out=gevac[:], in_=ps_ge[:, 0:128]),
                   waits=[mged], inc="sSC")
            gevd = S.after("sSC")
            ve(lambda h: nc.vector.tensor_mul(htil[:], ps_h[:, 0:128], gevac[:]),
               waits=[mhd, gevd], inc="sVE")
            htd = S.after("sVE")
            for oq2 in range(4):
                def mff(h, oq2=oq2):
                    nc.tensor.matmul(ps_ff[:, 0:512], htil[:],
                                     bwcb[:, oq2 * 1024: oq2 * 1024 + 512], start=True, stop=True)
                    return nc.tensor.matmul(ps_ff[:, 512:1024], htil[:],
                                            bwcb[:, oq2 * 1024 + 512: (oq2 + 1) * 1024],
                                            start=True, stop=True)
                S.emit("tensor", mff, waits=[htd] + ([S.after("sVE")] if oq2 else []), inc="sPE")
                mffd = S.after("sPE")
                def rx3(h, oq2=oq2):
                    return nc.vector.tensor_add(x3f[:, oq2 * 1024:(oq2 + 1) * 1024],
                                                ps_ff[:, 0:1024], x2f[:, oq2 * 1024:(oq2 + 1) * 1024])
                ve(rx3, waits=[mffd], inc="sVE")
            x3d = S.after("sVE")
            def st2(h):
                h.drain()
                nc.scalar.activation(out=outf[:], in_=x3f[:], func=AF.Copy, accum_out=lp(0))
                h.drain()
                return nc.scalar.activation(out=outf[:], in_=x3f[:], func=AF.Square, accum_out=lp(1))
            S.emit("scalar", st2, waits=[x3d], inc="sSC")
            st2d = S.after("sSC")
            ve(ln1t, waits=[st2d], inc="sVE")
            ln2td = S.after("sVE")
            S.emit("scalar", rst1, waits=[ln2td], inc="sSC")
            rst2d = S.after("sSC")
            ve(b2v, waits=[rst2d], inc="sVE")
            b2d2 = S.after("sVE")
            S.emit("scalar", lambda h: nc.scalar.activation(out=outf[:], in_=x3f[:], func=AF.Identity,
                                                            bias=lp(7), scale=lp(6)),
                   waits=[b2d2], inc="sSC")
            od = S.after("sSC")
            S.dma("sync", lambda h, i=i: h.dma_start(out=out_t[i * 128:(i + 1) * 128, :], in_=outf[:]),
                  waits=[od], inc="sST")
            prev_i_done = S.after("sST")
        S.emit("sync", lambda h: h.nop(), waits=[S.after("sST")])
        ctxC2.close()

    if dev_phase in ("A", "Araw"):
        S.dma("sync", lambda h: h.dma_start(out=dev["dbg_qrt"], in_=qrt_d), waits=[phaseA_st], inc="sST")
        if dev_phase == "A":
            S.dma("sync", lambda h: h.dma_start(out=dev["dbg_krt"], in_=krt_d), inc="sST")
            S.dma("sync", lambda h: h.dma_start(out=dev["dbg_v"], in_=v_d), inc="sST")
        S.emit("vector", lambda h: nc.vector.memset(qk_tmp[:, :], 0.0), inc="sVE")
        zr = S.after("sVE")
        ochunks = out_t.rearrange("(c p) d -> c p d", p=128)
        for c in range(8):
            for hf in range(2):
                S.dma("sync", lambda h, c=c, hf=hf: h.dma_start(
                          out=ochunks[c][:, hf * 2048:(hf + 1) * 2048], in_=qk_tmp[:, :]),
                      waits=[zr], inc="sST")
        S.emit("sync", lambda h: h.nop(), waits=[S.after("sST")])

    if dev_phase == "B":
        ctxB.close()
    if dev_phase in ("A", "Araw"):
        ctxA.close()

    with nc.Block() as blk:
        @blk.sync
        def _(h):
            for op in S.ops["sync"]:
                op(h)
        @blk.gpsimd
        def _(h):
            for op in S.ops["gpsimd"]:
                op(h)
        @blk.vector
        def _(h):
            for op in S.ops["vector"]:
                op(h)
        @blk.scalar
        def _(h):
            for op in S.ops["scalar"]:
                op(h)
        @blk.tensor
        def _(h):
            for op in S.ops["tensor"]:
                op(h)

    ctx.close()
    return nc


_PROG_CACHE = {}


def kernel(**inputs):
    wts = host_weights(inputs)
    hs_all = np.asarray(inputs["L0_attn_hidden_states"], dtype=np.float32)
    if "prog" not in _PROG_CACHE:
        _PROG_CACHE["prog"] = build_program(dev_phase="full")
        _PROG_CACHE["consts"] = {p: host_constants(p) for p in (0, 1)}
    nc = _PROG_CACHE["prog"]
    in_maps = []
    for c in range(8):
        b, p = c // 2, c % 2
        owncols = np.concatenate([np.arange(qb * 128, (qb + 1) * 128) for qb in OWN[p]])
        hsb = np.ascontiguousarray(hs_all[b])
        in_maps.append(dict(hs=hsb, hs_own=np.ascontiguousarray(hsb[owncols]),
                            **wts, **_PROG_CACHE["consts"][p]))
    res = run_bass_kernel_spmd(nc, in_maps, core_ids=list(range(8)))
    out = np.empty((B, SQ, D), np.float32)
    for c in range(8):
        b, p = c // 2, c % 2
        r = np.asarray(res.results[c]["out"])
        for i, qb in enumerate(OWN[p]):
            out[b, qb * 128:(qb + 1) * 128, :] = r[i * 128:(i + 1) * 128, :]
    return out


# version 05 (final)
# speedup vs baseline: 1.1452x; 1.0676x over previous
import math
import numpy as np
import ml_dtypes
import concourse.bass as bass
import concourse.mybir as mybir
from concourse.bass_utils import run_bass_kernel_spmd
from contextlib import ExitStack

F32 = mybir.dt.float32
BF16 = mybir.dt.bfloat16
AF = mybir.ActivationFunctionType
ALU = mybir.AluOpType
AX = mybir.AxisListType
BF = ml_dtypes.bfloat16

B, SQ, D, H, HKV, HD = 4, 2048, 4096, 32, 8, 128
NQB, NDT = SQ // 128, D // 128
NREM = 205
NEG30 = -1.0e30
SQRT_HD = float(np.float32(np.sqrt(np.float32(HD))))
OWN = {0: [0, 3, 4, 7, 8, 11, 12, 15], 1: [1, 2, 5, 6, 9, 10, 13, 14]}
PAIR_OFF = {0: (0, 384), 1: (128, 256)}
N_IT = 8
TARGET = 200.5

ENGS = ("sync", "gpsimd", "vector", "scalar", "tensor")


class Sched:
    def __init__(self, nc, ctx):
        self.nc = nc
        self.ops = {e: [] for e in ENGS}
        self.ctr = {}
        self.ctx = ctx

    def sem(self, name):
        s = self.ctx.enter_context(self.nc.semaphore(name))
        self.ctr[name] = [s, 0]

    def val(self, name):
        return self.ctr[name][1]

    def after(self, name):
        return (name, self.ctr[name][1])

    def emit(self, eng, fn, waits=(), inc=None, n=1):
        if not hasattr(self, "log"):
            self.log = []
        self.log.append((eng, len(self.ops[eng]), [w for w in waits if w is not None],
                         inc, n, getattr(fn, "__qualname__", str(fn))))
        if inc is not None:
            self.ctr[inc][1] += n
        sem = self.ctr[inc][0] if inc is not None else None
        wl = [(self.ctr[w][0], v) for (w, v) in waits if v > 0]
        def op(h, fn=fn, wl=wl, sem=sem, n=n):
            for (s, v) in wl:
                h.wait_ge(s, v)
            ins = fn(h)
            if sem is not None:
                ins.then_inc(sem, n)
        self.ops[eng].append(op)

    def dma(self, eng, fn, waits=(), inc=None):
        self.emit(eng, fn, waits, inc, n=16)


def host_constants(parity):
    inv_freq = (1.0 / (np.float32(10000.0) ** (np.arange(0, HD, 2, dtype=np.float32) / np.float32(HD)))).astype(np.float32)
    t = np.arange(SQ, dtype=np.float32)
    freqs = np.outer(t, inv_freq).astype(np.float32)
    emb = np.concatenate([freqs, freqs], axis=-1)
    cos = np.cos(emb).astype(np.float32)
    sin = np.sin(emb).astype(np.float32)
    cosT = np.ascontiguousarray(cos.T)
    sswapT = np.ascontiguousarray(sin.T).copy()
    sswapT[:64, :] = -sswapT[:64, :]
    own = OWN[parity]
    owncols = np.concatenate([np.arange(qb * 128, (qb + 1) * 128) for qb in own])
    cosT_own = np.ascontiguousarray(cosT[:, owncols])
    sswapT_own = np.ascontiguousarray(sswapT[:, owncols])
    ident = np.eye(128, dtype=np.float32)
    causal_add = np.where(np.arange(128)[:, None] >= np.arange(128)[None, :], 0.0,
                          np.float32(NEG30)).astype(np.float32)
    valid = np.stack([qb * 128 + np.arange(128) + 1 for qb in own], axis=1).astype(np.float32)
    sel = (valid > NREM).astype(np.float32)
    from statistics import NormalDist
    nd = NormalDist()
    frac = np.clip(1.0 - NREM / np.maximum(valid, NREM + 1.0), 1e-6, 1 - 1e-6)
    zq = np.vectorize(nd.inv_cdf)(frac).astype(np.float32)
    phi = (np.exp(-0.5 * zq * zq) / np.sqrt(2 * np.pi)).astype(np.float32)
    sl0 = (1.0 / np.maximum(valid * phi, 1.0)).astype(np.float32)
    iota8 = np.tile(np.arange(8, dtype=np.float32)[None, :], (128, 1)).astype(np.float32)
    maskin = np.zeros((8, 128, 256), np.float32)
    for i, qb in enumerate(own):
        kpad = 256 * (i + 1)
        kcols = np.arange(kpad - 256, kpad)
        qpos = qb * 128 + np.arange(128)
        maskin[i] = np.where(kcols[None, :] <= qpos[:, None], 0.0, np.float32(NEG30))
    selb = np.repeat(sel.astype(np.int32)[:, :, None], 8, axis=2).reshape(128, 64)
    return dict(cosT=cosT, sswapT=sswapT, cosT_own=cosT_own, sswapT_own=sswapT_own,
                identc=ident, validc=valid, selc=sel, selbc=selb, zqc=zq, sl0c=sl0,
                iota8c=iota8, onesc=np.ones((1, 128), np.float32),
                sqhdc=np.full((128, 1), SQRT_HD, np.float32),
                maskin=maskin.reshape(8, 128 * 256))


def host_weights(inputs):
    WqT = np.ascontiguousarray(np.asarray(inputs["L0_attn_Wq"]).T)
    WkT = np.ascontiguousarray(np.asarray(inputs["L0_attn_Wk"]).T)
    WvT = np.ascontiguousarray(np.asarray(inputs["L0_attn_Wv"]).T)
    WoT = np.ascontiguousarray(np.asarray(inputs["L0_attn_Wo"]).T)
    routeT = np.ascontiguousarray(np.asarray(inputs["L0_ffn_route_w"]).T)
    A = np.asarray(inputs["L0_ffn_A"])
    AcatT = np.ascontiguousarray(A.reshape(128, D).T)
    Bw = np.asarray(inputs["L0_ffn_Bw"])
    BwcatT = np.ascontiguousarray(Bw.transpose(0, 2, 1).reshape(128, D))
    E8 = np.zeros((8, 128), dtype=np.float32)
    for e in range(8):
        E8[e, e * 16:(e + 1) * 16] = 1.0
    return dict(WqT=WqT, WkT=WkT, WvT=WvT, WoT=WoT, routeT=routeT,
                AcatT=AcatT, BwcatT=BwcatT, E8=E8)


def build_program(dev_phase="full"):
    own = OWN[0]
    nc = bass.Bass("TRN2", target_bir_lowering=False)

    def DI(name, shape, dt=F32):
        return nc.dram_tensor(name, list(shape), dt, kind="ExternalInput").ap()
    def DS(name, shape, dt):
        return nc.dram_tensor(name, list(shape), dt, kind="Internal").ap()

    hs = DI("hs", [SQ, D])
    hs_own = DI("hs_own", [1024, D])
    WqT = DI("WqT", [D, D]); WkT_i = DI("WkT", [D, HKV * HD]); WvT_i = DI("WvT", [D, HKV * HD])
    WoT_i = DI("WoT", [D, D])
    routeT_i = DI("routeT", [D, 8]); AcatT_i = DI("AcatT", [D, 128]); BwcatT_i = DI("BwcatT", [128, D])
    E8_i = DI("E8", [8, 128])
    cosT_i = DI("cosT", [HD, SQ]); sswapT_i = DI("sswapT", [HD, SQ])
    cosTo_i = DI("cosT_own", [HD, 1024]); sswapTo_i = DI("sswapT_own", [HD, 1024])
    ident_i = DI("identc", [128, 128])
    maskin_i = DI("maskin", [8, 128 * 256])
    valid_i = DI("validc", [128, 8]); sel_i = DI("selc", [128, 8])
    selb_i = nc.dram_tensor("selbc", [128, 64], mybir.dt.int32, kind="ExternalInput").ap()
    zq_i = DI("zqc", [128, 8]); sl0_i = DI("sl0c", [128, 8])
    ones_i = DI("onesc", [1, 128]); iota8_i = DI("iota8c", [128, 8])
    sqhd_i = DI("sqhdc", [128, 1])

    out_t = nc.dram_tensor("out", [1024, D], F32, kind="ExternalOutput").ap()

    qrt_d = DS("qrt_d", [H, HD, 1024], BF16)
    krt_d = DS("krt_d", [HKV, HD, SQ], BF16)
    v_d = DS("v_d", [HKV, NQB, 128, HD], BF16)
    ot_d = DS("ot_d", [H, 8, HD, 128], BF16)

    dev = {}
    if dev_phase == "B":
        dev["dbg_ot"] = nc.dram_tensor("dbg_ot", [H, 8, HD, 128], BF16, kind="ExternalOutput").ap()
        dev["dbg_stp"] = nc.dram_tensor("dbg_stp", [32, 128, 28 * 8], F32, kind="ExternalOutput").ap()
        dev["dbg_m8"] = nc.dram_tensor("dbg_m8", [32, 128, 128], F32, kind="ExternalOutput").ap()
        dev["dbg_sg"] = nc.dram_tensor("dbg_sg", [128, 8 * 2048], F32, kind="ExternalOutput").ap()
    if dev_phase in ("A", "Araw"):
        dev["dbg_qrt"] = nc.dram_tensor("dbg_qrt", [H, HD, 1024], BF16, kind="ExternalOutput").ap()
        if dev_phase == "A":
            dev["dbg_krt"] = nc.dram_tensor("dbg_krt", [HKV, HD, SQ], BF16, kind="ExternalOutput").ap()
            dev["dbg_v"] = nc.dram_tensor("dbg_v", [HKV, NQB, 128, HD], BF16, kind="ExternalOutput").ap()

    ctx = ExitStack()
    def SB(name, shape, dt):
        return ctx.enter_context(nc.sbuf_tensor(name, list(shape), dt))
    def PS(name, shape, dt):
        return ctx.enter_context(nc.psum_tensor(name, list(shape), dt))

    S = Sched(nc, ctx)
    for s in ("sLD", "sCA", "sW", "sPE", "sVE", "sSC", "sST", "sGP", "sEV", "sSG", "sAV", "sPF"):
        S.sem(s)

    ident_s = SB("ident_s", [128, 128], BF16)
    maskin_s = SB("maskin_s", [128, 8 * 256], F32)
    valid_s = SB("valid_s", [128, 8], F32)
    sel_s = SB("sel_s", [128, 8], F32)
    zq_s = SB("zq_s", [128, 8], F32)
    sl0_s = SB("sl0_s", [128, 8], F32)
    ones_s = SB("ones_s", [1, 128], F32)
    iota8_s = SB("iota8_s", [128, 8], F32)
    selb_s = SB("selb_s", [128, 64], mybir.dt.int32)
    sqhd_s = SB("sqhd_s", [128, 1], F32)
    sinv_s = SB("sinv_s", [128, 1], F32)
    identf_s = SB("identf_s", [128, 128], F32)

    ctxA = ExitStack()
    def SA(name, shape, dt):
        return ctxA.enter_context(nc.sbuf_tensor(name, list(shape), dt))
    def PA(name, shape, dt):
        return ctxA.enter_context(nc.psum_tensor(name, list(shape), dt))
    wpan = SA("wpan", [128, 2 * NDT * 128], BF16)
    hs_bf = SA("hs_bf", [128, 2 * D], BF16)
    qk_tmp = SA("qk_tmp", [128, 2048], F32)
    qk_tmp2 = SA("qk_tmp2", [128, 2048], F32)
    qk_stage = SA("qk_stage", [128, 2048], BF16)
    v_stage = qk_stage
    vn_stage = SA("vn_stage", [128, 2048], BF16)

    ps_pj2 = [PA("ps_pj0", [128, 1024], F32), PA("ps_pj1", [128, 1024], F32)]
    ps_tr = PA("ps_tr", [128, 512], BF16)
    cosT_s = SA("cosT_s", [HD, SQ], F32)
    sswapT_s = SA("sswapT_s", [HD, SQ], F32)
    cosTo_s = SA("cosTo_s", [HD, 1024], F32)
    sswapTo_s = SA("sswapTo_s", [HD, 1024], F32)

    for (dname, src) in (("cosT_s", cosT_i), ("sswapT_s", sswapT_i), ("cosTo_s", cosTo_i),
                         ("sswapTo_s", sswapTo_i), ("valid_s", valid_i),
                         ("sel_s", sel_i), ("zq_s", zq_i), ("sl0_s", sl0_i), ("ones_s", ones_i),
                         ("iota8_s", iota8_i), ("sqhd_s", sqhd_i), ("identf_s", ident_i)):
        dst = {"cosT_s": cosT_s, "sswapT_s": sswapT_s, "cosTo_s": cosTo_s,
               "sswapTo_s": sswapTo_s, "valid_s": valid_s,
               "sel_s": sel_s, "zq_s": zq_s, "sl0_s": sl0_s, "ones_s": ones_s,
               "iota8_s": iota8_s, "sqhd_s": sqhd_s, "identf_s": identf_s}[dname]
        S.dma("sync", lambda h, d=dst, s=src: h.dma_start(out=d[:], in_=s), inc="sLD")
    S.dma("gpsimd", lambda h: h.dma_start(out=ident_s[:], in_=ident_i), inc="sCA")
    S.dma("sync", lambda h: h.dma_start(out=selb_s[:], in_=selb_i), inc="sLD")
    S.dma("sync", lambda h: h.dma_start(
              out=maskin_s.rearrange("p (i c) -> p i c", c=256),
              in_=maskin_i.rearrange("i (p c) -> p i c", p=128)), inc="sLD")
    consts_ld = S.after("sLD")
    consts_ca = S.after("sCA")
    S.emit("vector", lambda h: nc.vector.reciprocal(sinv_s[:], sqhd_s[:]),
           waits=[consts_ld], inc="sVE")
    sinv_ready = S.after("sVE")

    def build_hsT(src_ap, ntiles, dst, seg_w, tag, pre=(None, 0)):
        evg = (None, 0)
        for g in range(ntiles // 2):
            for i2 in range(2):
                tt = g * 2 + i2
                w = ([evg] if g > 0 and i2 == 0 else []) + ([pre] if g == 0 and i2 == 0 else [])
                S.dma("gpsimd",
                      lambda h, i2=i2, tt=tt: h.dma_start(out=hs_bf[:, i2 * D:(i2 + 1) * D],
                                                          in_=src_ap[tt * 128:(tt + 1) * 128, :]),
                      waits=w, inc="sCA")
            ca_done = S.after("sCA")
            for dt in range(NDT):
                for i2 in range(2):
                    w = [ca_done, S.after("sSC")] if i2 == 0 else []
                    S.emit("tensor",
                           lambda h, i2=i2, dt=dt: nc.tensor.transpose(
                               ps_tr[:, i2 * 128:(i2 + 1) * 128],
                               hs_bf[:, i2 * D + dt * 128: i2 * D + (dt + 1) * 128], ident_s[:]),
                           waits=w, inc=("sPE" if i2 == 1 else None))
                S.emit("scalar",
                       lambda h, dt=dt, g=g, dst=dst, seg_w=seg_w: nc.scalar.copy(
                           out=dst[:, dt * seg_w + g * 256: dt * seg_w + (g + 1) * 256],
                           in_=ps_tr[:, 0:256]),
                       waits=[S.after("sPE")], inc="sSC")
            evg = S.after("sSC")
        return S.after("sSC")

    ctxAo = ExitStack()
    hsT_own = ctxAo.enter_context(nc.sbuf_tensor("hsT_own", [128, NDT * 1024], BF16))
    hsTo_done = build_hsT(hs_own, 8, hsT_own, 1024, "own")

    def wsrc(kind, idx):
        if kind == "q":
            return WqT[:, idx * 128:(idx + 1) * 128]
        if kind == "k":
            return WkT_i[:, idx * 128:(idx + 1) * 128]
        return WvT_i[:, idx * 128:(idx + 1) * 128]

    pj_free = {0: (None, 0), 1: (None, 0)}
    panel_hist = []

    def proj_group(otiles, hsT_src, seg_w, first_ready, t_off=0):
        for jj, (kind, idx) in enumerate(otiles):
            slot = jj % 2
            S.dma("gpsimd",
                  lambda h, slot=slot, kind=kind, idx=idx: h.dma_start(
                      out=wpan[:, slot * NDT * 128:(slot + 1) * NDT * 128]
                          .rearrange("p (dt c) -> p dt c", c=128),
                      in_=wsrc(kind, idx).rearrange("(dt p) c -> p dt c", p=128)),
                  waits=([panel_hist[-2]] if len(panel_hist) >= 2 else []),
                  inc="sW")
            w_ready = S.after("sW")
            first_waits = [w_ready, first_ready, pj_free[slot]]
            nmm = seg_w // 512
            pj = ps_pj2[slot]
            for dt in range(NDT):
                pan = wpan[:, slot * NDT * 128 + dt * 128: slot * NDT * 128 + (dt + 1) * 128]
                st, fin = (dt == 0), (dt == NDT - 1)
                def mmp(h, pan=pan, dt=dt, st=st, fin=fin, nmm=nmm, seg_w=seg_w,
                        hsT_src=hsT_src, pj=pj):
                    last = None
                    for c in range(nmm):
                        last = nc.tensor.matmul(pj[:, c * 512:(c + 1) * 512], pan,
                                                hsT_src[:, dt * seg_w + c * 512: dt * seg_w + (c + 1) * 512],
                                                start=st, stop=fin)
                    return last
                S.emit("tensor", mmp, waits=(first_waits if dt == 0 else []),
                       inc=("sPE" if fin else None))
            panel_hist.append(S.after("sPE"))
            mm_done = S.after("sPE")

            if kind in ("q", "k"):
                W = seg_w
                ct = cosTo_s if kind == "q" else cosT_s
                sw = sswapTo_s if kind == "q" else sswapT_s
                S.emit("vector", lambda h, W=W, ct=ct, t_off=t_off, pj=pj: nc.vector.tensor_mul(
                           qk_tmp[:, :W], pj[:, :W], ct[:, t_off:t_off + W]),
                       waits=[mm_done, sinv_ready])
                S.emit("vector", lambda h, W=W, sw=sw, t_off=t_off, pj=pj: nc.vector.tensor_mul(
                           qk_tmp2[0:64, :W], pj[64:128, :W], sw[0:64, t_off:t_off + W]))
                S.emit("vector", lambda h, W=W, sw=sw, t_off=t_off, pj=pj: nc.vector.tensor_mul(
                           qk_tmp2[64:128, :W], pj[0:64, :W], sw[64:128, t_off:t_off + W]))
                def rope_add(h, W=W):
                    h.drain()
                    return nc.vector.tensor_add(qk_stage[:, :W], qk_tmp[:, :W], qk_tmp2[:, :W])
                S.emit("vector", rope_add, inc="sVE")
                pj_free[slot] = S.after("sVE")
                rope_done = S.after("sVE")
                dst = qrt_d[idx] if kind == "q" else krt_d[idx][:, t_off:t_off + W]
                S.dma("sync", lambda h, dst=dst, W=W: h.dma_start(out=dst, in_=qk_stage[:, :W]),
                      waits=[rope_done], inc="sST")
                park = S.after("sST")
                S.emit("vector", lambda h: nc.vector.engine_nop(), waits=[park])
            else:
                S.emit("scalar", lambda h, pj=pj, seg_w=seg_w: nc.scalar.copy(
                           out=v_stage[:, 0:seg_w], in_=pj[:, 0:seg_w]),
                       waits=[mm_done, S.after("sST")], inc="sSC")
                pj_free[slot] = S.after("sSC")
                ev_done = S.after("sSC")
                nkt4 = seg_w // 512
                for ktg in range(nkt4):
                    for i2 in range(4):
                        kt = ktg * 4 + i2
                        w = [ev_done, S.after("sSC")] if i2 == 0 else []
                        S.emit("tensor",
                               lambda h, kt=kt, i2=i2: nc.tensor.transpose(
                                   ps_tr[:, i2 * 128:(i2 + 1) * 128],
                                   v_stage[:, kt * 128:(kt + 1) * 128], ident_s[:]),
                               waits=w, inc=("sPE" if i2 == 3 else None))
                    S.emit("scalar",
                           lambda h, ktg=ktg: nc.scalar.copy(
                               out=vn_stage[:, ktg * 512:(ktg + 1) * 512], in_=ps_tr[:]),
                           waits=[S.after("sPE")], inc="sSC")
                vt_done = S.after("sSC")
                kt0 = t_off // 128
                S.dma("sync", lambda h, idx=idx, kt0=kt0, seg_w=seg_w: h.dma_start(
                          out=v_d[idx][kt0:kt0 + seg_w // 128].rearrange("kt p c -> p kt c"),
                          in_=vn_stage[:, 0:seg_w].rearrange("p (kt c) -> p kt c", c=128)),
                      waits=[vt_done], inc="sST")
                park = S.after("sST")
                S.emit("scalar", lambda h: nc.scalar.copy(out=v_stage[0:1, 0:1], in_=v_stage[0:1, 0:1]),
                       waits=[park])

    proj_group([("q", i) for i in range(32)], hsT_own, 1024, hsTo_done)
    ctxAo.close()
    hsT = ctxA.enter_context(nc.sbuf_tensor("hsT", [128, NDT * 1024], BF16))
    for half in range(2):
        hsT_h_done = build_hsT(hs[half * 1024:(half + 1) * 1024, :], 8, hsT, 1024, "full%d" % half,
                               pre=S.after("sPE"))
        proj_group([("k", i) for i in range(8)] + [("v", i) for i in range(8)],
                   hsT, 1024, hsT_h_done, t_off=half * 1024)

    phaseA_st = S.after("sST")

    if dev_phase in ("B", "full"):
        ctxA.close()
        ctxB = ExitStack()
        def SBX(name, shape, dt):
            return ctxB.enter_context(nc.sbuf_tensor(name, list(shape), dt))
        def PSX(name, shape, dt):
            return ctxB.enter_context(nc.psum_tensor(name, list(shape), dt))
        SgB = SBX("SgB", [128, 8 * 2048], F32)
        SgS = SBX("SgS", [128, 8 * 1024], F32)
        Ppre2 = SBX("Ppre2", [128, 2 * 2048], F32)
        Bscr = SBX("Bscr", [128, 2048], F32)
        Ascr = SBX("Ascr", [128, 2048], F32)
        Tbs = SBX("Tbs", [128, 2048], BF16)
        Tas = SBX("Tas", [128, 2048], BF16)
        junkb = SBX("junkb", [128, 2048], BF16)
        Pfin2 = SBX("Pfin2", [128, 2 * 2048], BF16)
        PTb = SBX("PTb", [128, 16 * 512], BF16)
        krt2 = SBX("krt2", [128, 2 * 2048], BF16)
        v2 = SBX("v2", [128, 2 * 2 * 2048], BF16)
        qrt2 = SBX("qrt2", [128, 8 * 128], BF16)
        NPL = 28
        stp = SBX("stp", [128, 2 * NPL * 8], F32)
        NPI = 6
        stpi = SBX("stpi", [128, 2 * NPI * 8], mybir.dt.int32)
        W64 = SBX("W64", [128, 64], F32)
        Gb64 = SBX("Gb64", [128, 64], F32)
        Ga64 = SBX("Ga64", [128, 64], F32)
        m8b64 = SBX("m8b64", [128, 64], F32)
        m8a64 = SBX("m8a64", [128, 64], F32)
        negc8 = SBX("negc8", [128, 8], F32)
        ot_stage = SBX("ot_stage", [128, 2 * 512], BF16)
        sinvn_s = SBX("sinvn_s", [128, 1], F32)
        ps_S = [PSX("ps_S0", [128, 512], F32), PSX("ps_S1", [128, 512], F32)]
        ps_pt = [PSX("ps_pt0", [128, 1024], BF16), PSX("ps_pt1", [128, 1024], BF16)]
        ps_av = [PSX("ps_av0", [128, 512], F32), PSX("ps_av1", [128, 512], F32)]

        PL = {n: j for j, n in enumerate(
            ["MU", "SQS", "SD", "T", "SL", "NEGT", "TP", "CP", "C", "RMIN",
             "B0", "VMS", "VMB", "VMA", "SIDE", "IDX", "Z", "ZI", "TLO", "THI",
             "SELB", "X0", "X1", "X2", "X3", "X4", "PAD", "PAD2"])}
        PLI = {n: j for j, n in enumerate(["UP", "DN", "O", "SIDE", "SELB", "OK"])}
        def pli(n, sl):
            j = sl * NPI + PLI[n]
            return stpi[:, j * 8:(j + 1) * 8]
        def pl(n, sl):
            j = sl * NPL + PL[n]
            return stp[:, j * 8:(j + 1) * 8]
        def plc(n, sl, h8):
            j = sl * NPL + PL[n]
            return stp[:, j * 8 + h8: j * 8 + h8 + 1]

        ORDER = [4, 0, 5, 1, 6, 2, 7, 3]
        tiles = [(oct_, i) for oct_ in range(4) for i in ORDER]
        NT = len(tiles)
        def tinfo(k):
            oct_, i = tiles[k]
            return oct_, i, 256 * (i + 1), 2 * (i + 1), k % 2
        ev = [dict() for _ in range(NT)]
        mm_ev = []
        evac_ev = []
        N_CNT = 8
        NVH = 1
        INTERLEAVE_EXC = True
        PIPE_DEEP = True
        W_LO, W_HI, TGT = float(NREM - 8), float(NREM + 7), 204.5
        V = "vector"

        bar = phaseA_st
        first_b = {e: True for e in ENGS}
        def wb(eng, waits):
            w = [x for x in waits if x is not None]
            if first_b[eng]:
                w.append(bar)
                first_b[eng] = False
            return w
        def ve(fn, waits=(), inc=None):
            S.emit(V, fn, waits, inc)

        S.emit("vector", lambda h: nc.vector.tensor_scalar(
                   sinvn_s[:], sinv_s[:], -1.0, None, op0=ALU.mult),
               waits=wb("vector", [sinv_ready]), inc="sVE")
        S.emit("vector", lambda h: nc.vector.memset(stp[:, :], 0.0))
        S.emit("vector", lambda h: nc.vector.memset(negc8[:, :], -1e29), inc="sVE")
        stp_z = S.after("sVE")

        def Sg_of(k):
            return SgB if k % 2 == 0 else SgS
        def sgw(k):
            return 2048 if k % 2 == 0 else 1024

        def emit_load(u):
            oct_, i, kpad, nkt, sl = tinfo(u)
            kv0 = oct_ * 2
            w0 = [mm_ev[-1]] if mm_ev else []
            S.dma("sync", lambda h, oct_=oct_, i=i: h.dma_start(
                      out=qrt2[:, :].rearrange("p (hh q) -> p hh q", q=128),
                      in_=qrt_d[oct_ * 8:(oct_ + 1) * 8, :, i * 128:(i + 1) * 128]
                          .rearrange("hh p q -> p hh q")),
                  waits=wb("sync", w0), inc="sLD")
            for g in range(2):
                S.dma("sync", lambda h, g=g, kv=kv0 + g, kpad=kpad: h.dma_start(
                          out=krt2[:, g * 2048: g * 2048 + kpad],
                          in_=krt_d[kv][:, 0:kpad]),
                      inc="sLD")
            ev[u]["ld"] = S.after("sLD")

        def emit_load_v(u):
            oct_, i, kpad, nkt, sl = tinfo(u)
            kv0 = oct_ * 2
            w0 = [("sAV", u - 1)] if u >= 2 else []
            for g in range(2):
                S.dma("sync", lambda h, g=g, kv=kv0 + g, kpad=kpad, nkt=nkt, sl=sl: h.dma_start(
                          out=v2[:, sl * 4096 + g * 2048: sl * 4096 + g * 2048 + kpad]
                              .rearrange("p (kt c) -> p kt c", c=128),
                          in_=v_d[kv][0:nkt].rearrange("kt p c -> p kt c")),
                      waits=(w0 if g == 0 else []), inc="sLD")
            ev[u]["ldv"] = S.after("sLD")

        def emit_smm_head(u, h8):
            oct_, i, kpad, nkt, sl = tinfo(u)
            nch = (kpad + 511) // 512
            ev[u]["nch"] = nch
            g = h8 // 4
            for c in range(nch):
                cw = min(512, kpad - c * 512)
                gcc = len(mm_ev)
                pp = gcc % 2
                wv = []
                if h8 == 0 and c == 0:
                    wv = wb("tensor", [ev[u]["ld"]])
                if c == 0 and u >= 2:
                    wv.append(("sPF", (u - 2) * 8 + h8 + 1))
                if gcc >= 2:
                    wv.append(("sEV", gcc - 1))
                def mm(h, h8=h8, g=g, c=c, cw=cw, pp=pp):
                    return nc.tensor.matmul(
                        ps_S[pp][:, 0:cw],
                        qrt2[:, h8 * 128:(h8 + 1) * 128],
                        krt2[:, g * 2048 + c * 512: g * 2048 + c * 512 + cw],
                        start=True, stop=True)
                S.emit("tensor", mm, waits=wv, inc="sPE")
                mm_ev.append(S.after("sPE"))

        def build_evq_head(u, h8, gcc0):
            oct_, i, kpad, nkt, sl = tinfo(u)
            Sg = Sg_of(u); W = sgw(u)
            nch = ev[u]["nch"]
            q = ev[u].setdefault("evq", [])
            ev[u].setdefault("stat", {})
            gcc = gcc0
            for c in range(nch):
                cw = min(512, kpad - c * 512)
                def evf(h, h8=h8, c=c, cw=cw, pp=gcc % 2, Sg=Sg, W=W):
                    return nc.scalar.copy(
                        out=Sg[:, h8 * W + c * 512: h8 * W + c * 512 + cw],
                        in_=ps_S[pp][:, 0:cw])
                q.append((evf, [mm_ev[gcc]], "ev", gcc))
                gcc += 1
            def sqf(h, h8=h8, Sg=Sg, W=W, kpad=kpad, sl=sl):
                src = Sg[:, h8 * W: h8 * W + kpad].rearrange(
                    "p (a f) -> p a f", f=4)[:, :, 0:1]
                return nc.scalar.activation(out=junkb[:, 0:kpad // 4], in_=src,
                                            func=AF.Square,
                                            accum_out=plc("SQS", sl, h8))
            def muf(h, h8=h8, Sg=Sg, W=W, kpad=kpad, sl=sl):
                src = Sg[:, h8 * W: h8 * W + kpad].rearrange(
                    "p (a f) -> p a f", f=4)[:, :, 0:1]
                return nc.scalar.activation(out=junkb[:, 0:kpad // 4], in_=src,
                                            func=AF.Copy,
                                            accum_out=plc("MU", sl, h8))
            q.append((sqf, [], "st", h8))
            q.append((muf, [], "st", h8))

        def drain_evq(u, n=None):
            if u is None or u >= NT:
                return
            qq = ev[u].get("evq")
            if not qq:
                return
            cnt = len(qq) if n is None else min(n, len(qq))
            for _ in range(cnt):
                fn, w, kind, idx = qq.pop(0)
                if kind == "ev":
                    S.emit("scalar", fn, waits=wb("scalar", w), inc="sEV")
                else:
                    S.emit("scalar", fn, waits=wb("scalar", w), inc="sSC")
                    ev[u]["stat"][idx] = S.after("sSC")

        def emit_postload(u):
            oct_, i, kpad, nkt, sl = tinfo(u)
            Sg = Sg_of(u); W = sgw(u)
            if i == 0:
                for h8 in range(8):
                    def rmf(h, h8=h8, Sg=Sg, W=W, kpad=kpad, sl=sl):
                        return nc.vector.tensor_reduce(
                            plc("RMIN", sl, h8), Sg[:, h8 * W: h8 * W + kpad],
                            axis=AX.X, op=ALU.min)
                    ve(rmf, waits=[ev[u]["stat"][h8], stp_z] if h8 == 0 else [ev[u]["stat"][h8]],
                       inc=("sVE" if h8 == 7 else None))
                ev[u]["rmin"] = S.after("sVE")
            for h8 in range(8):
                w = [ev[u]["stat"][h8]]
                if i == 0 and h8 == 0:
                    w.append(ev[u]["rmin"])
                def dcf(h, h8=h8, Sg=Sg, W=W, kpad=kpad, i=i):
                    return nc.vector.tensor_tensor(
                        out=Sg[:, h8 * W + kpad - 256: h8 * W + kpad],
                        in0=Sg[:, h8 * W + kpad - 256: h8 * W + kpad],
                        in1=maskin_s[:, i * 256:(i + 1) * 256], op=ALU.add)
                S.emit("vector", dcf, waits=w,
                       inc=("sVE" if h8 == 7 else None))
            ev[u]["dca"] = S.after("sVE")
            n4 = float(kpad // 4)
            def init1(h, sl=sl, n4=n4):
                nc.gpsimd.tensor_scalar(pl("MU", sl), pl("MU", sl), 1.0 / n4, None, op0=ALU.mult)
                nc.gpsimd.tensor_scalar(pl("SQS", sl), pl("SQS", sl), 1.0 / n4, None, op0=ALU.mult)
                nc.gpsimd.tensor_tensor(pl("PAD2", sl), pl("MU", sl), pl("MU", sl), op=ALU.mult)
                nc.gpsimd.tensor_tensor(pl("SQS", sl), pl("SQS", sl), pl("PAD2", sl), op=ALU.subtract)
                return nc.gpsimd.tensor_scalar(pl("SQS", sl), pl("SQS", sl), 1e-6, None, op0=ALU.max)
            S.emit("gpsimd", init1,
                   waits=[ev[u]["stat"][h] for h in range(8)] + [stp_z], inc="sGP")
            i1d = S.after("sGP")
            def sd1(h, sl=sl):
                nc.scalar.activation(pl("SD", sl), pl("SQS", sl), AF.Ln)
                h.drain()
                return nc.scalar.activation(pl("SD", sl), pl("SD", sl), AF.Exp, scale=0.5)
            S.emit("scalar", sd1, waits=[i1d], inc="sSC")
            sdd = S.after("sSC")
            zqc = zq_s[:, i:i + 1]
            slc = sl0_s[:, i:i + 1]
            selc = sel_s[:, i:i + 1]
            def init2(h, sl=sl, zqc=zqc, slc=slc, selc=selc):
                ts, tt = nc.vector.tensor_scalar, nc.vector.tensor_tensor
                ts(pl("X0", sl), pl("SD", sl), zqc, None, op0=ALU.mult)
                ts(pl("SL", sl), pl("SD", sl), slc, None, op0=ALU.mult)
                tt(pl("T", sl), pl("MU", sl), pl("X0", sl), op=ALU.add)
                nc.vector.memset(pl("TLO", sl), -29000.0)
                nc.vector.memset(pl("THI", sl), 29000.0)
                return ts(pl("NEGT", sl), pl("T", sl), -1.0, None, op0=ALU.mult)
            ve(init2, waits=[sdd], inc="sVE")
            ev[u]["init"] = S.after("sVE")

        def emit_counts(k, r):
            oct_, i, kpad, nkt, sl = tinfo(k)
            Sg = Sg_of(k); W = sgw(k)
            w_first = [ev[k]["init"], ev[k]["dca"]] if r == 1 else [ev[k]["upd"]]
            def cv(h, Sg=Sg, kpad=kpad, sl=sl):
                return nc.vector.tensor_scalar(
                    Bscr[:, 0:kpad], Sg[:, 0:kpad], plc("T", sl, 0), None,
                    op0=ALU.is_gt, op1=ALU.add, accum_out=plc("C", sl, 0))
            ve(cv, waits=list(w_first), inc="sVE")
            ev[k]["cv"] = S.after("sVE")
            for h8 in range(1, 8):
                def cs(h, h8=h8, Sg=Sg, W=W, kpad=kpad, sl=sl):
                    return nc.scalar.activation(
                        out=junkb[:, 0:kpad], in_=Sg[:, h8 * W: h8 * W + kpad],
                        func=AF.Sign, bias=plc("NEGT", sl, h8), scale=1.0,
                        accum_out=plc("C", sl, h8))
                S.emit("scalar", cs, waits=(list(w_first) if h8 == NVH else []),
                       inc=("sSC" if h8 == 7 else None))
            ev[k]["cs"] = S.after("sSC")

        def emit_upd(k, r):
            oct_, i, kpad, nkt, sl = tinfo(k)
            def upd(h, sl=sl, r=r, kpad=kpad):
                P = lambda n: pl(n, sl)
                PI = lambda n: pli(n, sl)
                ts, tt, sel = nc.vector.tensor_scalar, nc.vector.tensor_tensor, nc.vector.select
                def pad():
                    nc.vector.tensor_copy(P("PAD"), P("C"))
                cs_cols = stp[:, (sl * NPL + PL["C"]) * 8 + NVH:(sl * NPL + PL["C"]) * 8 + 8]
                ts(cs_cols, cs_cols, float(kpad), 0.5, op0=ALU.add, op1=ALU.mult)
                tt(P("X2"), P("T"), P("TP"), op=ALU.subtract)
                ts(P("X0"), P("C"), TGT, None, op0=ALU.subtract)
                tt(P("X1"), P("C"), P("CP"), op=ALU.subtract)
                ts(PI("UP"), P("C"), float(NREM), None, op0=ALU.is_ge)
                ts(PI("DN"), P("C"), float(NREM), None, op0=ALU.is_lt)
                ts(P("X3"), P("X0"), W_LO - TGT, W_HI - TGT, op0=ALU.max, op1=ALU.min)
                if r >= 2:
                    nc.vector.reciprocal(P("X4"), P("X1"))
                else:
                    pad()
                tt(P("X0"), P("X0"), P("X3"), op=ALU.subtract)
                if r >= 2:
                    tt(P("X2"), P("X2"), P("X4"), op=ALU.mult)
                else:
                    pad()
                sel(P("TLO"), PI("UP"), P("T"), P("TLO"))
                if r >= 2:
                    tt(P("X1"), P("X1"), P("X1"), op=ALU.mult)
                    ts(P("X4"), P("X2"), -1.0, None, op0=ALU.mult)
                    ts(PI("OK"), P("X1"), 0.25, None, op0=ALU.is_ge)
                    tt(P("X2"), P("X2"), P("X4"), op=ALU.max)
                else:
                    pad(); pad(); pad(); pad()
                sel(P("THI"), PI("DN"), P("T"), P("THI"))
                if r >= 2:
                    ts(P("X2"), P("X2"), 1e-5, 1e3, op0=ALU.max, op1=ALU.min)
                else:
                    pad()
                nc.vector.tensor_copy(P("TP"), P("T"))
                nc.vector.tensor_copy(P("CP"), P("C"))
                if r >= 2:
                    sel(P("SL"), PI("OK"), P("X2"), P("SL"))
                else:
                    pad()
                tt(P("X4"), P("TLO"), P("THI"), op=ALU.add)
                tt(P("X0"), P("X0"), P("SL"), op=ALU.mult)
                ts(P("X4"), P("X4"), 0.5, None, op0=ALU.mult)
                tt(P("X3"), P("T"), P("X0"), op=ALU.add)
                pad()
                tt(P("X1"), P("X3"), P("THI"), op=ALU.is_gt)
                tt(P("X0"), P("X3"), P("TLO"), op=ALU.is_lt)
                pad()
                tt(P("X1"), P("X1"), P("X0"), op=ALU.max)
                pad()
                nc.vector.tensor_copy(PI("O"), P("X1"))
                pad()
                sel(P("T"), PI("O"), P("X4"), P("X3"))
                pad()
                return ts(P("NEGT"), P("T"), -1.0, None, op0=ALU.mult)
            ve(upd, waits=[ev[k]["cs"]], inc="sVE")
            ev[k]["upd"] = S.after("sVE")

        def emit_fin(k):
            oct_, i, kpad, nkt, sl = tinfo(k)
            def fin(h, sl=sl, i=i):
                P = lambda n: pl(n, sl)
                PI = lambda n: pli(n, sl)
                ts, tt, sel = nc.vector.tensor_scalar, nc.vector.tensor_tensor, nc.vector.select
                if i == 0:
                    sel(P("X0"), selb_s[:, i * 8:(i + 1) * 8], P("T"), P("RMIN"))
                    nc.vector.tensor_copy(P("PAD"), P("C"))
                    nc.vector.tensor_copy(P("T"), P("X0"))
                    nc.vector.tensor_copy(P("PAD"), P("C"))
                return ts(P("B0"), P("T"), sinvn_s[:, 0:1], None, op0=ALU.mult)
            ve(fin, waits=[ev[k]["cs"], ev[k]["cv"]], inc="sVE")
            ev[k]["fin"] = S.after("sVE")

        def emit_ext_head(k, h8):
            oct_, i, kpad, nkt, sl = tinfo(k)
            Sg = Sg_of(k); W = sgw(k)
            def exc(h, h8=h8, Sg=Sg, W=W, kpad=kpad, sl=sl):
                ts, stt = nc.vector.tensor_scalar, nc.vector.scalar_tensor_tensor
                sg = Sg[:, h8 * W: h8 * W + kpad]
                ts(Tbs[:, 0:kpad], sg, plc("T", sl, h8), -2e30, op0=ALU.is_gt, op1=ALU.mult)
                ts(Tas[:, 0:kpad], sg, plc("T", sl, h8), -2e30, op0=ALU.is_le, op1=ALU.mult)
                stt(out=Bscr[:, 0:kpad], in0=Tbs[:, 0:kpad], scalar=1.0,
                    in1=sg, op0=ALU.mult, op1=ALU.add)
                stt(out=Ascr[:, 0:kpad], in0=Tas[:, 0:kpad], scalar=1.0,
                    in1=sg, op0=ALU.mult, op1=ALU.subtract)
                nc.vector.max(out=m8b64[:, h8 * 8:(h8 + 1) * 8], in_=Bscr[:, 0:kpad])
                return nc.vector.max(out=m8a64[:, h8 * 8:(h8 + 1) * 8], in_=Ascr[:, 0:kpad])
            ve(exc, waits=([ev[k]["fin"]] if h8 == 0 else []))
            def vmh(h, h8=h8, sl=sl, i=i, kpad=kpad):
                P = lambda n: plc(n, sl, h8)
                PL8 = lambda n: pl(n, sl)
                ts, tt, sel = nc.vector.tensor_scalar, nc.vector.tensor_tensor, nc.vector.select
                def pad():
                    nc.vector.tensor_copy(pl("PAD", sl), iota8_s[:])
                if h8 >= NVH:
                    ts(P("C"), P("C"), float(kpad), 0.5, op0=ALU.add, op1=ALU.mult)
                    pad()
                ts(P("X4"), P("C"), -0.25, 8388608.0, op0=ALU.add, op1=ALU.add)
                pad()
                ts(P("C"), P("X4"), -8388608.0, None, op0=ALU.add)
                pad()
                ts(P("X0"), P("C"), TGT, None, op0=ALU.subtract)
                ts(pli("SIDE", sl)[:, h8:h8 + 1], P("C"), float(NREM), None, op0=ALU.is_ge)
                ts(P("X1"), P("X0"), -1.0, None, op0=ALU.mult)
                pad()
                tt(P("X0"), P("X0"), P("X1"), op=ALU.max)
                pad()
                ts(P("X0"), P("X0"), -0.5, None, op0=ALU.add)
                pad()
                ts(P("IDX"), P("X0"), 0.0, 7.0, op0=ALU.max, op1=ALU.min)
                pad()
                ts(W64[:, 0:8], iota8_s[:], P("IDX"), None, op0=ALU.is_equal)
                pad()
                tt(Gb64[:, 0:8], W64[:, 0:8], m8b64[:, h8 * 8:(h8 + 1) * 8], op=ALU.mult)
                tt(Ga64[:, 0:8], W64[:, 0:8], m8a64[:, h8 * 8:(h8 + 1) * 8], op=ALU.mult)
                nc.vector.tensor_reduce(P("VMB"), Gb64[:, 0:8], axis=AX.X, op=ALU.add)
                nc.vector.tensor_reduce(P("VMA"), Ga64[:, 0:8], axis=AX.X, op=ALU.add)
                pad()
                ts(P("X1"), P("VMA"), -1.0, None, op0=ALU.mult)
                pad()
                sel(P("VMS"), pli("SIDE", sl)[:, h8:h8 + 1], P("X1"), P("VMB"))
                if i == 0:
                    pad()
                    sel(P("X2"), selb_s[:, i * 8 + h8: i * 8 + h8 + 1], P("VMS"),
                        negc8[:, 0:1])
                    pad()
                    nc.vector.tensor_copy(P("VMS"), P("X2"))
                return nc.vector.engine_nop()
            ve(vmh, inc="sVE")
            vmd = S.after("sVE")
            if dev_phase == "B":
                S.dma("sync", lambda h, k=k, sl=sl: h.dma_start(
                          out=dev["dbg_stp"][k],
                          in_=stp[:, sl * NPL * 8:(sl + 1) * NPL * 8]),
                      waits=[vmd], inc="sST")
                if h8 == 7:
                    S.dma("sync", lambda h, k=k: h.dma_start(
                              out=dev["dbg_m8"][k][:, 0:64], in_=m8b64[:]), inc="sST")
                    S.dma("sync", lambda h, k=k: h.dma_start(
                              out=dev["dbg_m8"][k][:, 64:128], in_=m8a64[:]), inc="sST")
            def exf(h, h8=h8, Sg=Sg, W=W, kpad=kpad, sl=sl):
                return nc.scalar.activation(
                    out=Ppre2[:, (h8 % 2) * 2048:(h8 % 2) * 2048 + kpad],
                    in_=Sg[:, h8 * W: h8 * W + kpad], func=AF.Exp,
                    bias=plc("B0", sl, h8), scale=sinv_s[:, 0:1])
            wx = [ev[k]["fin"]]
            if h8 >= 2:
                wx.append(("sPF", k * 8 + h8 - 1))
            S.emit("scalar", exf, waits=wx, inc="sSC")
            expd = S.after("sSC")
            def pff(h, h8=h8, Sg=Sg, W=W, kpad=kpad, sl=sl):
                nc.vector.scalar_tensor_tensor(
                    out=Ppre2[:, (h8 % 2) * 2048:(h8 % 2) * 2048 + kpad],
                    in0=Sg[:, h8 * W: h8 * W + kpad], scalar=plc("VMS", sl, h8),
                    in1=Ppre2[:, (h8 % 2) * 2048:(h8 % 2) * 2048 + kpad],
                    op0=ALU.is_ge, op1=ALU.mult, accum_out=plc("Z", sl, h8))
                nc.vector.tensor_copy(pl("PAD", sl), iota8_s[:])
                return nc.vector.reciprocal(plc("ZI", sl, h8), plc("Z", sl, h8))
            ve(pff, waits=[expd, vmd], inc="sPF")
            def pzf(h, h8=h8, kpad=kpad, sl=sl):
                return nc.scalar.activation(
                    out=Pfin2[:, (h8 % 2) * 2048:(h8 % 2) * 2048 + kpad],
                    in_=Ppre2[:, (h8 % 2) * 2048:(h8 % 2) * 2048 + kpad],
                    func=AF.Identity, scale=plc("ZI", sl, h8))
            wz = [("sPF", k * 8 + h8 + 1)]
            if h8 >= 2:
                wz.append(ev[k]["trp%d" % (h8 - 2)])
            S.emit("scalar", pzf, waits=wz, inc="sSC")
            ev[k]["pz%d" % h8] = S.after("sSC")

        def emit_ext_head2(k, h8):
            oct_, i, kpad, nkt, sl = tinfo(k)
            pzd = ev[k]["pz%d" % h8]
            x = h8 % 4
            nbat = (nkt + 7) // 8
            ptbev = ev[k].setdefault("ptbev", {})
            ptbev[h8] = []
            for b in range(nbat):
                lo = b * 8
                hi = min(nkt, lo + 8)
                pb = (len(ptb_hist)) % 2
                def trf(h, h8=h8, lo=lo, hi=hi, pb=pb):
                    last = None
                    for kt in range(lo, hi):
                        last = nc.tensor.transpose(
                            ps_pt[pb][:, (kt - lo) * 128:(kt - lo + 1) * 128],
                            Pfin2[:, (h8 % 2) * 2048 + kt * 128:
                                  (h8 % 2) * 2048 + (kt + 1) * 128],
                            ident_s[:])
                    return last
                wt = [pzd]
                if len(ptb_hist) >= 2:
                    wt.append(ptb_hist[-2])
                S.emit("tensor", trf, waits=wt, inc="sPE")
                trd = S.after("sPE")
                def tev(h, x=x, lo=lo, hi=hi, pb=pb):
                    return nc.vector.tensor_copy(
                        out=PTb.rearrange("p (kt xx c) -> p kt xx c", xx=4, c=128)[:, lo:hi, x],
                        in_=ps_pt[pb][:, 0:(hi - lo) * 128]
                            .rearrange("p (kt c) -> p kt c", c=128))
                ve(tev, waits=[trd], inc="sVE")
                ptb_hist.append(S.after("sVE"))
                ptbev[h8].append(S.after("sVE"))
            ev[k]["trp%d" % h8] = trd
            if h8 % 4 == 3:
                g = h8 // 4
                wav = [e for hh in range(4 * g, 4 * g + 4) for e in ptbev[hh]]
                wav.append(ev[k]["ldv"])
                if k >= 1 and ("avev%d" % g) in ev[k - 1]:
                    wav.append(ev[k - 1]["avev%d" % g])
                def avf(h, g=g, nkt=nkt, sl=sl):
                    last = None
                    for kt in range(nkt):
                        last = nc.tensor.matmul(
                            ps_av[g][:, 0:512],
                            v2[:, sl * 4096 + g * 2048 + kt * 128:
                               sl * 4096 + g * 2048 + (kt + 1) * 128],
                            PTb[:, kt * 512:(kt + 1) * 512],
                            start=(kt == 0), stop=(kt == nkt - 1))
                    return last
                S.emit("tensor", avf, waits=wav, inc="sPE")
                avd = S.after("sPE")
                pending_avev.append((k, g, avd))

        ptb_hist = []
        pending_avev = []

        def flush_avev():
            while pending_avev:
                kk, g, avd = pending_avev.pop(0)
                def aev(h, g=g):
                    return nc.scalar.copy(out=ot_stage[:, g * 512:(g + 1) * 512],
                                          in_=ps_av[g][:, 0:512])
                S.emit("scalar", aev, waits=[avd], inc="sSC")
                ev[kk]["avev%d" % g] = S.after("sSC")
                oct2, i2 = tiles[kk]
                hh0 = oct2 * 8 + g * 4
                S.dma("sync", lambda h, hh0=hh0, i2=i2, g=g: h.dma_start(
                          out=ot_d[hh0:hh0 + 4, i2].rearrange("hh p q -> p hh q"),
                          in_=ot_stage[:, g * 512:(g + 1) * 512]
                              .rearrange("p (hh q) -> p hh q", q=128)),
                      waits=[ev[kk]["avev%d" % g]], inc="sST")
                if g == 1:
                    S.emit("vector", lambda h: nc.vector.engine_nop(),
                           waits=[avd], inc="sAV")

        emit_load(0)
        emit_load_v(0)
        for h8 in range(8):
            g0 = len(mm_ev)
            emit_smm_head(0, h8)
            build_evq_head(0, h8, g0)
        drain_evq(0, None)
        emit_postload(0)
        for k in range(NT):
            u = k + 1 if k + 1 < NT else None
            if u is not None:
                emit_load(u)
            for r in range(1, N_CNT + 1):
                emit_counts(k, r)
                drain_evq(u, None)
                flush_avev()
                if u is not None:
                    g0 = len(mm_ev)
                    emit_smm_head(u, r - 1)
                    build_evq_head(u, r - 1, g0)
                if k >= 1:
                    emit_ext_head(k - 1, r - 1)
                if r < N_CNT:
                    emit_upd(k, r)
                if k >= 1:
                    emit_ext_head2(k - 1, r - 1)
            drain_evq(u, None)
            emit_fin(k)
            if u is not None:
                emit_load_v(u)
                emit_postload(u)
        for h8 in range(8):
            emit_ext_head(NT - 1, h8)
            emit_ext_head2(NT - 1, h8)
            flush_avev()
        phaseB_st = S.after("sST")
        if dev_phase == "B":
            S.dma("sync", lambda h: h.dma_start(out=dev["dbg_ot"], in_=ot_d),
                  waits=[phaseB_st], inc="sST")
            S.emit("vector", lambda h: nc.vector.memset(SgB[:, 0:2048], 0.0), inc="sVE")
            zr2 = S.after("sVE")
            ochunks = out_t.rearrange("(c p) d -> c p d", p=128)
            for c in range(8):
                for hf in range(2):
                    S.dma("sync", lambda h, c=c, hf=hf: h.dma_start(
                              out=ochunks[c][:, hf * 2048:(hf + 1) * 2048], in_=SgB[:, 0:2048]),
                          waits=[zr2], inc="sST")
            S.emit("sync", lambda h: h.nop(), waits=[S.after("sST")])

    if dev_phase == "full":
        ctxB.close()
        first_c = {e: True for e in ENGS}
        def wc(eng, waits):
            w = list(waits)
            if first_c[eng]:
                w.append(phaseB_st)
                first_c[eng] = False
            return w
        ctxC1 = ExitStack()
        wopan = ctxC1.enter_context(nc.sbuf_tensor("wopan", [128, NDT * 1024], BF16))
        otw = ctxC1.enter_context(nc.sbuf_tensor("otw", [128, 32 * 128], BF16))
        x1st = ctxC1.enter_context(nc.sbuf_tensor("x1st", [128, 1024], F32))
        ps_wo = [ctxC1.enter_context(nc.psum_tensor("ps_wo0", [128, 1024], F32)),
                 ctxC1.enter_context(nc.psum_tensor("ps_wo1", [128, 1024], F32))]
        x1_d = DS("x1_d", [8, 128, D], F32)

        otw_guard = (None, 0)
        x1_parks = (None, 0)
        for oq in range(4):
            for dt2 in range(NDT):
                S.dma("gpsimd", lambda h, oq=oq, dt2=dt2: h.dma_start(
                          out=wopan[:, dt2 * 1024:(dt2 + 1) * 1024],
                          in_=WoT_i[dt2 * 128:(dt2 + 1) * 128, oq * 1024:(oq + 1) * 1024]),
                      waits=wc("gpsimd", [S.after("sPE")] if dt2 == 0 and oq > 0 else []),
                      inc="sCA")
            pan_rdy = S.after("sCA")
            for i in range(8):
                S.dma("sync", lambda h, i=i: h.dma_start(
                          out=otw[:, :].rearrange("p (hh c) -> p hh c", c=128),
                          in_=ot_d[:, i].rearrange("hh p c -> p hh c")),
                      waits=wc("sync", [otw_guard]), inc="sLD")
                otl = S.after("sLD")
                buf = i % 2
                def mwo(h, buf=buf, i=i):
                    last = None
                    for dt2 in range(NDT):
                        for c5 in range(2):
                            last = nc.tensor.matmul(
                                ps_wo[buf][:, c5 * 512:(c5 + 1) * 512],
                                otw[:, dt2 * 128:(dt2 + 1) * 128],
                                wopan[:, dt2 * 1024 + c5 * 512: dt2 * 1024 + (c5 + 1) * 512],
                                start=(dt2 == 0), stop=(dt2 == NDT - 1))
                    return last
                S.emit("tensor", mwo, waits=wc("tensor", [pan_rdy, otl, x1_parks]), inc="sPE")
                mwd = S.after("sPE")
                S.emit("scalar", lambda h, buf=buf: nc.scalar.copy(out=x1st[:], in_=ps_wo[buf][:]),
                       waits=wc("scalar", [mwd]), inc="sSC")
                evd = S.after("sSC")
                S.dma("sync", lambda h, i=i, oq=oq: h.dma_start(
                          out=x1_d[i][:, oq * 1024:(oq + 1) * 1024], in_=x1st[:]),
                      waits=[evd], inc="sST")
                x1_parks = S.after("sST")
                otw_guard = mwd
        c1_done = S.after("sST")
        ctxC1.close()

        ctxC2 = ExitStack()
        def SC(name, shape, dt):
            return ctxC2.enter_context(nc.sbuf_tensor(name, list(shape), dt))
        def PC(name, shape, dt):
            return ctxC2.enter_context(nc.psum_tensor(name, list(shape), dt))
        x1l = SC("x1l", [128, D], F32)
        hsr = SC("hsr", [128, D], F32)
        x2f = SC("x2f", [128, D], F32)
        x2b = SC("x2b", [128, D], BF16)
        x2T = SC("x2T", [128, NDT * 128], BF16)
        x3f = SC("x3f", [128, D], F32)
        outf = SC("outf", [128, D], F32)
        routb = SC("routb", [128, NDT * 8], BF16)
        acatb = SC("acatb", [128, NDT * 128], BF16)
        bwcb = SC("bwcb", [128, D], BF16)
        e8f = SC("e8f", [8, 128], F32)
        gatep = SC("gatep", [128, 128], F32)
        gT = SC("gT", [8, 128], F32)
        gevac = SC("gevac", [128, 128], F32)
        htil = SC("htil", [128, 128], BF16)
        lnp = SC("lnp", [128, 16], F32)
        ps_t2 = PC("ps_t2", [128, 1024], BF16)
        ps_rt = PC("ps_rt", [128, 512], F32)
        ps_h = PC("ps_h", [128, 512], F32)
        ps_ge = PC("ps_ge", [128, 512], F32)
        ps_gt = PC("ps_gt", [128, 512], F32)
        ps_ff = PC("ps_ff", [128, 1024], F32)
        def lp(j):
            return lnp[:, j:j + 1]

        S.dma("gpsimd", lambda h: h.dma_start(
                  out=routb[:, :].rearrange("p (dt e) -> p dt e", e=8),
                  in_=routeT_i.rearrange("(dt p) e -> p dt e", p=128)),
              waits=[c1_done], inc="sCA")
        S.dma("gpsimd", lambda h: h.dma_start(
                  out=acatb[:, :].rearrange("p (dt c) -> p dt c", c=128),
                  in_=AcatT_i.rearrange("(dt p) c -> p dt c", p=128)), inc="sCA")
        S.dma("gpsimd", lambda h: h.dma_start(out=bwcb[:], in_=BwcatT_i), inc="sCA")
        S.dma("sync", lambda h: h.dma_start(out=e8f[:], in_=E8_i), waits=[c1_done], inc="sLD")
        cld = S.after("sCA")
        cld2 = S.after("sLD")
        ve(lambda h: nc.vector.memset(gatep[:, :], 0.0), waits=[c1_done], inc="sVE")
        gz = S.after("sVE")

        prev_i_done = (None, 0)
        for i in range(8):
            S.dma("sync", lambda h, i=i: h.dma_start(out=x1l[:], in_=x1_d[i]),
                  waits=[prev_i_done], inc="sLD")
            S.dma("sync", lambda h, i=i: h.dma_start(out=hsr[:], in_=hs_own[i * 128:(i + 1) * 128, :]),
                  inc="sLD")
            lds = S.after("sLD")
            ve(lambda h: nc.vector.tensor_add(x1l[:], x1l[:], hsr[:]),
               waits=[lds, gz], inc="sVE")
            r1d = S.after("sVE")
            def st1(h):
                h.drain()
                nc.scalar.activation(out=x2f[:], in_=x1l[:], func=AF.Copy, accum_out=lp(0))
                h.drain()
                return nc.scalar.activation(out=x2f[:], in_=x1l[:], func=AF.Square, accum_out=lp(1))
            S.emit("scalar", st1, waits=wc("scalar", [r1d]) if first_c["scalar"] else [r1d], inc="sSC")
            st1d = S.after("sSC")
            def ln1t(h):
                nc.vector.tensor_scalar(lp(2), lp(0), 1.0 / D, None, op0=ALU.mult)
                h.drain()
                nc.vector.tensor_scalar(lp(3), lp(1), 1.0 / D, None, op0=ALU.mult)
                h.drain()
                nc.vector.tensor_mul(lp(4), lp(2), lp(2))
                h.drain()
                nc.vector.tensor_sub(lp(3), lp(3), lp(4))
                h.drain()
                return nc.vector.tensor_scalar(lp(3), lp(3), 1e-5, None, op0=ALU.add)
            ve(ln1t, waits=[st1d], inc="sVE")
            ln1td = S.after("sVE")
            def rst1(h):
                h.drain()
                nc.scalar.activation(lp(5), lp(3), AF.Ln)
                h.drain()
                return nc.scalar.activation(lp(6), lp(5), AF.Exp, scale=-0.5)
            S.emit("scalar", rst1, waits=[ln1td], inc="sSC")
            rst1d = S.after("sSC")
            def b2v(h):
                nc.vector.tensor_mul(lp(7), lp(2), lp(6))
                h.drain()
                return nc.vector.tensor_scalar(lp(7), lp(7), -1.0, None, op0=ALU.mult)
            ve(b2v, waits=[rst1d], inc="sVE")
            b2d = S.after("sVE")
            S.emit("scalar", lambda h: nc.scalar.activation(out=x2f[:], in_=x1l[:], func=AF.Identity,
                                                            bias=lp(7), scale=lp(6)),
                   waits=[b2d], inc="sSC")
            x2d = S.after("sSC")
            ve(lambda h: nc.vector.tensor_copy(out=x2b[:], in_=x2f[:]), waits=[x2d], inc="sVE")
            x2bd = S.after("sVE")
            for tg in range(4):
                def trx(h, tg=tg):
                    last = None
                    for j2 in range(8):
                        dt2 = tg * 8 + j2
                        last = nc.tensor.transpose(ps_t2[:, j2 * 128:(j2 + 1) * 128],
                                                   x2b[:, dt2 * 128:(dt2 + 1) * 128], ident_s[:])
                    return last
                S.emit("tensor", trx, waits=wc("tensor", [x2bd, S.after("sSC")]), inc="sPE")
                trxd = S.after("sPE")
                S.emit("scalar", lambda h, tg=tg: nc.scalar.copy(
                           out=x2T[:, tg * 1024:(tg + 1) * 1024], in_=ps_t2[:]),
                       waits=[trxd], inc="sSC")
            x2td = S.after("sSC")
            def mrt(h):
                last = None
                for dt2 in range(NDT):
                    last = nc.tensor.matmul(ps_rt[:, 0:8], x2T[:, dt2 * 128:(dt2 + 1) * 128],
                                            routb[:, dt2 * 8:(dt2 + 1) * 8],
                                            start=(dt2 == 0), stop=(dt2 == NDT - 1))
                return last
            S.emit("tensor", mrt, waits=[x2td, cld, S.after("sVE")], inc="sPE")
            mrtd = S.after("sPE")
            def top1(h):
                nc.vector.tensor_copy(gatep[:, 0:8], ps_rt[:, 0:8])
                h.drain()
                nc.vector.tensor_reduce(lp(8), gatep[:, 0:8], axis=AX.X, op=ALU.max)
                h.drain()
                nc.vector.tensor_scalar(gatep[:, 8:16], gatep[:, 0:8], lp(8), None, op0=ALU.is_ge)
                h.drain()
                nc.vector.memset(gatep[:, 16:24], 0.0)
                h.drain()
                nc.vector.tensor_copy(gatep[:, 17:24], gatep[:, 8:15])
                h.drain()
                nc.vector.tensor_copy(gatep[:, 24:32], gatep[:, 16:24])
                h.drain()
                nc.vector.tensor_add(gatep[:, 26:32], gatep[:, 26:32], gatep[:, 16:22])
                h.drain()
                nc.vector.tensor_copy(gatep[:, 32:40], gatep[:, 24:32])
                h.drain()
                nc.vector.tensor_add(gatep[:, 36:40], gatep[:, 36:40], gatep[:, 24:28])
                h.drain()
                nc.vector.tensor_scalar(gatep[:, 40:48], gatep[:, 32:40], 0.5, None, op0=ALU.is_lt)
                h.drain()
                nc.vector.tensor_mul(gatep[:, 48:56], gatep[:, 8:16], gatep[:, 40:48])
                h.drain()
                nc.vector.tensor_scalar(gatep[:, 56:64], gatep[:, 48:56], lp(8), None, op0=ALU.mult)
                h.drain()
                return nc.vector.tensor_copy(gatep[:, 0:8], gatep[:, 56:64])
            ve(top1, waits=[mrtd], inc="sVE")
            t1d = S.after("sVE")
            def mh(h):
                last = None
                for dt2 in range(NDT):
                    last = nc.tensor.matmul(ps_h[:, 0:128], acatb[:, dt2 * 128:(dt2 + 1) * 128],
                                            x2T[:, dt2 * 128:(dt2 + 1) * 128],
                                            start=(dt2 == 0), stop=(dt2 == NDT - 1))
                return last
            S.emit("tensor", mh, waits=[x2td], inc="sPE")
            mhd = S.after("sPE")
            def mgt(h):
                return nc.tensor.transpose(ps_gt[:, 0:128], gatep[:, 0:128], identf_s[:])
            S.emit("tensor", mgt, waits=[t1d], inc="sPE")
            mgtd = S.after("sPE")
            S.emit("scalar", lambda h: nc.scalar.copy(out=gT[:], in_=ps_gt[0:8, 0:128]),
                   waits=[mgtd], inc="sSC")
            gtd = S.after("sSC")
            def mge(h):
                return nc.tensor.matmul(ps_ge[:, 0:128], e8f[:], gT[:], start=True, stop=True)
            S.emit("tensor", mge, waits=[gtd, cld2], inc="sPE")
            mged = S.after("sPE")
            S.emit("scalar", lambda h: nc.scalar.copy(out=gevac[:], in_=ps_ge[:, 0:128]),
                   waits=[mged], inc="sSC")
            gevd = S.after("sSC")
            ve(lambda h: nc.vector.tensor_mul(htil[:], ps_h[:, 0:128], gevac[:]),
               waits=[mhd, gevd], inc="sVE")
            htd = S.after("sVE")
            for oq2 in range(4):
                def mff(h, oq2=oq2):
                    nc.tensor.matmul(ps_ff[:, 0:512], htil[:],
                                     bwcb[:, oq2 * 1024: oq2 * 1024 + 512], start=True, stop=True)
                    return nc.tensor.matmul(ps_ff[:, 512:1024], htil[:],
                                            bwcb[:, oq2 * 1024 + 512: (oq2 + 1) * 1024],
                                            start=True, stop=True)
                S.emit("tensor", mff, waits=[htd] + ([S.after("sVE")] if oq2 else []), inc="sPE")
                mffd = S.after("sPE")
                def rx3(h, oq2=oq2):
                    return nc.vector.tensor_add(x3f[:, oq2 * 1024:(oq2 + 1) * 1024],
                                                ps_ff[:, 0:1024], x2f[:, oq2 * 1024:(oq2 + 1) * 1024])
                ve(rx3, waits=[mffd], inc="sVE")
            x3d = S.after("sVE")
            def st2(h):
                h.drain()
                nc.scalar.activation(out=outf[:], in_=x3f[:], func=AF.Copy, accum_out=lp(0))
                h.drain()
                return nc.scalar.activation(out=outf[:], in_=x3f[:], func=AF.Square, accum_out=lp(1))
            S.emit("scalar", st2, waits=[x3d], inc="sSC")
            st2d = S.after("sSC")
            ve(ln1t, waits=[st2d], inc="sVE")
            ln2td = S.after("sVE")
            S.emit("scalar", rst1, waits=[ln2td], inc="sSC")
            rst2d = S.after("sSC")
            ve(b2v, waits=[rst2d], inc="sVE")
            b2d2 = S.after("sVE")
            S.emit("scalar", lambda h: nc.scalar.activation(out=outf[:], in_=x3f[:], func=AF.Identity,
                                                            bias=lp(7), scale=lp(6)),
                   waits=[b2d2], inc="sSC")
            od = S.after("sSC")
            S.dma("sync", lambda h, i=i: h.dma_start(out=out_t[i * 128:(i + 1) * 128, :], in_=outf[:]),
                  waits=[od], inc="sST")
            prev_i_done = S.after("sST")
        S.emit("sync", lambda h: h.nop(), waits=[S.after("sST")])
        ctxC2.close()

    if dev_phase in ("A", "Araw"):
        S.dma("sync", lambda h: h.dma_start(out=dev["dbg_qrt"], in_=qrt_d), waits=[phaseA_st], inc="sST")
        if dev_phase == "A":
            S.dma("sync", lambda h: h.dma_start(out=dev["dbg_krt"], in_=krt_d), inc="sST")
            S.dma("sync", lambda h: h.dma_start(out=dev["dbg_v"], in_=v_d), inc="sST")
        S.emit("vector", lambda h: nc.vector.memset(qk_tmp[:, :], 0.0), inc="sVE")
        zr = S.after("sVE")
        ochunks = out_t.rearrange("(c p) d -> c p d", p=128)
        for c in range(8):
            for hf in range(2):
                S.dma("sync", lambda h, c=c, hf=hf: h.dma_start(
                          out=ochunks[c][:, hf * 2048:(hf + 1) * 2048], in_=qk_tmp[:, :]),
                      waits=[zr], inc="sST")
        S.emit("sync", lambda h: h.nop(), waits=[S.after("sST")])

    if dev_phase == "B":
        ctxB.close()
    if dev_phase in ("A", "Araw"):
        ctxA.close()

    with nc.Block() as blk:
        @blk.sync
        def _(h):
            for op in S.ops["sync"]:
                op(h)
        @blk.gpsimd
        def _(h):
            for op in S.ops["gpsimd"]:
                op(h)
        @blk.vector
        def _(h):
            for op in S.ops["vector"]:
                op(h)
        @blk.scalar
        def _(h):
            for op in S.ops["scalar"]:
                op(h)
        @blk.tensor
        def _(h):
            for op in S.ops["tensor"]:
                op(h)

    ctx.close()
    return nc


_PROG_CACHE = {}


def kernel(**inputs):
    wts = host_weights(inputs)
    hs_all = np.asarray(inputs["L0_attn_hidden_states"], dtype=np.float32)
    if "prog" not in _PROG_CACHE:
        _PROG_CACHE["prog"] = build_program(dev_phase="full")
        _PROG_CACHE["consts"] = {p: host_constants(p) for p in (0, 1)}
    nc = _PROG_CACHE["prog"]
    in_maps = []
    for c in range(8):
        b, p = c // 2, c % 2
        owncols = np.concatenate([np.arange(qb * 128, (qb + 1) * 128) for qb in OWN[p]])
        hsb = np.ascontiguousarray(hs_all[b])
        in_maps.append(dict(hs=hsb, hs_own=np.ascontiguousarray(hsb[owncols]),
                            **wts, **_PROG_CACHE["consts"][p]))
    res = run_bass_kernel_spmd(nc, in_maps, core_ids=list(range(8)))
    out = np.empty((B, SQ, D), np.float32)
    for c in range(8):
        b, p = c // 2, c % 2
        r = np.asarray(res.results[c]["out"])
        for i, qb in enumerate(OWN[p]):
            out[b, qb * 128:(qb + 1) * 128, :] = r[i * 128:(i + 1) * 128, :]
    return out
